# version 05
# baseline (speedup 1.0000x reference)
import contextlib
import numpy as np
import ml_dtypes
import concourse.bass as bass
import concourse.mybir as mybir
from concourse.bass_utils import run_bass_kernel_spmd

F32 = mybir.dt.float32
BF16 = mybir.dt.bfloat16
Alu = mybir.AluOpType
Act = mybir.ActivationFunctionType
AxX = mybir.AxisListType.X
BF = ml_dtypes.bfloat16

NCORE = 8
B, L, DM = 2, 8192, 1024
DI, S, G = 2048, 16, 16
T = 2048
TH = T + 3
NST = G + 3
OC = 24
FI = 3072
MAGIC = 12582912.0
QMAX = 32767.0
EPS = 1e-5
ENGS = ("sync", "vector", "scalar", "gpsimd", "tensor")
RG = [list(range(NCORE))]


def build_nc(debug=False):
    nc = bass.Bass("TRN2", target_bir_lowering=False,
                   detect_race_conditions=False)

    def din(name, shape, dt=F32):
        return nc.dram_tensor(name, shape, dt, kind="ExternalInput")

    x_in = din("x_in", [TH, DM])
    w_in = din("w_in", [DM, 2 * DI])
    w_dt = din("w_dt", [DI, DI])
    w_xp = din("w_xp", [DI, 2 * S])
    w_out = din("w_out", [DI, DM])
    cw = din("cw", [DI, 4])
    cb = din("cb", [DI, 1])
    bdt = din("bdt", [DI, 1])
    alog = din("alog", [DI, S])
    ddiag = din("ddiag", [DI, 128])
    wu_nat = din("wu_nat", [FI, DM])
    wg_nat = din("wg_nat", [FI, DM])
    wd_nat = din("wd_nat", [DM, FI])
    ident_in = din("ident", [128, 128], BF16)
    onesF_in = din("ones1F", [1, 128], BF16)
    sel32_in = din("sel32", [32, 32 * 128], BF16)
    onehot_in = din("onehot", [8, 1], BF16)
    magic_in = din("magics", [128, 2], F32)

    out_d = nc.dram_tensor("out", [T, DM], F32, kind="ExternalOutput")
    dbg = {}
    if debug:
        for nm in ("dbg_xs", "dbg_dt", "dbg_xt", "dbg_gtd", "dbg_sz"):
            dbg[nm] = nc.dram_tensor(nm, [G, 128, T], BF16, kind="ExternalOutput")
        dbg["dbg_x2"] = nc.dram_tensor("dbg_x2", [T, DM], F32, kind="ExternalOutput")
        dbg["dbg_da0"] = nc.dram_tensor("dbg_da0", [128, T], F32, kind="ExternalOutput")
        dbg["dbg_db0"] = nc.dram_tensor("dbg_db0", [128, T], BF16, kind="ExternalOutput")
        dbg["dbg_h0"] = nc.dram_tensor("dbg_h0", [128, T], BF16, kind="ExternalOutput")
        dbg["dbg_tmp0"] = nc.dram_tensor("dbg_tmp0", [128, T], BF16, kind="ExternalOutput")
        dbg["dbg_init0"] = nc.dram_tensor("dbg_init0", [128, S], F32, kind="ExternalOutput")
        dbg["dbg_bb0"] = nc.dram_tensor("dbg_bb0", [128, T], BF16, kind="ExternalOutput")
        dbg["dbg_cb0"] = nc.dram_tensor("dbg_cb0", [128, T], BF16, kind="ExternalOutput")
        dbg["dbg_na"] = nc.dram_tensor("dbg_na", [128, G * S], F32, kind="ExternalOutput")
        dbg["dbg_y0"] = nc.dram_tensor("dbg_y0", [128, T], F32, kind="ExternalOutput")
        dbg["dbg_bcb"] = nc.dram_tensor("dbg_bcb", [32, T], BF16, kind="ExternalOutput")
        dbg["dbg_bbc_f"] = nc.dram_tensor("dbg_bbc_f", [128, S * T], BF16, kind="ExternalOutput")
        dbg["dbg_cbc_f"] = nc.dram_tensor("dbg_cbc_f", [128, S * T], BF16, kind="ExternalOutput")
        dbg["dbg_wuT"] = nc.dram_tensor("dbg_wuT", [128, DM], BF16, kind="ExternalOutput")
        dbg["dbg_wdT"] = nc.dram_tensor("dbg_wdT", [128, FI], BF16, kind="ExternalOutput")
        dbg["dbg_xqT"] = nc.dram_tensor("dbg_xqT", [128, 8 * T], BF16, kind="ExternalOutput")
        dbg["dbg_up"] = nc.dram_tensor("dbg_up", [128, T], F32, kind="ExternalOutput")
        dbg["dbg_gate"] = nc.dram_tensor("dbg_gate", [128, T], F32, kind="ExternalOutput")
        dbg["dbg_g2"] = nc.dram_tensor("dbg_g2", [128, T], F32, kind="ExternalOutput")
        dbg["dbg_oo"] = nc.dram_tensor("dbg_oo", [128, T], F32, kind="ExternalOutput")
        dbg["dbg_sc1"] = nc.dram_tensor("dbg_sc1", [128, 8], F32, kind="ExternalOutput")
        dbg["dbg_mx"] = nc.dram_tensor("dbg_mx", [1, 4], F32, kind="ExternalOutput")
        dbg["dbg_mrow"] = nc.dram_tensor("dbg_mrow", [1, 128], F32, kind="ExternalOutput")
        dbg["dbg_sc1a"] = nc.dram_tensor("dbg_sc1a", [1, 8], F32, kind="ExternalOutput")
        dbg["dbg_sc1c"] = nc.dram_tensor("dbg_sc1c", [1, 24], BF16, kind="ExternalOutput")

    sc_dt = nc.dram_tensor("sc_dt", [G, 128, T], BF16)
    sc_xt = nc.dram_tensor("sc_xt", [G, 128, T], BF16)
    sc_xs = nc.dram_tensor("sc_xs", [G, 128, T], BF16)
    sc_sz = nc.dram_tensor("sc_sz", [G, 128, T], BF16)
    sc_gated = nc.dram_tensor("sc_gated", [G, 128, T], BF16)
    sc_wuT = nc.dram_tensor("sc_wuT", [FI // 128, 128, DM], BF16)
    sc_wgT = nc.dram_tensor("sc_wgT", [FI // 128, 128, DM], BF16)
    sc_wdT = nc.dram_tensor("sc_wdT", [DM // 128, 128, FI], BF16)
    sc_up = nc.dram_tensor("sc_up", [OC, 128, T], F32)
    sc_gate = nc.dram_tensor("sc_gate", [OC, 128, T], F32)
    sc_g2 = nc.dram_tensor("sc_g2", [OC, 128, T], F32)
    sc_o = nc.dram_tensor("sc_o", [OC, 128, T], F32)
    sc_x2 = nc.dram_tensor("sc_x2", [16, 128, DM], F32)
    ag_i = [nc.dram_tensor(f"ag_i{r}", [1, 2048], BF16) for r in range(NST)]
    ag_o = [nc.dram_tensor(f"ag_o{r}", [8, 2048], BF16) for r in range(NST)]
    agm_i = [nc.dram_tensor(f"agm_i{k}", [1, 4], F32) for k in range(4)]
    agm_o = [nc.dram_tensor(f"agm_o{k}", [8, 4], F32) for k in range(4)]

    ARENA_N = 52992
    arena_cm = nc.sbuf_tensor("arena", [128, ARENA_N], F32)
    arena = arena_cm.__enter__()
    parena_cm = nc.psum_tensor("parena", [128, 4096], F32)
    parena = parena_cm.__enter__()
    segs = {"s": [[0, ARENA_N, None]], "p": [[0, 4096, None]]}
    seg_names = {}

    def _alloc(pool, name, n_f32, align, high=False):
        n_f32 = -(-n_f32 // align) * align
        it = reversed(segs[pool]) if high else segs[pool]
        for sg in list(it):
            if sg[2] is None and sg[1] >= n_f32:
                if high:
                    off = sg[0] + sg[1] - n_f32
                    if sg[1] > n_f32:
                        segs[pool].insert(segs[pool].index(sg) + 1,
                                          [off, n_f32, name])
                        sg[1] -= n_f32
                    else:
                        sg[2] = name
                    seg_names[name] = (pool, off, n_f32)
                    return off
                off = sg[0]
                if sg[1] > n_f32:
                    segs[pool].insert(segs[pool].index(sg) + 1,
                                      [off + n_f32, sg[1] - n_f32, None])
                sg[1] = n_f32
                sg[2] = name
                seg_names[name] = (pool, off, n_f32)
                return off
        raise MemoryError(f"{pool} arena full for {name} ({n_f32} f32/part); "
                          f"segs={[(s[0], s[1], s[2]) for s in segs[pool]]}")

    def _view(base, off, shape, dt):
        p = shape[0]
        n = int(np.prod(shape[1:]))
        if dt == F32:
            ap = base[0:p, off:off + n]
        elif dt in (BF16,):
            ap = base[0:p, off:off + (n + 1) // 2].bitcast(BF16)[0:p, 0:n]
        else:
            raise ValueError(dt)
        if len(shape) > 2:
            pat = "q (" + " ".join(f"d{i}" for i in range(1, len(shape))) + ") -> q " +                   " ".join(f"d{i}" for i in range(1, len(shape)))
            ap = ap.rearrange(pat, **{f"d{i}": shape[i] for i in range(1, len(shape))})
        return ap

    def sb(name, shape, dt, high=False):
        n = int(np.prod(shape[1:]))
        nf = n if dt == F32 else (n + 1) // 2
        off = _alloc("s", name, nf, 16, high)
        return _view(arena, off, shape, dt)

    def ps(name, shape, dt=F32):
        n = int(np.prod(shape[1:]))
        nf = n if dt == F32 else (n + 1) // 2
        off = _alloc("p", name, nf, 512)
        return _view(parena, off, shape, dt)

    def sb_free(*names):
        for nm in names:
            pool, off, nf = seg_names.pop(nm)
            for sg in segs[pool]:
                if sg[0] == off:
                    sg[2] = None
                    break
            merged = []
            for sg in segs[pool]:
                if merged and merged[-1][2] is None and sg[2] is None:
                    merged[-1][1] += sg[1]
                else:
                    merged.append(sg)
            segs[pool] = merged

    sems, semc = {}, {}

    _semcms = []

    def mksem(name):
        cm = nc.semaphore(name)
        h = cm.__enter__()
        _semcms.append(cm)
        sems[name] = h
        semc[name] = 0

    for s_ in ("qS", "qS2", "qTP", "qFB", "qFC", "qFD", "qW2", "qG", "qG2",
               "sV", "sC", "sT", "sG", "sCC",
               "qX", "qD", "qT2", "qX2", "qZ2", "qI", "qDtW", "qXtW",
               "qSzW", "qGtW", "qP7", "qFW", "qFA", "qOut"):
        mksem(s_)

    EMIT = {e: [] for e in ENGS}
    _iswait = {e: [] for e in ENGS}
    EV = {e: [] for e in ENGS}

    def em(eng, fn, wait=False):
        EMIT[eng].append(fn)
        _iswait[eng].append(wait)

    def W(eng, semname, val):
        if val and val > 0:
            em(eng, lambda e, s=sems[semname], v=val: e.wait_ge(s, v),
               wait=True)
            EV[eng].append(("wait", semname, val))

    def INC(eng, semname, n=1):
        assert EMIT[eng] and not _iswait[eng][-1], (eng, semname)
        semc[semname] += n
        prev = EMIT[eng][-1]
        EMIT[eng][-1] = (lambda e, f=prev, s=sems[semname], k=n:
                         f(e).then_inc(s, k))
        EV[eng].append(("inc", semname, semc[semname]))
        return semc[semname]

    def DMA(eng, q, out_ap, in_ap):
        semc[q] += 16
        em(eng, lambda e, o=out_ap, i=in_ap, s=sems[q]:
           e.dma_start(out=o, in_=i).then_inc(s, 16))
        EV[eng].append(("dma", q, semc[q]))
        return semc[q]

    def CCEV(val):
        EV["gpsimd"].append(("inc", "sCC", val))

    def audit_races():
        waited = {}
        for e in ENGS:
            for (k, s, v) in EV[e]:
                if k == "wait":
                    waited.setdefault(s, set()).add((e, v))
        pos = {e: 0 for e in ENGS}
        clock = {e: {} for e in ENGS}
        snap = {}
        semval = {s: 0 for s in semc}
        bad = []
        progress = True
        while progress:
            progress = False
            for e in ENGS:
                while pos[e] < len(EV[e]):
                    k, s, v = EV[e][pos[e]]
                    if k == "wait":
                        if semval.get(s, 0) < v:
                            break
                        c = clock[e]
                        for ss, vv in snap.get((s, v), {}).items():
                            if c.get(ss, 0) < vv:
                                c[ss] = vv
                        if c.get(s, 0) < v:
                            c[s] = v
                    else:
                        if k == "dma":
                            kn = clock[e].get(s, 0)
                            for (we, wv) in waited.get(s, ()):
                                if we != e and kn < wv < v:
                                    bad.append((e, s, v, wv, we))
                        semval[s] = max(semval.get(s, 0), v)
                        sn = dict(clock[e])
                        sn[s] = v
                        prev = snap.get((s, v))
                        if prev is None:
                            snap[(s, v)] = sn
                        else:
                            for ss, vv in sn.items():
                                if prev.get(ss, 0) > vv:
                                    sn[ss] = prev[ss]
                            snap[(s, v)] = sn
                    pos[e] += 1
                    progress = True
        stuck = {e: (pos[e], EV[e][pos[e]] if pos[e] < len(EV[e]) else None)
                 for e in ENGS if pos[e] < len(EV[e])}
        return bad, stuck

    def CC(k_in, k_out):
        semc["sCC"] += 1
        em("gpsimd", lambda e, a=k_in, b=k_out: e.collective_compute(
            "AllGather", Alu.bypass, replica_groups=RG,
            ins=[a.ap().opt()], outs=[b.ap().opt()]).then_inc(sems["sCC"], 1))
        EV["gpsimd"].append(("inc", "sCC", semc["sCC"]))
        return semc["sCC"]

    def dbl_rows(q, dst2d, width_slice, upto=128):
        k, qv = 1, semc[q]
        while k < upto:
            kk = min(k, upto - k)
            qv = DMA("sync", q, dst2d[k:k + kk, width_slice],
                     dst2d[0:kk, width_slice])
            k += kk
        return qv

    ident = sb("identb_s", [128, 128], BF16)
    vdead = sb("vdead", [1, 8], F32, high=True)

    def vpad():
        em("vector", lambda e: e.tensor_copy(vdead[0:1, 0:4],
                                             ident[0:1, 0:4]))
    ones1F = sb("ones1F_s", [1, 128], BF16)
    sel32 = sb("sel32_s", [32, 32 * 128], BF16)
    onehot = sb("onehot_s", [8, 1], BF16)
    magics = sb("magics_s", [128, 2], F32)
    negA = sb("negA", [128, G * S], F32)
    cw_sb = sb("cw_sb", [128, G * 4], F32)
    cb_sb = sb("cb_sb", [128, G], F32)
    bdt_sb = sb("bdt_sb", [128, G], F32)
    DMA("sync", "qS", ident[:, :], ident_in.ap())
    DMA("sync", "qS", ones1F[:, :], onesF_in.ap())
    DMA("sync", "qS", sel32[:, :], sel32_in.ap())
    DMA("sync", "qS", onehot[:, :], onehot_in.ap())
    DMA("sync", "qS", magics[:, :], magic_in.ap())
    for g in range(G):
        DMA("sync", "qS", negA[:, g * S:(g + 1) * S],
            alog.ap()[128 * g:128 * (g + 1), :])
        DMA("sync", "qS", cw_sb[:, g * 4:(g + 1) * 4],
            cw.ap()[128 * g:128 * (g + 1), :])
        DMA("sync", "qS", cb_sb[:, g:g + 1], cb.ap()[128 * g:128 * (g + 1), :])
        DMA("sync", "qS", bdt_sb[:, g:g + 1], bdt.ap()[128 * g:128 * (g + 1), :])
    qs_p0 = semc["qS"]

    x_sb = sb("x_sb", [128, 17 * DM], F32)
    xrows = [128] * 16 + [3]
    em("gpsimd", lambda e: e.memset(x_sb[0:128, 16 * DM:17 * DM], 0.0))
    sg_xz = INC("gpsimd", "sG")
    qs_x = [DMA("sync", "qX", x_sb[0:xrows[i], i * DM:(i + 1) * DM],
                x_in.ap()[128 * i:128 * i + xrows[i], :]) for i in range(16)]
    W("sync", "sG", sg_xz)
    qs_x.append(DMA("sync", "qX", x_sb[0:3, 16 * DM:17 * DM],
                    x_in.ap()[2048:2051, :]))

    W("scalar", "qS", qs_p0)
    em("scalar", lambda e: e.activation(negA[:, :], negA[:, :], Act.Exp))
    sc_nA = INC("scalar", "sC")
    W("vector", "sC", sc_nA)
    em("vector", lambda e: e.tensor_scalar(negA[:, :], negA[:, :], -1.0, None,
                                           Alu.mult))
    sv_negA = INC("vector", "sV")

    xln = sb("xln", [128, 17 * DM], BF16)
    st = sb("st", [128, 17 * 6], F32)
    scr_sq = sb("scr_sq", [128, DM], F32)
    NT1 = 17
    su, sq_, rc, rs, mn, b2 = (st[:, k * NT1:(k + 1) * NT1] for k in range(6))
    W("vector", "qX", qs_x[-1])
    W("scalar", "qX", qs_x[-1])
    W("vector", "sG", sg_xz)
    W("scalar", "sG", sg_xz)
    for i in range(17):
        em("vector", lambda e, i=i: e.tensor_reduce(
            su[:, i:i + 1], x_sb[:, i * DM:(i + 1) * DM], AxX, Alu.add))
        em("scalar", lambda e, i=i: e.activation(
            scr_sq[:, :], x_sb[:, i * DM:(i + 1) * DM], Act.Square,
            accum_out=sq_[:, i:i + 1]))
    sc_sq = INC("scalar", "sC")
    vpad()
    vpad()
    em("vector", lambda e: e.tensor_scalar(mn[:, :], su[:, :], 1.0 / DM, None,
                                           Alu.mult))
    W("vector", "sC", sc_sq)
    em("vector", lambda e: e.tensor_scalar(rc[:, :], sq_[:, :], 1.0 / DM, EPS,
                                           Alu.mult, Alu.add))
    em("vector", lambda e: e.scalar_tensor_tensor(rs[:, :], mn[:, :], 1.0,
                                                  mn[:, :], Alu.mult, Alu.mult))
    vpad()
    vpad()
    em("vector", lambda e: e.tensor_tensor(rc[:, :], rc[:, :], rs[:, :],
                                           Alu.subtract))
    vpad()
    vpad()
    em("vector", lambda e: e.reciprocal(rc[:, :], rc[:, :]))
    sv1 = INC("vector", "sV")
    W("scalar", "sV", sv1)
    em("scalar", lambda e: e.activation(rs[:, :], rc[:, :], Act.Sqrt))
    sc_rs = INC("scalar", "sC")
    W("vector", "sC", sc_rs)
    em("vector", lambda e: e.scalar_tensor_tensor(b2[:, :], mn[:, :], -1.0,
                                                  rs[:, :], Alu.mult, Alu.mult))
    sv_b2 = INC("vector", "sV")
    W("scalar", "sV", sv_b2)
    for i in range(17):
        em("scalar", lambda e, i=i: e.activation(
            xln[0:xrows[i], i * DM:(i + 1) * DM],
            x_sb[0:xrows[i], i * DM:(i + 1) * DM], Act.Identity,
            bias=b2[0:xrows[i], i:i + 1], scale=rs[0:xrows[i], i:i + 1]))
    sc_ln = INC("scalar", "sC")

    xlnT = sb("xlnT", [128, 8 * TH], BF16)
    ptp = [ps(f"ptp{j}", [128, 128], BF16) for j in range(2)]
    W("tensor", "qS", qs_p0)
    W("tensor", "sC", sc_ln)
    ntp = 0
    for i in range(17):
        for k in range(8):
            j = ntp % 2
            if ntp >= 2:
                W("tensor", "sC", sc_ln + ntp - 1)
            em("tensor", lambda e, i=i, k=k, j=j: e.transpose(
                ptp[j][0:128, 0:xrows[i]],
                xln[0:xrows[i], i * DM + 128 * k:i * DM + 128 * (k + 1)],
                ident[0:xrows[i], 0:xrows[i]]))
            stq = INC("tensor", "sT")
            W("scalar", "sT", stq)
            em("scalar", lambda e, i=i, k=k, j=j: e.copy(
                xlnT[:, k * TH + 128 * i:k * TH + 128 * i + xrows[i]],
                ptp[j][0:128, 0:xrows[i]]))
            INC("scalar", "sC")
            ntp += 1
    sc_xlnT = semc["sC"]
    sb_free("x_sb", "xln", "st", "ptp0", "ptp1", "scr_sq")

    xsT = sb("xsT", [128, G * T], BF16)
    buf_w = [sb(f"buf_w{j}", [128, FI], F32) for j in range(2)]
    wqb = [sb(f"wqb{j}", [128, FI], BF16) for j in range(2)]
    tpb = [sb(f"tpb{j}", [128, FI], BF16) for j in range(2)]
    sq_s = sb("sq_s", [128, 2 * 96], F32)
    sq_i = sb("sq_i", [128, 2 * 96], F32)
    pQs = ps("pQs", [128, 128], F32)

    def emit_dequant():
        vdone = [0, 0]
        tdone = [0, 0]
        tpdma = [0, 0]
        evQ = []
        for (w_n, w_q, O, I) in [
                (wu_nat, sc_wuT, FI, DM),
                (wg_nat, sc_wgT, FI, DM),
                (wd_nat, sc_wdT, DM, FI)]:
            ot, kt, NB = O // 128, I // 128, I // 32
            for j in range(ot):
                bj = j % 2
                W("sync", "sV", vdone[1 - bj])
                qv = DMA("sync", "qS2", buf_w[bj][:, 0:I],
                         w_n.ap()[128 * j:128 * (j + 1), :])
                W("vector", "qS2", qv)
                W("vector", "sT", tdone[bj])
                em("vector", lambda e, bj=bj, I=I, NB=NB: e.tensor_reduce(
                    sq_s[:, bj * 96:bj * 96 + NB].rearrange(
                        "p (b o) -> p b o", o=1),
                    buf_w[bj][:, 0:I].rearrange("p (b q) -> p b q", q=32),
                    AxX, Alu.max, apply_absolute_value=True))
                vpad()
                vpad()
                em("vector", lambda e, bj=bj, NB=NB: e.tensor_scalar(
                    sq_s[:, bj * 96:bj * 96 + NB],
                    sq_s[:, bj * 96:bj * 96 + NB], 1.0 / 7.0, 1e-12,
                    Alu.mult, Alu.max))
                vpad()
                vpad()
                em("vector", lambda e, bj=bj, NB=NB: e.reciprocal(
                    sq_i[:, bj * 96:bj * 96 + NB],
                    sq_s[:, bj * 96:bj * 96 + NB]))
                vpad()
                vpad()
                em("vector", lambda e, bj=bj, I=I, NB=NB: e.tensor_tensor(
                    buf_w[bj][:, 0:I].rearrange("p (b q) -> p b q", q=32),
                    buf_w[bj][:, 0:I].rearrange("p (b q) -> p b q", q=32),
                    sq_i[:, bj * 96:bj * 96 + NB, None].broadcast_to(
                        [128, NB, 32]), Alu.mult))
                em("vector", lambda e, bj=bj, I=I: e.tensor_scalar(
                    buf_w[bj][:, 0:I], buf_w[bj][:, 0:I], MAGIC, MAGIC,
                    Alu.add, Alu.subtract))
                em("vector", lambda e, bj=bj, I=I, NB=NB: e.tensor_tensor(
                    wqb[bj][:, 0:I].rearrange("p (b q) -> p b q", q=32),
                    buf_w[bj][:, 0:I].rearrange("p (b q) -> p b q", q=32),
                    sq_s[:, bj * 96:bj * 96 + NB, None].broadcast_to(
                        [128, NB, 32]), Alu.mult))
                vdone[bj] = INC("vector", "sV")
                W("tensor", "sV", vdone[bj])
                for k in range(kt):
                    if evQ:
                        W("tensor", "sC", evQ[-1])
                    em("tensor", lambda e, bj=bj, k=k: e.matmul(
                        pQs[:, :], wqb[bj][:, 128 * k:128 * (k + 1)],
                        ident[:, :], start=True, stop=True))
                    stq = INC("tensor", "sT")
                    W("scalar", "sT", stq)
                    if k == 0:
                        W("scalar", "qTP", tpdma[1 - bj])
                    em("scalar", lambda e, bj=bj, k=k: e.copy(
                        tpb[bj][:, 128 * k:128 * (k + 1)], pQs[:, :]))
                    evQ.append(INC("scalar", "sC"))
                tdone[bj] = semc["sT"]
                W("sync", "sC", evQ[-1])
                tpdma[bj] = DMA("sync", "qTP", w_q.ap()[j, :, :],
                                tpb[bj][:, 0:I])

    W("sync", "sC", semc["sC"])
    W("sync", "sT", semc["sT"])
    W("sync", "sV", semc["sV"])
    emit_dequant()
    qS2_dq = semc["qTP"]

    win_b = [sb(f"win{j}", [128, 8 * 128], BF16) for j in range(2)]
    xc_b = [sb(f"xc{j}", [128, TH + 1], BF16) for j in range(2)]
    cv = [sb(f"cv{j}", [128, T], BF16) for j in range(4)]
    pxc = [ps(f"pxc{j}", [128, 512]) for j in range(2)]
    slabs = [(0, 3), (3, 512), (515, 512), (1027, 512), (1539, 512)]
    xs_dma = []
    xc_ev = []
    cv_done = []
    st_xc = []

    for g in range(G):
        bj = g % 2
        if g >= 1:
            W("gpsimd", "sT", st_xc[-1])
        w_dmas = [DMA("gpsimd", "qG", win_b[bj][:, k * 128:(k + 1) * 128],
                      w_in.ap()[128 * k:128 * (k + 1), 128 * g:128 * (g + 1)])
                  for k in range(8)]
        W("tensor", "qG", w_dmas[-1])
        if g == 0:
            W("tensor", "sC", sc_xlnT)
        if g >= 2:
            W("tensor", "sV", cv_done[g - 2])
        for si, (t0, ln_) in enumerate(slabs):
            gs = g * 5 + si
            pj = gs % 2
            if gs >= 2:
                W("tensor", "sC", xc_ev[gs - 2])
            for k in range(8):
                em("tensor", lambda e, k=k, t0=t0, ln_=ln_, pj=pj, bj=bj:
                   e.matmul(pxc[pj][:, 0:ln_],
                            win_b[bj][:, k * 128:(k + 1) * 128],
                            xlnT[:, k * TH + t0:k * TH + t0 + ln_],
                            start=(k == 0), stop=(k == 7)))
            st_xc.append(INC("tensor", "sT"))
            W("scalar", "sT", st_xc[-1])
            if g >= 2 and si == 0:
                W("scalar", "sV", cv_done[g - 2])
            em("scalar", lambda e, t0=t0, ln_=ln_, pj=pj, bj=bj: e.copy(
                xc_b[bj][:, t0:t0 + ln_], pxc[pj][:, 0:ln_]))
            xc_ev.append(INC("scalar", "sC"))
        W("vector", "sC", xc_ev[-1])
        for kk in range(4):
            em("vector", lambda e, g=g, bj=bj, kk=kk: e.tensor_scalar(
                cv[kk][:, :], xc_b[bj][:, kk:kk + T],
                cw_sb[:, 4 * g + kk:4 * g + kk + 1], None, Alu.mult))
        em("vector", lambda e: e.tensor_tensor(cv[0][:, :], cv[0][:, :],
                                               cv[1][:, :], Alu.add))
        em("vector", lambda e: e.tensor_tensor(cv[2][:, :], cv[2][:, :],
                                               cv[3][:, :], Alu.add))
        em("vector", lambda e: e.tensor_tensor(cv[0][:, :], cv[0][:, :],
                                               cv[2][:, :], Alu.add))
        cv_done.append(INC("vector", "sV"))
        W("scalar", "sV", cv_done[-1])
        em("scalar", lambda e, g=g: e.activation(
            xsT[:, g * T:(g + 1) * T], cv[0][:, :], Act.Silu,
            bias=cb_sb[:, g:g + 1]))
        sc_sil = INC("scalar", "sC")
        W("sync", "sC", sc_sil)
        xs_dma.append(DMA("sync", "qS", sc_xs.ap()[g, :, :],
                          xsT[:, g * T:(g + 1) * T]))
        W("vector", "sC", sc_sil)
    sc_xsT_all = semc["sC"]
    sb_free("win0", "win1", "xc0", "xc1", "cv0", "cv1", "cv2", "cv3",
            "pxc0", "pxc1")

    wdt_b = [sb(f"wdt{j}", [128, 16 * 128], BF16) for j in range(2)]
    dtb = sb("dtb", [128, T], BF16)
    xtb = sb("xtb", [128, T], BF16)
    pdt = [ps(f"pdt{j}", [128, 512]) for j in range(2)]
    dt_scr = sb("dt_scr", [128, 512], F32)
    BCb = sb("BCb", [32, T], BF16)
    pbc = ps("pbc", [32, 512])
    dt_ev = []
    st_dt = []
    W("gpsimd", "sT", semc["sT"])
    for g in range(G):
        bj = g % 2
        if g >= 1:
            W("gpsimd", "sT", st_dt[-1])
        w_dmas = [DMA("gpsimd", "qG", wdt_b[bj][:, k * 128:(k + 1) * 128],
                      w_dt.ap()[128 * k:128 * (k + 1), 128 * g:128 * (g + 1)])
                  for k in range(G)]
        W("tensor", "qG", w_dmas[-1])
        if g == 0:
            W("tensor", "sC", sc_xsT_all)
        for si in range(4):
            gs = g * 4 + si
            pj = gs % 2
            if gs >= 2:
                W("tensor", "sC", dt_ev[gs - 2])
            for k in range(G):
                em("tensor", lambda e, k=k, si=si, pj=pj, bj=bj: e.matmul(
                    pdt[pj][:, :], wdt_b[bj][:, k * 128:(k + 1) * 128],
                    xsT[:, k * T + 512 * si:k * T + 512 * (si + 1)],
                    start=(k == 0), stop=(k == G - 1)))
            st_dt.append(INC("tensor", "sT"))
            W("scalar", "sT", st_dt[-1])
            em("scalar", lambda e, g=g, pj=pj: e.activation(
                dt_scr[:, :], pdt[pj][:, :], Act.Exp, bias=bdt_sb[:, g:g + 1]))
            em("scalar", lambda e, si=si: e.activation(
                dtb[:, 512 * si:512 * (si + 1)], dt_scr[:, :], Act.Ln,
                bias=1.0))
            dt_ev.append(INC("scalar", "sC"))
        W("vector", "sC", dt_ev[-1])
        em("vector", lambda e, g=g: e.tensor_tensor(
            xtb[:, :], xsT[:, g * T:(g + 1) * T], dtb[:, :], Alu.mult))
        sv_xt = INC("vector", "sV")
        W("sync", "sC", dt_ev[-1])
        q_dt = DMA("sync", "qDtW", sc_dt.ap()[g, :, :], dtb[:, :])
        W("sync", "sV", sv_xt)
        q_xt = DMA("sync", "qXtW", sc_xt.ap()[g, :, :], xtb[:, :])
        W("scalar", "qDtW", q_dt)
        W("scalar", "sV", sv_xt)
        W("vector", "qXtW", q_xt)

    wxp_sb = sb("wxp_sb", [128, G * 32], BF16)
    W("gpsimd", "sT", semc["sT"])
    for k in range(G):
        DMA("gpsimd", "qG", wxp_sb[:, k * 32:(k + 1) * 32],
            w_xp.ap()[128 * k:128 * (k + 1), :])
    qg_xp = semc["qG"]
    W("tensor", "qG", qg_xp)
    bc_ev = []
    for si in range(4):
        if si >= 1:
            W("tensor", "sC", bc_ev[-1])
        for k in range(G):
            em("tensor", lambda e, k=k, si=si: e.matmul(
                pbc[:, :], wxp_sb[:, k * 32:(k + 1) * 32],
                xsT[:, k * T + 512 * si:k * T + 512 * (si + 1)],
                start=(k == 0), stop=(k == G - 1)))
        stq = INC("tensor", "sT")
        W("scalar", "sT", stq)
        em("scalar", lambda e, si=si: e.copy(BCb[:, 512 * si:512 * (si + 1)],
                                             pbc[:, :]))
        bc_ev.append(INC("scalar", "sC"))
    sc_bc = bc_ev[-1]

    winz_b = [sb(f"winz{j}", [128, 8 * 128], BF16) for j in range(2)]
    szb = sb("szb", [128, T], BF16)
    psz = [ps(f"psz{j}", [128, 512]) for j in range(2)]
    sz_ev = []
    st_z = []
    W("gpsimd", "sT", semc["sT"])
    for g in range(G):
        bj = g % 2
        if g >= 1:
            W("gpsimd", "sT", st_z[-1])
        w_dmas = [DMA("gpsimd", "qG", winz_b[bj][:, k * 128:(k + 1) * 128],
                      w_in.ap()[128 * k:128 * (k + 1),
                                DI + 128 * g:DI + 128 * (g + 1)])
                  for k in range(8)]
        W("tensor", "qG", w_dmas[-1])
        for si in range(4):
            gs = g * 4 + si
            pj = gs % 2
            if gs >= 2:
                W("tensor", "sC", sz_ev[gs - 2])
            for k in range(8):
                em("tensor", lambda e, k=k, si=si, pj=pj, bj=bj: e.matmul(
                    psz[pj][:, :], winz_b[bj][:, k * 128:(k + 1) * 128],
                    xlnT[:, k * TH + 3 + 512 * si:k * TH + 3 + 512 * (si + 1)],
                    start=(k == 0), stop=(k == 7)))
            st_z.append(INC("tensor", "sT"))
            W("scalar", "sT", st_z[-1])
            em("scalar", lambda e, si=si, pj=pj: e.activation(
                szb[:, 512 * si:512 * (si + 1)], psz[pj][:, :], Act.Silu))
            sz_ev.append(INC("scalar", "sC"))
        W("sync", "sC", sz_ev[-1])
        q_sz = DMA("sync", "qSzW", sc_sz.ap()[g, :, :], szb[:, :])
        W("scalar", "qSzW", q_sz)

    sb_free("xlnT", "winz0", "winz1", "psz0", "psz1", "szb", "wdt0", "wdt1",
            "dtb", "xtb", "pdt0", "pdt1", "dt_scr", "wxp_sb", "pbc",
            "buf_w0", "buf_w1", "wqb0", "wqb1", "tpb0", "tpb1", "sq_s",
            "sq_i", "pQs", "xsT", "cw_sb", "cb_sb", "bdt_sb")

    Bbc = sb("Bbc", [128, S * T], BF16)
    Cbc = sb("Cbc", [128, S * T], BF16)
    pbc2 = ps("pbc2", [128, 512])
    W("sync", "sC", semc["sC"])
    W("sync", "sT", semc["sT"])
    W("sync", "sV", semc["sV"])
    W("sync", "sG", semc["sG"])
    W("sync", "qS", xs_dma[-1])
    W("sync", "qS2", semc["qS2"])
    W("sync", "qDtW", semc["qDtW"])
    W("sync", "qXtW", semc["qXtW"])
    W("sync", "qSzW", semc["qSzW"])
    W("tensor", "sC", sc_bc)
    W("tensor", "qS", qs_p0)
    bc_ev2 = []
    for idx in range(2 * S):
        s_ = idx % S
        dst = Bbc if idx < S else Cbc
        srow = s_ if idx < S else S + s_
        for sl in range(4):
            if bc_ev2:
                if len(bc_ev2) >= 1:
                    W("tensor", "sC", bc_ev2[-1])
            em("tensor", lambda e, srow=srow, sl=sl: e.matmul(
                pbc2[:, :], sel32[:, 128 * srow:128 * (srow + 1)]
                if srow < 32 else None,
                BCb[0:32, 512 * sl:512 * (sl + 1)],
                start=True, stop=True))
            stq = INC("tensor", "sT")
            W("scalar", "sT", stq)
            em("scalar", lambda e, dst=dst, s_=s_, sl=sl: e.copy(
                dst[:, s_ * T + 512 * sl:s_ * T + 512 * (sl + 1)],
                pbc2[:, :]))
            bc_ev2.append(INC("scalar", "sC"))
    qs_bc = 0
    sv_bc = bc_ev2[-1]
    if debug and debug != 2:
        DMA("sync", "qS", dbg["dbg_bcb"].ap(), BCb[:, :])
        DMA("sync", "qS", dbg["dbg_bbc_f"].ap(), Bbc[:, :])
        DMA("sync", "qS", dbg["dbg_cbc_f"].ap(), Cbc[:, :])
    sb_free("BCb")
    W("scalar", "sT", semc["sT"])
    W("gpsimd", "sT", semc["sT"])
    W("sync", "sC", sv_bc)

    dd_s = [sb(f"dd_s{j}", [128, 128], BF16) for j in range(2)]
    W("gpsimd", "sT", semc["sT"])

    dab = [sb(f"dab{j}", [128, T], F32) for j in range(2)]
    initrow = sb("initrow", [1, 2048], BF16)
    agi_sb = sb("agi_sb", [8, 2048], BF16)
    dt_s = [sb(f"dt_s{j}", [128, T], BF16) for j in range(2)]
    xt_s = [sb(f"xt_s{j}", [128, T], BF16) for j in range(2)]
    xs_s = [sb("xs_s0", [128, T], BF16)]
    sz_s = [sb("sz_s0", [128, T], BF16)]
    dbb = [sb("dbb0", [128, T], BF16)]
    hb = [sb(f"hb{j}", [128, T], BF16) for j in range(2)]
    tmpb = [sb("tmpb0", [128, T], BF16)]
    gtd = [sb("gtd0", [128, T], BF16)]
    init_sb = [sb(f"init{j}", [128, S], F32) for j in range(2)]
    hend = [sb(f"hend{j}", [128, S], BF16) for j in range(2)]
    pY = ps("pY", [128, 2048])
    psel = ps("psel", [1, 512])

    em("gpsimd", lambda e: e.memset(agi_sb[:, :], 0.0))
    sg_z = INC("gpsimd", "sG")

    stage_end_sV = {}
    stage_end_sC = {}
    stage_start_sC = {}
    stage_start_sV = {}
    stage_end_sT = {}
    stage_gtd_dma = {}
    stage_he_dma = {}
    stage_sel = {}
    stage_d_done = {}
    stage_gate_sv = {}
    sv_gtd_prev = 0
    for r in range(NST):
        lg = r % G
        rp = r % 2
        if r >= 2:
            W("sync", "sV", stage_end_sV[r - 2])
            W("sync", "sC", stage_end_sC[r - 2])
            W("sync", "sT", stage_end_sT[r - 2])
        if r >= 1:
            W("sync", "sC", stage_start_sC[r - 1])
            W("sync", "sV", stage_start_sV[r - 1])
        q_dt = DMA("sync", "qD", dt_s[rp][:, :], sc_dt.ap()[lg, :, :])
        q_xt = DMA("sync", "qT2", xt_s[rp][:, :], sc_xt.ap()[lg, :, :])
        if r >= 1:
            W("sync", "sT", stage_d_done[r - 1])
            W("sync", "sV", stage_gate_sv[r - 1])
        q_xs = DMA("sync", "qX2", xs_s[0][:, :], sc_xs.ap()[lg, :, :])
        q_sz = DMA("sync", "qZ2", sz_s[0][:, :], sc_sz.ap()[lg, :, :])

        if r >= 2:
            W("gpsimd", "sT", stage_d_done[r - 1])
        q_dd = DMA("gpsimd", "qG", dd_s[rp][:, :],
                   ddiag.ap()[128 * lg:128 * (lg + 1), :])

        if r > 0:
            W("gpsimd", "sCC", r)
            if r >= 2:
                W("gpsimd", "sT", stage_sel[r - 1])
            q_agi = DMA("gpsimd", "qG2", agi_sb[:, :], ag_o[r - 1].ap())
            W("tensor", "qG2", q_agi)
        else:
            W("tensor", "sG", sg_z)
            W("tensor", "qS", qs_p0)
        if r >= 2:
            W("tensor", "sV", sv_gtd_prev)
        sel_ev = []
        for j4 in range(4):
            if j4 >= 1:
                W("tensor", "sC", sel_ev[-1])
            elif r >= 1:
                W("tensor", "qI", q_init_prev)
            em("tensor", lambda e, j4=j4: e.matmul(
                psel[:, :], onehot[:, :], agi_sb[:, 512 * j4:512 * (j4 + 1)],
                start=True, stop=True))
            stq = INC("tensor", "sT")
            W("scalar", "sT", stq)
            em("scalar", lambda e, j4=j4: e.copy(
                initrow[:, 512 * j4:512 * (j4 + 1)], psel[:, :]))
            sel_ev.append(INC("scalar", "sC"))
        stage_sel[r] = semc["sT"]
        W("gpsimd", "sC", sel_ev[-1])
        if r >= 2:
            W("gpsimd", "sV", stage_end_sV[r - 2])
        q_init = DMA("gpsimd", "qI", init_sb[rp][:, :], initrow[0:1, :])
        q_init_prev = q_init

        if r == 0:
            W("vector", "sC", sv_bc)
        W("vector", "qT2", q_xt)
        W("scalar", "qD", q_dt)
        if r == 0:
            W("scalar", "sV", sv_negA)
        sv_base = semc["sV"]
        sc_base = semc["sC"]
        st_base = semc["sT"]
        hend_sc = {}
        stage_start_sC[r] = sc_base + 1
        stage_start_sV[r] = sv_base + 1
        for s_ in range(S):
            sp = s_ % 2
            if s_ >= 2:
                W("scalar", "sV", sv_base + s_ - 1)
            em("scalar", lambda e, lg=lg, s_=s_, sp=sp, rp=rp: e.activation(
                dab[sp][:, :], dt_s[rp][:, :], Act.Copy, bias=1.0,
                scale=negA[:, lg * S + s_:lg * S + s_ + 1]))
            sc_da = INC("scalar", "sC")
            em("vector", lambda e, s_=s_, rp=rp: e.tensor_tensor(
                dbb[0][:, :], xt_s[rp][:, :], Bbc[:, s_ * T:(s_ + 1) * T],
                Alu.mult))
            W("vector", "sC", sc_da)
            if s_ == 0:
                W("vector", "qI", q_init)
            if s_ >= 2:
                W("vector", "sC", hend_sc[s_ - 2])
            em("vector", lambda e, s_=s_, sp=sp, rp=rp: e.tensor_tensor_scan(
                hb[sp][:, :], dab[sp][:, :], dbb[0][:, :],
                init_sb[rp][:, s_:s_ + 1], Alu.mult, Alu.add))
            if s_ >= 1:
                W("vector", "sT", st_base + s_)
            em("vector", lambda e, s_=s_, sp=sp: e.tensor_tensor(
                tmpb[0][:, :], hb[sp][:, :], Cbc[:, s_ * T:(s_ + 1) * T],
                Alu.mult))
            sv_tmp = INC("vector", "sV")
            if debug and debug != 2 and r == 0 and s_ == 0:
                W("sync", "sV", sv_tmp)
                DMA("sync", "qS", dbg["dbg_da0"].ap(), dab[sp][:, :])
                DMA("sync", "qS", dbg["dbg_db0"].ap(), dbb[0][:, :])
                DMA("sync", "qS", dbg["dbg_h0"].ap(), hb[sp][:, :])
                DMA("sync", "qS", dbg["dbg_tmp0"].ap(), tmpb[0][:, :])
                DMA("sync", "qS", dbg["dbg_init0"].ap(), init_sb[rp][:, :])
                DMA("sync", "qS", dbg["dbg_bb0"].ap(), Bbc[:, 0:T])
                DMA("sync", "qS", dbg["dbg_cb0"].ap(), Cbc[:, 0:T])
                DMA("sync", "qS", dbg["dbg_na"].ap(), negA[:, :])
            W("scalar", "sV", sv_tmp)
            if s_ == 0 and r >= 2:
                W("scalar", "qG2", stage_he_dma[r - 2])
            em("scalar", lambda e, s_=s_, sp=sp, rp=rp: e.copy(
                hend[rp][:, s_:s_ + 1], hb[sp][:, T - 1:T]))
            hend_sc[s_] = INC("scalar", "sC")
            if s_ == 0:
                W("tensor", "qX2", q_xs)
                W("tensor", "qG", q_dd)
                for j4 in range(4):
                    em("tensor", lambda e, j4=j4, rp=rp: e.matmul(
                        pY[:, 512 * j4:512 * (j4 + 1)], dd_s[rp][:, :],
                        xs_s[0][:, 512 * j4:512 * (j4 + 1)],
                        start=True, stop=False))
            W("tensor", "sV", sv_tmp)
            for j4 in range(4):
                em("tensor", lambda e, s_=s_, j4=j4: e.matmul(
                    pY[:, 512 * j4:512 * (j4 + 1)], ident[:, :],
                    tmpb[0][:, 512 * j4:512 * (j4 + 1)],
                    start=False, stop=(s_ == S - 1)))
            INC("tensor", "sT")
        stage_d_done[r] = st_base + 1
        st_y = semc["sT"]
        W("vector", "sT", st_y)
        W("vector", "qZ2", q_sz)
        if r >= 1:
            W("vector", "qGtW", stage_gtd_dma[r - 1])
        for j4 in range(4):
            em("vector", lambda e, j4=j4: e.tensor_tensor(
                gtd[0][:, 512 * j4:512 * (j4 + 1)],
                pY[:, 512 * j4:512 * (j4 + 1)],
                sz_s[0][:, 512 * j4:512 * (j4 + 1)], Alu.mult))
        sv_gtd = INC("vector", "sV")
        sv_gtd_prev = sv_gtd
        stage_gate_sv[r] = sv_gtd
        W("sync", "sV", sv_gtd)
        stage_gtd_dma[r] = DMA("sync", "qGtW", sc_gated.ap()[lg, :, :],
                               gtd[0][:, :])
        W("gpsimd", "sC", hend_sc[S - 1])
        stage_he_dma[r] = DMA("gpsimd", "qG2", ag_i[r].ap(), hend[rp][:, :])
        W("gpsimd", "qG2", stage_he_dma[r])
        CC(ag_i[r], ag_o[r])
        stage_end_sV[r] = semc["sV"]
        stage_end_sC[r] = semc["sC"]
        stage_end_sT[r] = semc["sT"]

    if debug and debug != 2:
        for g in range(G):
            for nm, scd in [("dbg_xs", sc_xs), ("dbg_dt", sc_dt),
                            ("dbg_xt", sc_xt), ("dbg_gtd", sc_gated),
                            ("dbg_sz", sc_sz)]:
                DMA("sync", "qS", dbg[nm].ap()[g, :, :], scd.ap()[g, :, :])

    sb_free("Bbc", "Cbc", "dt_s0", "dt_s1", "xt_s0", "xt_s1", "xs_s0",
            "sz_s0", "dab0", "dab1", "dbb0",
            "hb0", "hb1", "tmpb0", "gtd0", "agi_sb",
            "initrow", "init0", "init1", "hend0", "hend1", "pY", "psel",
            "pbc2", "dd_s0", "dd_s1")

    x2 = sb("x2", [128, 16 * DM], F32)
    xr_b = [sb(f"xr{j}", [128, DM], F32) for j in range(2)]
    gin_b = [sb(f"gin{j}", [128, G * 128], BF16) for j in range(2)]
    wo_sb = sb("wo_sb", [128, G * DM], BF16)
    pO = [ps(f"pO{j}", [128, 512]) for j in range(2)]
    for e_ in ("gpsimd", "sync"):
        W(e_, "sT", semc["sT"])
        W(e_, "sV", semc["sV"])
        W(e_, "sC", semc["sC"])
        W(e_, "sG", semc["sG"])
    W("sync", "qGtW", semc["qGtW"])
    W("sync", "qG2", semc["qG2"])
    for k in range(G):
        DMA("gpsimd", "qG", wo_sb[:, k * DM:(k + 1) * DM],
            w_out.ap()[128 * k:128 * (k + 1), :])
    qg_wo = semc["qG"]
    x2_ev = []
    st_o = []
    for i in range(16):
        bj = i % 2
        if i >= 1:
            W("sync", "sT", st_o[-1])
        if i >= 2:
            W("sync", "sV", x2_ev[-1])
        for k in range(G):
            DMA("sync", "qP7", gin_b[bj][:, k * 128:(k + 1) * 128],
                sc_gated.ap()[k, :, 128 * i:128 * (i + 1)])
        q_x = DMA("sync", "qP7", xr_b[bj][:, :],
                  x_in.ap()[3 + 128 * i:3 + 128 * (i + 1), :])
        W("tensor", "qP7", q_x)
        if i == 0:
            W("tensor", "qG", qg_wo)
        for hf in range(2):
            if i >= 1:
                W("tensor", "sV", x2_ev[-2] if len(x2_ev) >= 2 else 0)
            for k in range(G):
                em("tensor", lambda e, k=k, hf=hf, bj=bj: e.matmul(
                    pO[hf][:, :], gin_b[bj][:, k * 128:(k + 1) * 128],
                    wo_sb[:, k * DM + 512 * hf:k * DM + 512 * (hf + 1)],
                    start=(k == 0), stop=(k == G - 1)))
            st_o.append(INC("tensor", "sT"))
            W("vector", "sT", st_o[-1])
            W("vector", "qP7", q_x)
            em("vector", lambda e, i=i, hf=hf, bj=bj: e.tensor_tensor(
                x2[:, i * DM + 512 * hf:i * DM + 512 * (hf + 1)],
                xr_b[bj][:, 512 * hf:512 * (hf + 1)], pO[hf][:, :], Alu.add))
            x2_ev.append(INC("vector", "sV"))
    sv_x2 = x2_ev[-1]
    sb_free("gin0", "gin1", "xr0", "xr1", "pO0", "pO1", "wo_sb", "negA")

    W("sync", "sV", sv_x2)
    for i in range(16):
        DMA("sync", "qOut", sc_x2.ap()[i, :, :], x2[:, i * DM:(i + 1) * DM])
    q_x2w = semc["qOut"]
    if debug:
        for i in range(16):
            DMA("sync", "qS", dbg["dbg_x2"].ap()[128 * i:128 * (i + 1), :],
                x2[:, i * DM:(i + 1) * DM])

    xln2 = sb("xln2", [128, 16 * DM], F32)
    st2 = sb("st2", [128, 16 * 6], F32, high=True)
    scr2 = sb("scr2", [128, DM], F32, high=True)
    mx = sb("mx", [128, 4], F32, high=True)
    psc1 = ps("psc1", [128, 512])
    mrow = sb("mrow", [1, 128], F32, high=True)
    sc1 = sb("sc1", [128, 8], F32, high=True)
    sc1c = sb("sc1c", [1, 24], BF16, high=True)
    rsc = sb("rsc", [1, 8], F32, high=True)
    su2, sq2, rc2, rs2, mn2, b22 = (st2[:, k * 16:(k + 1) * 16]
                                    for k in range(6))
    for i in range(16):
        W("vector", "sV", sv_x2)
        W("scalar", "sV", sv_x2)
        em("vector", lambda e, i=i: e.tensor_reduce(
            su2[:, i:i + 1], x2[:, i * DM:(i + 1) * DM], AxX, Alu.add))
        em("scalar", lambda e, i=i: e.activation(
            scr2[:, :], x2[:, i * DM:(i + 1) * DM], Act.Square,
            accum_out=sq2[:, i:i + 1]))
    scq = INC("scalar", "sC")
    vpad()
    vpad()
    em("vector", lambda e: e.tensor_scalar(mn2[:, :], su2[:, :], 1.0 / DM,
                                           None, Alu.mult))
    W("vector", "sC", scq)
    em("vector", lambda e: e.tensor_scalar(rc2[:, :], sq2[:, :], 1.0 / DM, EPS,
                                           Alu.mult, Alu.add))
    em("vector", lambda e: e.scalar_tensor_tensor(rs2[:, :], mn2[:, :], 1.0,
                                                  mn2[:, :], Alu.mult,
                                                  Alu.mult))
    vpad()
    vpad()
    em("vector", lambda e: e.tensor_tensor(rc2[:, :], rc2[:, :], rs2[:, :],
                                           Alu.subtract))
    vpad()
    vpad()
    em("vector", lambda e: e.reciprocal(rc2[:, :], rc2[:, :]))
    svq = INC("vector", "sV")
    W("scalar", "sV", svq)
    em("scalar", lambda e: e.activation(rs2[:, :], rc2[:, :], Act.Sqrt))
    scq = INC("scalar", "sC")
    W("vector", "sC", scq)
    em("vector", lambda e: e.scalar_tensor_tensor(b22[:, :], mn2[:, :], -1.0,
                                                  rs2[:, :], Alu.mult,
                                                  Alu.mult))
    svq = INC("vector", "sV")
    W("scalar", "sV", svq)
    for i in range(16):
        em("scalar", lambda e, i=i: e.activation(
            xln2[:, i * DM:(i + 1) * DM], x2[:, i * DM:(i + 1) * DM],
            Act.Identity, bias=b22[:, i:i + 1], scale=rs2[:, i:i + 1]))
    sc_ln2 = INC("scalar", "sC")

    psc1_ev = [0]

    def gmax_to_scale(nsrc, src_fn, agk, col, wsem, wval, pair=False):
        nmax = 2 if pair else 1
        if nsrc:
            W("vector", wsem, wval)
            for i in range(nsrc):
                em("vector", lambda e, i=i: e.tensor_reduce(
                    mx[:, 2:3] if i else mx[:, 0:1], src_fn(i), AxX, Alu.max,
                    apply_absolute_value=True))
                if i:
                    vpad()
                    em("vector", lambda e: e.tensor_tensor(
                        mx[:, 0:1], mx[:, 0:1], mx[:, 2:3], Alu.max))
            sgq = INC("vector", "sV")
        else:
            assert wsem == "sV"
            sgq = wval
        W("sync", "sV", sgq)
        qv = DMA("sync", "qS", mrow[0:1, :], mx[:, 0:1])
        W("vector", "qS", qv)
        em("vector", lambda e: e.tensor_reduce(mx[0:1, 0:1], mrow[0:1, :],
                                               AxX, Alu.max))
        em("vector", lambda e: e.tensor_reduce(mx[0:1, 2:3], mrow[0:1, :],
                                               AxX, Alu.max))
        if pair:
            mr1 = INC("vector", "sV")
            W("sync", "sV", mr1)
            qv = DMA("sync", "qS", mrow[0:1, :], mx[:, 1:2])
            W("vector", "qS", qv)
            em("vector", lambda e: e.tensor_reduce(mx[0:1, 1:2], mrow[0:1, :],
                                                   AxX, Alu.max))
            em("vector", lambda e: e.tensor_reduce(mx[0:1, 3:4], mrow[0:1, :],
                                                   AxX, Alu.max))
        else:
            em("vector", lambda e: e.tensor_reduce(mx[0:1, 1:2], mrow[0:1, :],
                                                   AxX, Alu.max))
            em("vector", lambda e: e.tensor_reduce(mx[0:1, 3:4], mrow[0:1, :],
                                                   AxX, Alu.max))
        svq2 = INC("vector", "sV")
        W("sync", "sV", svq2)
        qv = DMA("sync", "qS", agm_i[agk].ap(), mx[0:1, 0:4])
        W("gpsimd", "qS", qv)
        ccv = CC(agm_i[agk], agm_o[agk])
        W("gpsimd", "sCC", ccv)
        qv = DMA("gpsimd", "qG2", mrow[0:1, 0:32], agm_o[agk].ap())
        W("vector", "qG2", qv)
        if pair:
            em("vector", lambda e: e.tensor_reduce(
                mx[0:1, 0:1],
                mrow[0:1, 0:32].rearrange("p (a b) -> p b a", b=2)[:, 0:1, :],
                AxX, Alu.max))
            em("vector", lambda e: e.tensor_reduce(
                mx[0:1, 1:2],
                mrow[0:1, 0:32].rearrange("p (a b) -> p b a", b=2)[:, 1:2, :],
                AxX, Alu.max))
        else:
            em("vector", lambda e: e.tensor_reduce(
                mx[0:1, 0:1], mrow[0:1, 0:32], AxX, Alu.max))
            vpad()
        vpad()
        for j in range(nmax):
            em("vector", lambda e, j=j, col=col: e.tensor_scalar(
                sc1[0:1, col + 2 * j:col + 2 * j + 1], mx[0:1, j:j + 1],
                1.0 / QMAX, 1e-12, Alu.mult, Alu.max))
        vpad()
        vpad()
        for j in range(nmax):
            em("vector", lambda e, j=j, col=col: e.reciprocal(
                sc1[0:1, col + 2 * j + 1:col + 2 * j + 2],
                sc1[0:1, col + 2 * j:col + 2 * j + 1]))
        wid = 2 * nmax
        vpad()
        vpad()
        em("vector", lambda e, col=col, wid=wid: e.tensor_copy(
            sc1c[0:1, col:col + wid], sc1[0:1, col:col + wid]))
        vpad()
        vpad()
        em("vector", lambda e, col=col, wid=wid: e.tensor_tensor(
            rsc[0:1, 0:wid], sc1[0:1, col:col + wid],
            sc1c[0:1, col:col + wid], Alu.subtract))
        vpad()
        vpad()
        em("vector", lambda e, col=col, wid=wid: e.tensor_copy(
            sc1c[0:1, 8 + col:8 + col + wid], rsc[0:1, 0:wid]))
        vpad()
        vpad()
        em("vector", lambda e, col=col, wid=wid: e.tensor_tensor(
            rsc[0:1, 0:wid], rsc[0:1, 0:wid],
            sc1c[0:1, 8 + col:8 + col + wid], Alu.subtract))
        vpad()
        vpad()
        em("vector", lambda e, col=col, wid=wid: e.tensor_copy(
            sc1c[0:1, 16 + col:16 + col + wid], rsc[0:1, 0:wid]))
        svq3 = INC("vector", "sV")
        if debug and agk == 0:
            W("sync", "sV", svq3)
            DMA("sync", "qS", dbg["dbg_mx"].ap(), mx[0:1, 0:4])
            DMA("sync", "qS", dbg["dbg_mrow"].ap(), mrow[0:1, :])
            DMA("sync", "qS", dbg["dbg_sc1a"].ap()[0:1, 0:2], sc1[0:1, 0:2])
            for c_ in range(3):
                DMA("sync", "qS", dbg["dbg_sc1c"].ap()[0:1, 8 * c_:8 * c_ + 2],
                    sc1c[0:1, 8 * c_:8 * c_ + 2])
        W("tensor", "sV", svq3)
        W("tensor", "sC", psc1_ev[0])
        for c in range(3):
            em("tensor", lambda e, col=col, wid=wid, c=c: e.matmul(
                psc1[:, 0:wid], ones1F[0:1, :],
                sc1c[0:1, 8 * c + col:8 * c + col + wid],
                start=(c == 0), stop=(c == 2)))
        stq = INC("tensor", "sT")
        W("scalar", "sT", stq)
        em("scalar", lambda e, col=col, wid=wid: e.copy(
            sc1[:, col:col + wid], psc1[:, 0:wid]))
        psc1_ev[0] = INC("scalar", "sC")
        return psc1_ev[0]

    qv_s1 = gmax_to_scale(16, lambda i: xln2[:, i * DM:(i + 1) * DM], 0, 0,
                          "sC", sc_ln2)

    x_q = sb("x_q", [128, 16 * DM], BF16)
    for i in range(16):
        em("scalar", lambda e, i=i: e.activation(
            scr2[:, :], xln2[:, i * DM:(i + 1) * DM], Act.Identity,
            bias=magics[:, 0:1], scale=sc1[:, 1:2]))
        em("scalar", lambda e: e.activation(
            scr2[:, :], scr2[:, :], Act.Identity, bias=magics[:, 1:2], scale=1.0))
        em("scalar", lambda e, i=i: e.activation(
            x_q[:, i * DM:(i + 1) * DM], scr2[:, :], Act.Identity, bias=0.0,
            scale=sc1[:, 0:1]))
    sv_xq = INC("scalar", "sC")
    sb_free("xln2", "st2")

    xqT = sb("xqT", [128, 8 * T], BF16)
    ptp2 = [ps(f"ptp2_{j}", [128, 128], BF16) for j in range(2)]
    W("tensor", "sC", sv_xq)
    ntp = 0
    xq_ev = []
    for i in range(16):
        for k in range(8):
            j = ntp % 2
            if ntp >= 2:
                W("tensor", "sC", xq_ev[ntp - 2])
            em("tensor", lambda e, i=i, k=k, j=j: e.transpose(
                ptp2[j][:, :], x_q[:, i * DM + 128 * k:i * DM + 128 * (k + 1)],
                ident[:, :]))
            stq = INC("tensor", "sT")
            W("scalar", "sT", stq)
            em("scalar", lambda e, i=i, k=k, j=j: e.copy(
                xqT[:, k * T + 128 * i:k * T + 128 * (i + 1)], ptp2[j][:, :]))
            xq_ev.append(INC("scalar", "sC"))
            ntp += 1
    sc_xqT = xq_ev[-1]
    if debug:
        W("sync", "sC", sc_xqT)
        DMA("sync", "qS", dbg["dbg_xqT"].ap(), xqT[:, :])
    sb_free("x_q", "ptp2_0", "ptp2_1")

    wq_t = [sb(f"wq_t{j}", [128, 8 * 128], BF16, high=True) for j in range(2)]
    pF = [ps(f"pF{j}", [128, 512]) for j in range(2)]
    fevac = [sb(f"fev{j}", [128, T], F32, high=True) for j in range(2)]

    def ffn_mm(w_scr, dst_scr, mcol, first, prev_sg, prev_dma):
        f_ev = []
        st_f = []
        red_done = []
        f_dma = []
        W("gpsimd", "sT", semc["sT"])
        if first:
            W("gpsimd", "qTP", qS2_dq)
        for o in range(OC):
            bj = o % 2
            if o >= 1:
                W("gpsimd", "sT", st_f[-1])
            wd_ = [DMA("gpsimd", "qG", wq_t[bj][:, :],
                       w_scr.ap()[o, :, :])]
            W("tensor", "qG", wd_[-1])
            if o == 0 and first:
                W("tensor", "sC", sc_xqT)
                W("tensor", "qTP", qS2_dq)
            for si in range(4):
                gs = o * 4 + si
                pj = gs % 2
                if gs >= 2:
                    W("tensor", "sC", f_ev[gs - 2])
                for k in range(8):
                    em("tensor", lambda e, k=k, si=si, pj=pj, bj=bj: e.matmul(
                        pF[pj][:, :], wq_t[bj][:, k * 128:(k + 1) * 128],
                        xqT[:, k * T + 512 * si:k * T + 512 * (si + 1)],
                        start=(k == 0), stop=(k == 7)))
                st_f.append(INC("tensor", "sT"))
                W("scalar", "sT", st_f[-1])
                if si == 0:
                    if o >= 2:
                        W("scalar", "sV", red_done[o - 2])
                    elif not first:
                        W("scalar", "sV", prev_sg - (1 - o))
                    if o >= 1:
                        W("scalar", "qFW", f_dma[o - 1])
                    elif not first:
                        W("scalar", "qFW", prev_dma[-1])
                em("scalar", lambda e, si=si, pj=pj, bj=bj: e.copy(
                    fevac[bj][:, 512 * si:512 * (si + 1)], pF[pj][:, :]))
                f_ev.append(INC("scalar", "sC"))
            W("sync", "sC", f_ev[-1])
            f_dma.append(DMA("sync", "qFW", dst_scr.ap()[o, :, :],
                             fevac[bj][:, :]))
            W("vector", "sC", f_ev[-1])
            em("vector", lambda e, bj=bj: e.tensor_reduce(
                mx[:, 2:3], fevac[bj][:, :], AxX, Alu.max,
                apply_absolute_value=True))
            vpad()
            if o == 0:
                em("vector", lambda e, mcol=mcol: e.tensor_copy(
                    mx[:, mcol:mcol + 1], mx[:, 2:3]))
            else:
                em("vector", lambda e, mcol=mcol: e.tensor_tensor(
                    mx[:, mcol:mcol + 1], mx[:, mcol:mcol + 1], mx[:, 2:3],
                    Alu.max))
            red_done.append(INC("vector", "sV"))
        return red_done[-1], f_dma

    sg_up, up_dma = ffn_mm(sc_wuT, sc_up, 0, True, 0, None)
    sg_gate, _gd = ffn_mm(sc_wgT, sc_gate, 1, False, sg_up, up_dma)
    sb_free("xqT", "wq_t0", "wq_t1", "pF0", "pF1", "fev0", "fev1")

    qv_s23 = gmax_to_scale(0, None, 1, 2, "sV", sg_gate, pair=True)

    f_b = [sb(f"f_b{j}", [128, T], F32, high=True) for j in range(2)]
    f_a = [sb(f"f_a{j}", [128, T], F32, high=True) for j in range(4)]
    f_b += [sb(f"f_b{j}", [128, T], F32, high=True) for j in range(2, 4)]
    f_c = [sb(f"f_c{j}", [128, T], F32, high=True) for j in range(2)]
    QN4 = ("qFA", "qFB", "qFC", "qFD")

    def qdq_ops(buf_ap, scol, icol):
        em("vector", lambda e: e.tensor_scalar(
            buf_ap, buf_ap, sc1[:, icol:icol + 1], MAGIC, Alu.mult, Alu.add))
        em("vector", lambda e: e.tensor_scalar(
            buf_ap, buf_ap, MAGIC, None, Alu.subtract))
        em("vector", lambda e: e.tensor_scalar(
            buf_ap, buf_ap, sc1[:, scol:scol + 1], None, Alu.mult))

    g2_dma = []
    g2_red = []
    sig_sc = []
    sv_qdq1 = []
    W("vector", "sC", qv_s23)
    W("sync", "qFW", semc["qFW"])
    if debug:
        DMA("sync", "qS", dbg["dbg_up"].ap(), sc_up.ap()[0, :, :])
        DMA("sync", "qS", dbg["dbg_gate"].ap(), sc_gate.ap()[0, :, :])
        DMA("sync", "qS", dbg["dbg_wuT"].ap(), sc_wuT.ap()[0, :, :])
        DMA("sync", "qS", dbg["dbg_wdT"].ap(), sc_wdT.ap()[0, :, :])
    for o in range(OC):
        bj = o % 4
        qn = QN4[bj]
        if o >= 4:
            W("sync", "sV", sv_qdq1[o - 4])
        if o >= 2:
            W("sync", "sV", g2_red[o - 2])
        qv = DMA("sync", qn, f_a[bj][:, :], sc_gate.ap()[o, :, :])
        W("scalar", qn, qv)
        if o >= 4:
            W("scalar", "sV", sv_qdq1[o - 4])
        em("scalar", lambda e, bj=bj: e.activation(
            f_a[bj][:, :], f_a[bj][:, :], Act.Identity, bias=magics[:, 0:1],
            scale=sc1[:, 5:6]))
        em("scalar", lambda e, bj=bj: e.activation(
            f_a[bj][:, :], f_a[bj][:, :], Act.Identity, bias=magics[:, 1:2],
            scale=1.0))
        em("scalar", lambda e, bj=bj: e.activation(
            f_a[bj][:, :], f_a[bj][:, :], Act.Identity, bias=0.0,
            scale=sc1[:, 4:5]))
        em("scalar", lambda e, bj=bj: e.activation(f_b[bj][:, :],
                                                   f_a[bj][:, :], Act.Sigmoid))
        sig_sc.append(INC("scalar", "sC"))
        W("vector", "sC", sig_sc[-1])
        if o >= 2:
            W("vector", "qFW" if o % 2 == 0 else "qW2", g2_dma[o - 2])
        em("vector", lambda e, bj=bj: e.tensor_scalar(
            f_b[bj][:, :], f_b[bj][:, :], 65536.0, MAGIC, Alu.mult, Alu.add))
        em("vector", lambda e, bj=bj: e.tensor_scalar(
            f_b[bj][:, :], f_b[bj][:, :], MAGIC, 65535.0, Alu.subtract,
            Alu.min))
        em("vector", lambda e, bj=bj: e.tensor_scalar(
            f_b[bj][:, :], f_b[bj][:, :], 1.0 / 65536.0, None, Alu.mult))
        em("vector", lambda e, bj=bj, c2=o % 2: e.tensor_tensor(
            f_c[c2][:, :], f_a[bj][:, :], f_b[bj][:, :], Alu.mult))
        sv2 = INC("vector", "sV")
        sv_qdq1.append(sv2)
        W("sync", "sV", sv2)
        g2_dma.append(DMA("sync", "qFW" if o % 2 == 0 else "qW2",
                          sc_g2.ap()[o, :, :], f_c[o % 2][:, :]))
        em("vector", lambda e, c2=o % 2: e.tensor_reduce(
            mx[:, 2:3], f_c[c2][:, :], AxX, Alu.max,
            apply_absolute_value=True))
        vpad()
        if o == 0:
            em("vector", lambda e: e.tensor_copy(mx[:, 0:1], mx[:, 2:3]))
        else:
            em("vector", lambda e: e.tensor_tensor(mx[:, 0:1], mx[:, 0:1],
                                                   mx[:, 2:3], Alu.max))
        g2_red.append(INC("vector", "sV"))

    qv_s4 = gmax_to_scale(0, None, 2, 6, "sV", g2_red[-1])

    o_dma = []
    o_red = []
    sv_qdq2 = []
    W("vector", "sC", qv_s4)
    W("sync", "qFW", semc["qFW"])
    if debug:
        DMA("sync", "qS", dbg["dbg_g2"].ap(), sc_g2.ap()[0, :, :])
    for o in range(OC):
        bj = o % 4
        qn = QN4[bj]
        if o >= 4:
            W("sync", "sV", sv_qdq2[o - 4])
        if o >= 2:
            W("sync", "sV", o_red[o - 2])
        qv1 = DMA("sync", qn, f_a[bj][:, :], sc_g2.ap()[o, :, :])
        qv2 = DMA("sync", qn, f_b[bj][:, :], sc_up.ap()[o, :, :])
        W("scalar", qn, qv2)
        if o >= 4:
            W("scalar", "sV", sv_qdq2[o - 4])
        em("scalar", lambda e, bj=bj: e.activation(
            f_a[bj][:, :], f_a[bj][:, :], Act.Identity, bias=magics[:, 0:1],
            scale=sc1[:, 7:8]))
        em("scalar", lambda e, bj=bj: e.activation(
            f_a[bj][:, :], f_a[bj][:, :], Act.Identity, bias=magics[:, 1:2],
            scale=1.0))
        em("scalar", lambda e, bj=bj: e.activation(
            f_a[bj][:, :], f_a[bj][:, :], Act.Identity, bias=0.0,
            scale=sc1[:, 6:7]))
        sq_sc = INC("scalar", "sC")
        W("vector", qn, qv2)
        if o >= 2:
            W("vector", "qFW" if o % 2 == 0 else "qW2", o_dma[o - 2])
        qdq_ops(f_b[bj][:, :], 2, 3)
        W("vector", "sC", sq_sc)
        em("vector", lambda e, bj=bj, c2=o % 2: e.tensor_tensor(
            f_c[c2][:, :], f_a[bj][:, :], f_b[bj][:, :], Alu.mult))
        sv2 = INC("vector", "sV")
        sv_qdq2.append(sv2)
        W("sync", "sV", sv2)
        o_dma.append(DMA("sync", "qFW" if o % 2 == 0 else "qW2",
                         sc_o.ap()[o, :, :], f_c[o % 2][:, :]))
        em("vector", lambda e, c2=o % 2: e.tensor_reduce(
            mx[:, 2:3], f_c[c2][:, :], AxX, Alu.max,
            apply_absolute_value=True))
        vpad()
        if o == 0:
            em("vector", lambda e: e.tensor_copy(mx[:, 0:1], mx[:, 2:3]))
        else:
            em("vector", lambda e: e.tensor_tensor(mx[:, 0:1], mx[:, 0:1],
                                                   mx[:, 2:3], Alu.max))
        o_red.append(INC("vector", "sV"))

    qv_s5 = gmax_to_scale(0, None, 3, 2, "sV", o_red[-1])

    sb_free("x2", "f_c0", "f_c1")
    oq = sb("oq", [128, OC * T], BF16)
    oq_dma_in = []
    oq_sv = []
    W("sync", "sV", sv_qdq2[-1])
    W("sync", "qFW", semc["qFW"])
    W("sync", "qOut", q_x2w)
    if debug:
        DMA("sync", "qS", dbg["dbg_oo"].ap(), sc_o.ap()[0, :, :])
        W("sync", "sC", qv_s5)
        DMA("sync", "qS", dbg["dbg_sc1"].ap(), sc1[:, :])
    W("vector", "sC", qv_s5)
    for o in range(OC):
        bj = o % 4
        qn = QN4[bj]
        if o >= 4:
            W("sync", oq_sv[o - 4][0], oq_sv[o - 4][1])
        qv = DMA("sync", qn, f_a[bj][:, :], sc_o.ap()[o, :, :])
        oq_dma_in.append(qv)
        if o % 2 == 0:
            W("scalar", qn, qv)
            em("scalar", lambda e, bj=bj: e.activation(
                f_a[bj][:, :], f_a[bj][:, :], Act.Identity,
                bias=magics[:, 0:1], scale=sc1[:, 3:4]))
            em("scalar", lambda e, bj=bj: e.activation(
                f_a[bj][:, :], f_a[bj][:, :], Act.Identity,
                bias=magics[:, 1:2], scale=1.0))
            em("scalar", lambda e, o=o, bj=bj: e.activation(
                oq[:, o * T:(o + 1) * T], f_a[bj][:, :], Act.Identity,
                bias=0.0, scale=sc1[:, 2:3]))
            oq_sv.append(("sC", INC("scalar", "sC")))
        else:
            W("vector", qn, qv)
            em("vector", lambda e, bj=bj: e.tensor_scalar(
                f_a[bj][:, :], f_a[bj][:, :], sc1[:, 3:4], MAGIC,
                Alu.mult, Alu.add))
            em("vector", lambda e, bj=bj: e.tensor_scalar(
                f_a[bj][:, :], f_a[bj][:, :], MAGIC, None, Alu.subtract))
            em("vector", lambda e, o=o, bj=bj: e.tensor_scalar(
                oq[:, o * T:(o + 1) * T], f_a[bj][:, :], sc1[:, 2:3], None,
                Alu.mult))
            oq_sv.append(("sV", INC("vector", "sV")))
    oq_last_sc = max(v for s, v in oq_sv if s == "sC")
    oq_last_sv = max(v for s, v in oq_sv if s == "sV")

    sb_free("f_a0", "f_a1", "f_a2", "f_a3", "f_b2", "f_b3")
    wd_sb = sb("wd_sb", [128, OC * DM], BF16)
    W("gpsimd", "sT", semc["sT"])
    W("gpsimd", "sV", semc["sV"])
    W("gpsimd", "sC", semc["sC"])
    W("gpsimd", "qTP", qS2_dq)
    for j2 in range(DM // 128):
        DMA("gpsimd", "qG",
            wd_sb[:, :].rearrange("p (k c) -> p k c", c=DM)[
                :, :, 128 * j2:128 * (j2 + 1)],
            sc_wdT.ap()[j2, :, :].rearrange("p (k c) -> p k c", c=128))
    qg_wd = semc["qG"]
    pD = [ps(f"pD{j}", [128, 512]) for j in range(2)]
    W("tensor", "sC", oq_last_sc)
    W("tensor", "sV", oq_last_sv)
    W("tensor", "qG", qg_wd)
    res_ev = []
    out_dmas = []
    x2_rd = []
    W("sync", "qOut", q_x2w)
    W("sync", "sC", oq_last_sc)
    W("sync", "sV", oq_last_sv)
    for i in range(16):
        bj2 = i % 2
        if i >= 1:
            W("sync", "sV", res_ev[-1])
        if i >= 2:
            W("sync", "sV", res_ev[-3])
        x2_rd.append(DMA("sync", "qFA", f_b[bj2][:, 0:DM], sc_x2.ap()[i, :, :]))
        for hf in range(2):
            if len(res_ev) >= 2:
                W("tensor", "sV", res_ev[-2])
            for k in range(OC):
                em("tensor", lambda e, k=k, i=i, hf=hf: e.matmul(
                    pD[hf][:, :], oq[:, k * T + 128 * i:k * T + 128 * (i + 1)],
                    wd_sb[:, k * DM + 512 * hf:k * DM + 512 * (hf + 1)],
                    start=(k == 0), stop=(k == OC - 1)))
            stq = INC("tensor", "sT")
            W("vector", "sT", stq)
            if i >= 1 and hf == 0:
                W("vector", "qOut", out_dmas[-1])
            if hf == 0:
                W("vector", "qFA", x2_rd[i])
            em("vector", lambda e, i=i, hf=hf, bj=bj2: e.tensor_tensor(
                scr2[:, 512 * hf:512 * (hf + 1)],
                f_b[bj][:, 512 * hf:512 * (hf + 1)],
                pD[hf][:, :], Alu.add))
            res_ev.append(INC("vector", "sV"))
        W("sync", "sV", res_ev[-1])
        out_dmas.append(DMA("sync", "qOut",
                            out_d.ap()[128 * i:128 * (i + 1), :],
                            scr2[:, 0:DM]))

    W("gpsimd", "qOut", semc["qOut"])
    W("gpsimd", "qS", semc["qS"])
    W("gpsimd", "qS2", semc["qS2"])

    bad, stuck = audit_races()
    if stuck:
        raise RuntimeError(f"audit deadlock: {stuck}")
    if bad and debug:
        print(f"audit: {len(bad)} races waived in debug build")
        bad = []
    if bad:
        summ = {}
        for (e, s, v, wv, we) in bad:
            summ.setdefault((e, s), []).append((v, wv, we))
        for k2, vs in summ.items():
            print("RACE", k2, vs[:4], f"(+{len(vs)-4} more)" if len(vs) > 4
                  else "")
        raise RuntimeError(f"{len(bad)} potential semaphore races")

    with nc.Block() as block:
        @block.sync
        def _(eng):
            for f in EMIT["sync"]:
                f(eng)

        @block.vector
        def _(eng):
            for f in EMIT["vector"]:
                f(eng)

        @block.scalar
        def _(eng):
            for f in EMIT["scalar"]:
                f(eng)

        @block.gpsimd
        def _(eng):
            for f in EMIT["gpsimd"]:
                f(eng)

        @block.tensor
        def _(eng):
            for f in EMIT["tensor"]:
                f(eng)

    for name, cnt in semc.items():
        assert cnt < 30000, (name, cnt)
    return nc


def _sel32():
    s = np.zeros((32, 32, 128), BF)
    for r in range(32):
        s[r, r, :] = BF(1.0)
    return s.reshape(32, 32 * 128)


def _perm(p):
    idx = []
    for j in range(G):
        gblk = (j - p) % G
        idx.extend(range(128 * gblk, 128 * (gblk + 1)))
    return np.asarray(idx)


def kernel(L0_attn_x, L0_attn_W_in, L0_attn_conv_w, L0_attn_conv_b,
           L0_attn_W_xproj, L0_attn_W_dt, L0_attn_b_dt, L0_attn_A_log,
           L0_attn_D_param, L0_attn_W_out, L0_ffn_w_gate, L0_ffn_w_up,
           L0_ffn_w_down, _debug=False, _nc=None):
    x = np.ascontiguousarray(np.asarray(L0_attn_x, np.float32))
    W_in = np.asarray(L0_attn_W_in, np.float32)
    conv_w = np.asarray(L0_attn_conv_w, np.float32)[:, 0, :]
    conv_b = np.asarray(L0_attn_conv_b, np.float32).reshape(DI, 1)
    W_xp = np.asarray(L0_attn_W_xproj, np.float32)
    W_dtm = np.asarray(L0_attn_W_dt, np.float32)
    b_dt = np.asarray(L0_attn_b_dt, np.float32).reshape(DI, 1)
    A_log = np.asarray(L0_attn_A_log, np.float32)
    D_par = np.asarray(L0_attn_D_param, np.float32)
    W_outm = np.asarray(L0_attn_W_out, np.float32)
    wg = np.ascontiguousarray(np.asarray(L0_ffn_w_gate, np.float32))
    wu = np.ascontiguousarray(np.asarray(L0_ffn_w_up, np.float32))
    wd = np.ascontiguousarray(np.asarray(L0_ffn_w_down, np.float32))

    base = {
        "wu_nat": wu, "wg_nat": wg, "wd_nat": wd,
        "ident": np.eye(128, dtype=BF),
        "ones1F": np.ones((1, 128), BF),
        "magics": np.repeat(np.array([[MAGIC, -MAGIC]], np.float32), 128,
                            axis=0),
        "sel32": _sel32(),
    }
    perm_sets = []
    for p in range(4):
        pe = _perm(p)
        dd = np.zeros((DI, 128), np.float32)
        for j in range(G):
            np.fill_diagonal(dd[128 * j:128 * (j + 1), :],
                             D_par[pe[128 * j:128 * (j + 1)]])
        perm_sets.append({
            "w_in": np.ascontiguousarray(
                np.concatenate([W_in[:, :DI][:, pe], W_in[:, DI:][:, pe]],
                               axis=1)),
            "w_dt": np.ascontiguousarray(W_dtm[pe][:, pe]),
            "w_xp": np.ascontiguousarray(W_xp[pe]),
            "w_out": np.ascontiguousarray(W_outm[pe]),
            "cw": np.ascontiguousarray(conv_w[pe]),
            "cb": np.ascontiguousarray(conv_b[pe]),
            "bdt": np.ascontiguousarray(b_dt[pe]),
            "alog": np.ascontiguousarray(A_log[pe]),
            "ddiag": dd,
        })

    in_maps = []
    for c in range(NCORE):
        bb, q = c // 4, c % 4
        t0 = q * T
        xin = np.zeros((TH, DM), np.float32)
        if q > 0:
            xin[:3] = x[bb, t0 - 3:t0]
        xin[3:] = x[bb, t0:t0 + T]
        onehot = np.zeros((8, 1), BF)
        if q > 0:
            onehot[c - 1, 0] = BF(1.0)
        im = dict(base)
        im.update(perm_sets[q])
        im["x_in"] = xin
        im["onehot"] = onehot
        in_maps.append(im)

    nc = _nc if _nc is not None else build_nc(debug=_debug)
    res = run_bass_kernel_spmd(nc, in_maps, core_ids=list(range(NCORE)))
    out = np.empty((B, L, DM), np.float32)
    for c in range(NCORE):
        bb, q = c // 4, c % 4
        out[bb, q * T:(q + 1) * T] = np.asarray(res.results[c]["out"])
    if _debug:
        return out, res
    return out


# version 06
# speedup vs baseline: 1.0074x; 1.0074x over previous
import contextlib
import numpy as np
import ml_dtypes
import concourse.bass as bass
import concourse.mybir as mybir
from concourse.bass_utils import run_bass_kernel_spmd

F32 = mybir.dt.float32
BF16 = mybir.dt.bfloat16
Alu = mybir.AluOpType
Act = mybir.ActivationFunctionType
AxX = mybir.AxisListType.X
BF = ml_dtypes.bfloat16

NCORE = 8
B, L, DM = 2, 8192, 1024
DI, S, G = 2048, 16, 16
T = 2048
TH = T + 3
NST = G + 3
OC = 24
FI = 3072
MAGIC = 12582912.0
QMAX = 32767.0
EPS = 1e-5
ENGS = ("sync", "vector", "scalar", "gpsimd", "tensor")
RG = [list(range(NCORE))]


def build_nc(debug=False):
    nc = bass.Bass("TRN2", target_bir_lowering=False,
                   detect_race_conditions=False)

    def din(name, shape, dt=F32):
        return nc.dram_tensor(name, shape, dt, kind="ExternalInput")

    x_in = din("x_in", [TH, DM])
    w_in = din("w_in", [DM, 2 * DI])
    w_dt = din("w_dt", [DI, DI])
    w_xp = din("w_xp", [DI, 2 * S])
    w_out = din("w_out", [DI, DM])
    cw = din("cw", [DI, 4])
    cb = din("cb", [DI, 1])
    bdt = din("bdt", [DI, 1])
    alog = din("alog", [DI, S])
    ddiag = din("ddiag", [DI, 128])
    wu_nat = din("wu_nat", [FI, DM])
    wg_nat = din("wg_nat", [FI, DM])
    wd_nat = din("wd_nat", [DM, FI])
    ident_in = din("ident", [128, 128], BF16)
    onesF_in = din("ones1F", [1, 128], BF16)
    sel32_in = din("sel32", [32, 32 * 128], BF16)
    onehot_in = din("onehot", [8, 1], BF16)
    magic_in = din("magics", [128, 2], F32)

    out_d = nc.dram_tensor("out", [T, DM], F32, kind="ExternalOutput")
    dbg = {}
    if debug:
        for nm in ("dbg_xs", "dbg_dt", "dbg_xt", "dbg_gtd", "dbg_sz"):
            dbg[nm] = nc.dram_tensor(nm, [G, 128, T], BF16, kind="ExternalOutput")
        dbg["dbg_x2"] = nc.dram_tensor("dbg_x2", [T, DM], F32, kind="ExternalOutput")
        dbg["dbg_da0"] = nc.dram_tensor("dbg_da0", [128, T], F32, kind="ExternalOutput")
        dbg["dbg_db0"] = nc.dram_tensor("dbg_db0", [128, T], BF16, kind="ExternalOutput")
        dbg["dbg_h0"] = nc.dram_tensor("dbg_h0", [128, T], BF16, kind="ExternalOutput")
        dbg["dbg_tmp0"] = nc.dram_tensor("dbg_tmp0", [128, T], BF16, kind="ExternalOutput")
        dbg["dbg_init0"] = nc.dram_tensor("dbg_init0", [128, S], F32, kind="ExternalOutput")
        dbg["dbg_bb0"] = nc.dram_tensor("dbg_bb0", [128, T], BF16, kind="ExternalOutput")
        dbg["dbg_cb0"] = nc.dram_tensor("dbg_cb0", [128, T], BF16, kind="ExternalOutput")
        dbg["dbg_na"] = nc.dram_tensor("dbg_na", [128, G * S], F32, kind="ExternalOutput")
        dbg["dbg_y0"] = nc.dram_tensor("dbg_y0", [128, T], F32, kind="ExternalOutput")
        dbg["dbg_bcb"] = nc.dram_tensor("dbg_bcb", [32, T], BF16, kind="ExternalOutput")
        dbg["dbg_bbc_f"] = nc.dram_tensor("dbg_bbc_f", [128, S * T], BF16, kind="ExternalOutput")
        dbg["dbg_cbc_f"] = nc.dram_tensor("dbg_cbc_f", [128, S * T], BF16, kind="ExternalOutput")
        dbg["dbg_wuT"] = nc.dram_tensor("dbg_wuT", [128, DM], BF16, kind="ExternalOutput")
        dbg["dbg_wdT"] = nc.dram_tensor("dbg_wdT", [128, FI], BF16, kind="ExternalOutput")
        dbg["dbg_xqT"] = nc.dram_tensor("dbg_xqT", [128, 8 * T], BF16, kind="ExternalOutput")
        dbg["dbg_up"] = nc.dram_tensor("dbg_up", [128, T], F32, kind="ExternalOutput")
        dbg["dbg_gate"] = nc.dram_tensor("dbg_gate", [128, T], F32, kind="ExternalOutput")
        dbg["dbg_g2"] = nc.dram_tensor("dbg_g2", [128, T], F32, kind="ExternalOutput")
        dbg["dbg_oo"] = nc.dram_tensor("dbg_oo", [128, T], F32, kind="ExternalOutput")
        dbg["dbg_sc1"] = nc.dram_tensor("dbg_sc1", [128, 8], F32, kind="ExternalOutput")
        dbg["dbg_mx"] = nc.dram_tensor("dbg_mx", [1, 4], F32, kind="ExternalOutput")
        dbg["dbg_mrow"] = nc.dram_tensor("dbg_mrow", [1, 128], F32, kind="ExternalOutput")
        dbg["dbg_sc1a"] = nc.dram_tensor("dbg_sc1a", [1, 8], F32, kind="ExternalOutput")
        dbg["dbg_sc1c"] = nc.dram_tensor("dbg_sc1c", [1, 24], BF16, kind="ExternalOutput")

    sc_dt = nc.dram_tensor("sc_dt", [G, 128, T], BF16)
    sc_xt = nc.dram_tensor("sc_xt", [G, 128, T], BF16)
    sc_xs = nc.dram_tensor("sc_xs", [G, 128, T], BF16)
    sc_sz = nc.dram_tensor("sc_sz", [G, 128, T], BF16)
    sc_gated = nc.dram_tensor("sc_gated", [G, 128, T], BF16)
    sc_wuT = nc.dram_tensor("sc_wuT", [FI // 128, 128, DM], BF16)
    sc_wgT = nc.dram_tensor("sc_wgT", [FI // 128, 128, DM], BF16)
    sc_wdT = nc.dram_tensor("sc_wdT", [DM // 128, 128, FI], BF16)
    sc_up = nc.dram_tensor("sc_up", [OC, 128, T], F32)
    sc_gate = nc.dram_tensor("sc_gate", [OC, 128, T], F32)
    sc_g2 = nc.dram_tensor("sc_g2", [OC, 128, T], F32)
    sc_o = nc.dram_tensor("sc_o", [OC, 128, T], F32)
    sc_x2 = nc.dram_tensor("sc_x2", [16, 128, DM], F32)
    ag_i = [nc.dram_tensor(f"ag_i{r}", [1, 2048], BF16) for r in range(NST)]
    ag_o = [nc.dram_tensor(f"ag_o{r}", [8, 2048], BF16) for r in range(NST)]
    agm_i = [nc.dram_tensor(f"agm_i{k}", [1, 4], F32) for k in range(4)]
    agm_o = [nc.dram_tensor(f"agm_o{k}", [8, 4], F32) for k in range(4)]

    ARENA_N = 52992
    arena_cm = nc.sbuf_tensor("arena", [128, ARENA_N], F32)
    arena = arena_cm.__enter__()
    parena_cm = nc.psum_tensor("parena", [128, 4096], F32)
    parena = parena_cm.__enter__()
    segs = {"s": [[0, ARENA_N, None]], "p": [[0, 4096, None]]}
    seg_names = {}

    def _alloc(pool, name, n_f32, align, high=False):
        n_f32 = -(-n_f32 // align) * align
        it = reversed(segs[pool]) if high else segs[pool]
        for sg in list(it):
            if sg[2] is None and sg[1] >= n_f32:
                if high:
                    off = sg[0] + sg[1] - n_f32
                    if sg[1] > n_f32:
                        segs[pool].insert(segs[pool].index(sg) + 1,
                                          [off, n_f32, name])
                        sg[1] -= n_f32
                    else:
                        sg[2] = name
                    seg_names[name] = (pool, off, n_f32)
                    return off
                off = sg[0]
                if sg[1] > n_f32:
                    segs[pool].insert(segs[pool].index(sg) + 1,
                                      [off + n_f32, sg[1] - n_f32, None])
                sg[1] = n_f32
                sg[2] = name
                seg_names[name] = (pool, off, n_f32)
                return off
        raise MemoryError(f"{pool} arena full for {name} ({n_f32} f32/part); "
                          f"segs={[(s[0], s[1], s[2]) for s in segs[pool]]}")

    def _view(base, off, shape, dt):
        p = shape[0]
        n = int(np.prod(shape[1:]))
        if dt == F32:
            ap = base[0:p, off:off + n]
        elif dt in (BF16,):
            ap = base[0:p, off:off + (n + 1) // 2].bitcast(BF16)[0:p, 0:n]
        else:
            raise ValueError(dt)
        if len(shape) > 2:
            pat = "q (" + " ".join(f"d{i}" for i in range(1, len(shape))) + ") -> q " +                   " ".join(f"d{i}" for i in range(1, len(shape)))
            ap = ap.rearrange(pat, **{f"d{i}": shape[i] for i in range(1, len(shape))})
        return ap

    def sb(name, shape, dt, high=False):
        n = int(np.prod(shape[1:]))
        nf = n if dt == F32 else (n + 1) // 2
        off = _alloc("s", name, nf, 16, high)
        return _view(arena, off, shape, dt)

    def ps(name, shape, dt=F32):
        n = int(np.prod(shape[1:]))
        nf = n if dt == F32 else (n + 1) // 2
        off = _alloc("p", name, nf, 512)
        return _view(parena, off, shape, dt)

    def sb_free(*names):
        for nm in names:
            pool, off, nf = seg_names.pop(nm)
            for sg in segs[pool]:
                if sg[0] == off:
                    sg[2] = None
                    break
            merged = []
            for sg in segs[pool]:
                if merged and merged[-1][2] is None and sg[2] is None:
                    merged[-1][1] += sg[1]
                else:
                    merged.append(sg)
            segs[pool] = merged

    sems, semc = {}, {}

    _semcms = []

    def mksem(name):
        cm = nc.semaphore(name)
        h = cm.__enter__()
        _semcms.append(cm)
        sems[name] = h
        semc[name] = 0

    for s_ in ("qS", "qS2", "qTP", "qFB", "qFC", "qFD", "qW2", "qG", "qG2",
               "sV", "sC", "sT", "sG", "sCC",
               "qX", "qD", "qT2", "qX2", "qZ2", "qI", "qDtW", "qXtW",
               "qSzW", "qGtW", "qP7", "qFW", "qFA", "qOut"):
        mksem(s_)

    EMIT = {e: [] for e in ENGS}
    _iswait = {e: [] for e in ENGS}
    EV = {e: [] for e in ENGS}

    def em(eng, fn, wait=False):
        EMIT[eng].append(fn)
        _iswait[eng].append(wait)

    def W(eng, semname, val):
        if val and val > 0:
            em(eng, lambda e, s=sems[semname], v=val: e.wait_ge(s, v),
               wait=True)
            EV[eng].append(("wait", semname, val))

    def INC(eng, semname, n=1):
        assert EMIT[eng] and not _iswait[eng][-1], (eng, semname)
        semc[semname] += n
        prev = EMIT[eng][-1]
        EMIT[eng][-1] = (lambda e, f=prev, s=sems[semname], k=n:
                         f(e).then_inc(s, k))
        EV[eng].append(("inc", semname, semc[semname]))
        return semc[semname]

    def DMA(eng, q, out_ap, in_ap):
        semc[q] += 16
        em(eng, lambda e, o=out_ap, i=in_ap, s=sems[q]:
           e.dma_start(out=o, in_=i).then_inc(s, 16))
        EV[eng].append(("dma", q, semc[q]))
        return semc[q]

    def CCEV(val):
        EV["gpsimd"].append(("inc", "sCC", val))

    def audit_races():
        waited = {}
        for e in ENGS:
            for (k, s, v) in EV[e]:
                if k == "wait":
                    waited.setdefault(s, set()).add((e, v))
        pos = {e: 0 for e in ENGS}
        clock = {e: {} for e in ENGS}
        snap = {}
        semval = {s: 0 for s in semc}
        bad = []
        progress = True
        while progress:
            progress = False
            for e in ENGS:
                while pos[e] < len(EV[e]):
                    k, s, v = EV[e][pos[e]]
                    if k == "wait":
                        if semval.get(s, 0) < v:
                            break
                        c = clock[e]
                        for ss, vv in snap.get((s, v), {}).items():
                            if c.get(ss, 0) < vv:
                                c[ss] = vv
                        if c.get(s, 0) < v:
                            c[s] = v
                    else:
                        if k == "dma":
                            kn = clock[e].get(s, 0)
                            for (we, wv) in waited.get(s, ()):
                                if we != e and kn < wv < v:
                                    bad.append((e, s, v, wv, we))
                        semval[s] = max(semval.get(s, 0), v)
                        sn = dict(clock[e])
                        sn[s] = v
                        prev = snap.get((s, v))
                        if prev is None:
                            snap[(s, v)] = sn
                        else:
                            for ss, vv in sn.items():
                                if prev.get(ss, 0) > vv:
                                    sn[ss] = prev[ss]
                            snap[(s, v)] = sn
                    pos[e] += 1
                    progress = True
        stuck = {e: (pos[e], EV[e][pos[e]] if pos[e] < len(EV[e]) else None)
                 for e in ENGS if pos[e] < len(EV[e])}
        return bad, stuck

    def CC(k_in, k_out):
        semc["sCC"] += 1
        em("gpsimd", lambda e, a=k_in, b=k_out: e.collective_compute(
            "AllGather", Alu.bypass, replica_groups=RG,
            ins=[a.ap().opt()], outs=[b.ap().opt()]).then_inc(sems["sCC"], 1))
        EV["gpsimd"].append(("inc", "sCC", semc["sCC"]))
        return semc["sCC"]

    def dbl_rows(q, dst2d, width_slice, upto=128):
        k, qv = 1, semc[q]
        while k < upto:
            kk = min(k, upto - k)
            qv = DMA("sync", q, dst2d[k:k + kk, width_slice],
                     dst2d[0:kk, width_slice])
            k += kk
        return qv

    ident = sb("identb_s", [128, 128], BF16)
    vdead = sb("vdead", [1, 8], F32, high=True)

    def vpad():
        em("vector", lambda e: e.tensor_copy(vdead[0:1, 0:4],
                                             ident[0:1, 0:4]))
    ones1F = sb("ones1F_s", [1, 128], BF16)
    sel32 = sb("sel32_s", [32, 32 * 128], BF16)
    onehot = sb("onehot_s", [8, 1], BF16)
    magics = sb("magics_s", [128, 2], F32)
    negA = sb("negA", [128, G * S], F32)
    cw_sb = sb("cw_sb", [128, G * 4], F32)
    cb_sb = sb("cb_sb", [128, G], F32)
    bdt_sb = sb("bdt_sb", [128, G], F32)
    DMA("sync", "qS", ident[:, :], ident_in.ap())
    DMA("sync", "qS", ones1F[:, :], onesF_in.ap())
    DMA("sync", "qS", sel32[:, :], sel32_in.ap())
    DMA("sync", "qS", onehot[:, :], onehot_in.ap())
    DMA("sync", "qS", magics[:, :], magic_in.ap())
    for g in range(G):
        DMA("sync", "qS", negA[:, g * S:(g + 1) * S],
            alog.ap()[128 * g:128 * (g + 1), :])
        DMA("sync", "qS", cw_sb[:, g * 4:(g + 1) * 4],
            cw.ap()[128 * g:128 * (g + 1), :])
        DMA("sync", "qS", cb_sb[:, g:g + 1], cb.ap()[128 * g:128 * (g + 1), :])
        DMA("sync", "qS", bdt_sb[:, g:g + 1], bdt.ap()[128 * g:128 * (g + 1), :])
    qs_p0 = semc["qS"]

    x_sb = sb("x_sb", [128, 17 * DM], F32)
    xrows = [128] * 16 + [3]
    em("gpsimd", lambda e: e.memset(x_sb[0:128, 16 * DM:17 * DM], 0.0))
    sg_xz = INC("gpsimd", "sG")
    qs_x = [DMA("sync", "qX", x_sb[0:xrows[i], i * DM:(i + 1) * DM],
                x_in.ap()[128 * i:128 * i + xrows[i], :]) for i in range(16)]
    W("sync", "sG", sg_xz)
    qs_x.append(DMA("sync", "qX", x_sb[0:3, 16 * DM:17 * DM],
                    x_in.ap()[2048:2051, :]))

    W("scalar", "qS", qs_p0)
    em("scalar", lambda e: e.activation(negA[:, :], negA[:, :], Act.Exp))
    sc_nA = INC("scalar", "sC")
    W("vector", "sC", sc_nA)
    em("vector", lambda e: e.tensor_scalar(negA[:, :], negA[:, :], -1.0, None,
                                           Alu.mult))
    sv_negA = INC("vector", "sV")

    xln = sb("xln", [128, 17 * DM], BF16)
    st = sb("st", [128, 17 * 6], F32)
    scr_sq = sb("scr_sq", [128, DM], F32)
    NT1 = 17
    su, sq_, rc, rs, mn, b2 = (st[:, k * NT1:(k + 1) * NT1] for k in range(6))
    W("vector", "qX", qs_x[-1])
    W("scalar", "qX", qs_x[-1])
    W("vector", "sG", sg_xz)
    W("scalar", "sG", sg_xz)
    for i in range(17):
        em("vector", lambda e, i=i: e.tensor_reduce(
            su[:, i:i + 1], x_sb[:, i * DM:(i + 1) * DM], AxX, Alu.add))
        em("scalar", lambda e, i=i: e.activation(
            scr_sq[:, :], x_sb[:, i * DM:(i + 1) * DM], Act.Square,
            accum_out=sq_[:, i:i + 1]))
    sc_sq = INC("scalar", "sC")
    vpad()
    vpad()
    em("vector", lambda e: e.tensor_scalar(mn[:, :], su[:, :], 1.0 / DM, None,
                                           Alu.mult))
    W("vector", "sC", sc_sq)
    em("vector", lambda e: e.tensor_scalar(rc[:, :], sq_[:, :], 1.0 / DM, EPS,
                                           Alu.mult, Alu.add))
    em("vector", lambda e: e.scalar_tensor_tensor(rs[:, :], mn[:, :], 1.0,
                                                  mn[:, :], Alu.mult, Alu.mult))
    vpad()
    vpad()
    em("vector", lambda e: e.tensor_tensor(rc[:, :], rc[:, :], rs[:, :],
                                           Alu.subtract))
    vpad()
    vpad()
    em("vector", lambda e: e.reciprocal(rc[:, :], rc[:, :]))
    sv1 = INC("vector", "sV")
    W("scalar", "sV", sv1)
    em("scalar", lambda e: e.activation(rs[:, :], rc[:, :], Act.Sqrt))
    sc_rs = INC("scalar", "sC")
    W("vector", "sC", sc_rs)
    em("vector", lambda e: e.scalar_tensor_tensor(b2[:, :], mn[:, :], -1.0,
                                                  rs[:, :], Alu.mult, Alu.mult))
    sv_b2 = INC("vector", "sV")
    W("scalar", "sV", sv_b2)
    for i in range(17):
        em("scalar", lambda e, i=i: e.activation(
            xln[0:xrows[i], i * DM:(i + 1) * DM],
            x_sb[0:xrows[i], i * DM:(i + 1) * DM], Act.Identity,
            bias=b2[0:xrows[i], i:i + 1], scale=rs[0:xrows[i], i:i + 1]))
    sc_ln = INC("scalar", "sC")

    xlnT = sb("xlnT", [128, 8 * TH], BF16)
    ptp = [ps(f"ptp{j}", [128, 128], BF16) for j in range(2)]
    W("tensor", "qS", qs_p0)
    W("tensor", "sC", sc_ln)
    ntp = 0
    for i in range(17):
        for k in range(8):
            j = ntp % 2
            if ntp >= 2:
                W("tensor", "sC", sc_ln + ntp - 1)
            em("tensor", lambda e, i=i, k=k, j=j: e.transpose(
                ptp[j][0:128, 0:xrows[i]],
                xln[0:xrows[i], i * DM + 128 * k:i * DM + 128 * (k + 1)],
                ident[0:xrows[i], 0:xrows[i]]))
            stq = INC("tensor", "sT")
            W("scalar", "sT", stq)
            em("scalar", lambda e, i=i, k=k, j=j: e.copy(
                xlnT[:, k * TH + 128 * i:k * TH + 128 * i + xrows[i]],
                ptp[j][0:128, 0:xrows[i]]))
            INC("scalar", "sC")
            ntp += 1
    sc_xlnT = semc["sC"]
    sb_free("x_sb", "xln", "st", "ptp0", "ptp1", "scr_sq")

    xsT = sb("xsT", [128, G * T], BF16)
    buf_w = [sb(f"buf_w{j}", [128, FI], F32) for j in range(2)]
    wqb = [sb(f"wqb{j}", [128, FI], BF16) for j in range(2)]
    tpb = [sb(f"tpb{j}", [128, FI], BF16) for j in range(2)]
    sq_s = sb("sq_s", [128, 2 * 96], F32)
    sq_i = sb("sq_i", [128, 2 * 96], F32)
    pQs = ps("pQs", [128, 128], F32)

    def emit_dequant():
        vdone = [0, 0]
        tdone = [0, 0]
        tpdma = [0, 0]
        evQ = []
        for (w_n, w_q, O, I) in [
                (wu_nat, sc_wuT, FI, DM),
                (wg_nat, sc_wgT, FI, DM),
                (wd_nat, sc_wdT, DM, FI)]:
            ot, kt, NB = O // 128, I // 128, I // 32
            for j in range(ot):
                bj = j % 2
                W("sync", "sV", vdone[1 - bj])
                qv = DMA("sync", "qS2", buf_w[bj][:, 0:I],
                         w_n.ap()[128 * j:128 * (j + 1), :])
                W("vector", "qS2", qv)
                W("vector", "sT", tdone[bj])
                em("vector", lambda e, bj=bj, I=I, NB=NB: e.tensor_reduce(
                    sq_s[:, bj * 96:bj * 96 + NB].rearrange(
                        "p (b o) -> p b o", o=1),
                    buf_w[bj][:, 0:I].rearrange("p (b q) -> p b q", q=32),
                    AxX, Alu.max, apply_absolute_value=True))
                vpad()
                vpad()
                em("vector", lambda e, bj=bj, NB=NB: e.tensor_scalar(
                    sq_s[:, bj * 96:bj * 96 + NB],
                    sq_s[:, bj * 96:bj * 96 + NB], 1.0 / 7.0, 1e-12,
                    Alu.mult, Alu.max))
                vpad()
                vpad()
                em("vector", lambda e, bj=bj, NB=NB: e.reciprocal(
                    sq_i[:, bj * 96:bj * 96 + NB],
                    sq_s[:, bj * 96:bj * 96 + NB]))
                vpad()
                vpad()
                em("vector", lambda e, bj=bj, I=I, NB=NB: e.tensor_tensor(
                    buf_w[bj][:, 0:I].rearrange("p (b q) -> p b q", q=32),
                    buf_w[bj][:, 0:I].rearrange("p (b q) -> p b q", q=32),
                    sq_i[:, bj * 96:bj * 96 + NB, None].broadcast_to(
                        [128, NB, 32]), Alu.mult))
                em("vector", lambda e, bj=bj, I=I: e.tensor_scalar(
                    buf_w[bj][:, 0:I], buf_w[bj][:, 0:I], MAGIC, MAGIC,
                    Alu.add, Alu.subtract))
                em("vector", lambda e, bj=bj, I=I, NB=NB: e.tensor_tensor(
                    wqb[bj][:, 0:I].rearrange("p (b q) -> p b q", q=32),
                    buf_w[bj][:, 0:I].rearrange("p (b q) -> p b q", q=32),
                    sq_s[:, bj * 96:bj * 96 + NB, None].broadcast_to(
                        [128, NB, 32]), Alu.mult))
                vdone[bj] = INC("vector", "sV")
                W("tensor", "sV", vdone[bj])
                for k in range(kt):
                    if evQ:
                        W("tensor", "sC", evQ[-1])
                    em("tensor", lambda e, bj=bj, k=k: e.matmul(
                        pQs[:, :], wqb[bj][:, 128 * k:128 * (k + 1)],
                        ident[:, :], start=True, stop=True))
                    stq = INC("tensor", "sT")
                    W("scalar", "sT", stq)
                    if k == 0:
                        W("scalar", "qTP", tpdma[1 - bj])
                    em("scalar", lambda e, bj=bj, k=k: e.copy(
                        tpb[bj][:, 128 * k:128 * (k + 1)], pQs[:, :]))
                    evQ.append(INC("scalar", "sC"))
                tdone[bj] = semc["sT"]
                W("sync", "sC", evQ[-1])
                tpdma[bj] = DMA("sync", "qTP", w_q.ap()[j, :, :],
                                tpb[bj][:, 0:I])

    W("sync", "sC", semc["sC"])
    W("sync", "sT", semc["sT"])
    W("sync", "sV", semc["sV"])
    emit_dequant()
    qS2_dq = semc["qTP"]

    win_b = [sb(f"win{j}", [128, 8 * 128], BF16) for j in range(2)]
    xc_b = [sb(f"xc{j}", [128, TH + 1], BF16) for j in range(2)]
    cv = [sb(f"cv{j}", [128, T], BF16) for j in range(4)]
    pxc = [ps(f"pxc{j}", [128, 512]) for j in range(2)]
    slabs = [(0, 3), (3, 512), (515, 512), (1027, 512), (1539, 512)]
    xs_dma = []
    xc_ev = []
    cv_done = []
    st_xc = []

    for g in range(G):
        bj = g % 2
        if g >= 1:
            W("gpsimd", "sT", st_xc[-1])
        w_dmas = [DMA("gpsimd", "qG", win_b[bj][:, k * 128:(k + 1) * 128],
                      w_in.ap()[128 * k:128 * (k + 1), 128 * g:128 * (g + 1)])
                  for k in range(8)]
        W("tensor", "qG", w_dmas[-1])
        if g == 0:
            W("tensor", "sC", sc_xlnT)
        if g >= 2:
            W("tensor", "sV", cv_done[g - 2])
        for si, (t0, ln_) in enumerate(slabs):
            gs = g * 5 + si
            pj = gs % 2
            if gs >= 2:
                W("tensor", "sC", xc_ev[gs - 2])
            for k in range(8):
                em("tensor", lambda e, k=k, t0=t0, ln_=ln_, pj=pj, bj=bj:
                   e.matmul(pxc[pj][:, 0:ln_],
                            win_b[bj][:, k * 128:(k + 1) * 128],
                            xlnT[:, k * TH + t0:k * TH + t0 + ln_],
                            start=(k == 0), stop=(k == 7)))
            st_xc.append(INC("tensor", "sT"))
            W("scalar", "sT", st_xc[-1])
            if g >= 2 and si == 0:
                W("scalar", "sV", cv_done[g - 2])
            em("scalar", lambda e, t0=t0, ln_=ln_, pj=pj, bj=bj: e.copy(
                xc_b[bj][:, t0:t0 + ln_], pxc[pj][:, 0:ln_]))
            xc_ev.append(INC("scalar", "sC"))
        W("vector", "sC", xc_ev[-1])
        for kk in range(4):
            em("vector", lambda e, g=g, bj=bj, kk=kk: e.tensor_scalar(
                cv[kk][:, :], xc_b[bj][:, kk:kk + T],
                cw_sb[:, 4 * g + kk:4 * g + kk + 1], None, Alu.mult))
        em("vector", lambda e: e.tensor_tensor(cv[0][:, :], cv[0][:, :],
                                               cv[1][:, :], Alu.add))
        em("vector", lambda e: e.tensor_tensor(cv[2][:, :], cv[2][:, :],
                                               cv[3][:, :], Alu.add))
        em("vector", lambda e: e.tensor_tensor(cv[0][:, :], cv[0][:, :],
                                               cv[2][:, :], Alu.add))
        cv_done.append(INC("vector", "sV"))
        W("scalar", "sV", cv_done[-1])
        em("scalar", lambda e, g=g: e.activation(
            xsT[:, g * T:(g + 1) * T], cv[0][:, :], Act.Silu,
            bias=cb_sb[:, g:g + 1]))
        sc_sil = INC("scalar", "sC")
        W("sync", "sC", sc_sil)
        xs_dma.append(DMA("sync", "qS", sc_xs.ap()[g, :, :],
                          xsT[:, g * T:(g + 1) * T]))
        W("vector", "sC", sc_sil)
    sc_xsT_all = semc["sC"]
    sb_free("win0", "win1", "xc0", "xc1", "cv0", "cv1", "cv2", "cv3",
            "pxc0", "pxc1")

    wdt_b = [sb(f"wdt{j}", [128, 16 * 128], BF16) for j in range(2)]
    dtb = sb("dtb", [128, T], BF16)
    xtb = sb("xtb", [128, T], BF16)
    pdt = [ps(f"pdt{j}", [128, 512]) for j in range(2)]
    dt_scr = sb("dt_scr", [128, 512], F32)
    BCb = sb("BCb", [32, T], BF16)
    pbc = ps("pbc", [32, 512])
    dt_ev = []
    st_dt = []
    W("gpsimd", "sT", semc["sT"])
    for g in range(G):
        bj = g % 2
        if g >= 1:
            W("gpsimd", "sT", st_dt[-1])
        w_dmas = [DMA("gpsimd", "qG", wdt_b[bj][:, k * 128:(k + 1) * 128],
                      w_dt.ap()[128 * k:128 * (k + 1), 128 * g:128 * (g + 1)])
                  for k in range(G)]
        W("tensor", "qG", w_dmas[-1])
        if g == 0:
            W("tensor", "sC", sc_xsT_all)
        for si in range(4):
            gs = g * 4 + si
            pj = gs % 2
            if gs >= 2:
                W("tensor", "sC", dt_ev[gs - 2])
            for k in range(G):
                em("tensor", lambda e, k=k, si=si, pj=pj, bj=bj: e.matmul(
                    pdt[pj][:, :], wdt_b[bj][:, k * 128:(k + 1) * 128],
                    xsT[:, k * T + 512 * si:k * T + 512 * (si + 1)],
                    start=(k == 0), stop=(k == G - 1)))
            st_dt.append(INC("tensor", "sT"))
            W("scalar", "sT", st_dt[-1])
            em("scalar", lambda e, g=g, pj=pj: e.activation(
                dt_scr[:, :], pdt[pj][:, :], Act.Exp, bias=bdt_sb[:, g:g + 1]))
            em("scalar", lambda e, si=si: e.activation(
                dtb[:, 512 * si:512 * (si + 1)], dt_scr[:, :], Act.Ln,
                bias=1.0))
            dt_ev.append(INC("scalar", "sC"))
        W("vector", "sC", dt_ev[-1])
        em("vector", lambda e, g=g: e.tensor_tensor(
            xtb[:, :], xsT[:, g * T:(g + 1) * T], dtb[:, :], Alu.mult))
        sv_xt = INC("vector", "sV")
        W("sync", "sC", dt_ev[-1])
        q_dt = DMA("sync", "qDtW", sc_dt.ap()[g, :, :], dtb[:, :])
        W("sync", "sV", sv_xt)
        q_xt = DMA("sync", "qXtW", sc_xt.ap()[g, :, :], xtb[:, :])
        W("scalar", "qDtW", q_dt)
        W("scalar", "sV", sv_xt)
        W("vector", "qXtW", q_xt)

    wxp_sb = sb("wxp_sb", [128, G * 32], BF16)
    W("gpsimd", "sT", semc["sT"])
    for k in range(G):
        DMA("gpsimd", "qG", wxp_sb[:, k * 32:(k + 1) * 32],
            w_xp.ap()[128 * k:128 * (k + 1), :])
    qg_xp = semc["qG"]
    W("tensor", "qG", qg_xp)
    bc_ev = []
    for si in range(4):
        if si >= 1:
            W("tensor", "sC", bc_ev[-1])
        for k in range(G):
            em("tensor", lambda e, k=k, si=si: e.matmul(
                pbc[:, :], wxp_sb[:, k * 32:(k + 1) * 32],
                xsT[:, k * T + 512 * si:k * T + 512 * (si + 1)],
                start=(k == 0), stop=(k == G - 1)))
        stq = INC("tensor", "sT")
        W("scalar", "sT", stq)
        em("scalar", lambda e, si=si: e.copy(BCb[:, 512 * si:512 * (si + 1)],
                                             pbc[:, :]))
        bc_ev.append(INC("scalar", "sC"))
    sc_bc = bc_ev[-1]

    winz_b = [sb(f"winz{j}", [128, 8 * 128], BF16) for j in range(2)]
    szb = sb("szb", [128, T], BF16)
    psz = [ps(f"psz{j}", [128, 512]) for j in range(2)]
    sz_ev = []
    st_z = []
    W("gpsimd", "sT", semc["sT"])
    for g in range(G):
        bj = g % 2
        if g >= 1:
            W("gpsimd", "sT", st_z[-1])
        w_dmas = [DMA("gpsimd", "qG", winz_b[bj][:, k * 128:(k + 1) * 128],
                      w_in.ap()[128 * k:128 * (k + 1),
                                DI + 128 * g:DI + 128 * (g + 1)])
                  for k in range(8)]
        W("tensor", "qG", w_dmas[-1])
        for si in range(4):
            gs = g * 4 + si
            pj = gs % 2
            if gs >= 2:
                W("tensor", "sC", sz_ev[gs - 2])
            for k in range(8):
                em("tensor", lambda e, k=k, si=si, pj=pj, bj=bj: e.matmul(
                    psz[pj][:, :], winz_b[bj][:, k * 128:(k + 1) * 128],
                    xlnT[:, k * TH + 3 + 512 * si:k * TH + 3 + 512 * (si + 1)],
                    start=(k == 0), stop=(k == 7)))
            st_z.append(INC("tensor", "sT"))
            W("scalar", "sT", st_z[-1])
            em("scalar", lambda e, si=si, pj=pj: e.activation(
                szb[:, 512 * si:512 * (si + 1)], psz[pj][:, :], Act.Silu))
            sz_ev.append(INC("scalar", "sC"))
        W("sync", "sC", sz_ev[-1])
        q_sz = DMA("sync", "qSzW", sc_sz.ap()[g, :, :], szb[:, :])
        W("scalar", "qSzW", q_sz)

    sb_free("xlnT", "winz0", "winz1", "psz0", "psz1", "szb", "wdt0", "wdt1",
            "dtb", "xtb", "pdt0", "pdt1", "dt_scr", "wxp_sb", "pbc",
            "buf_w0", "buf_w1", "wqb0", "wqb1", "tpb0", "tpb1", "sq_s",
            "sq_i", "pQs", "xsT", "cw_sb", "cb_sb", "bdt_sb")

    Bbc = sb("Bbc", [128, S * T], BF16)
    Cbc = sb("Cbc", [128, S * T], BF16)
    pbc2 = ps("pbc2", [128, 512])
    W("sync", "sC", semc["sC"])
    W("sync", "sT", semc["sT"])
    W("sync", "sV", semc["sV"])
    W("sync", "sG", semc["sG"])
    W("sync", "qS", xs_dma[-1])
    W("sync", "qS2", semc["qS2"])
    W("sync", "qDtW", semc["qDtW"])
    W("sync", "qXtW", semc["qXtW"])
    W("sync", "qSzW", semc["qSzW"])
    W("tensor", "sC", sc_bc)
    W("tensor", "qS", qs_p0)
    bc_ev2 = []
    for idx in range(2 * S):
        s_ = idx % S
        dst = Bbc if idx < S else Cbc
        srow = s_ if idx < S else S + s_
        for sl in range(4):
            if bc_ev2:
                if len(bc_ev2) >= 1:
                    W("tensor", "sC", bc_ev2[-1])
            em("tensor", lambda e, srow=srow, sl=sl: e.matmul(
                pbc2[:, :], sel32[:, 128 * srow:128 * (srow + 1)]
                if srow < 32 else None,
                BCb[0:32, 512 * sl:512 * (sl + 1)],
                start=True, stop=True))
            stq = INC("tensor", "sT")
            W("scalar", "sT", stq)
            em("scalar", lambda e, dst=dst, s_=s_, sl=sl: e.copy(
                dst[:, s_ * T + 512 * sl:s_ * T + 512 * (sl + 1)],
                pbc2[:, :]))
            bc_ev2.append(INC("scalar", "sC"))
    qs_bc = 0
    sv_bc = bc_ev2[-1]
    if debug and debug != 2:
        DMA("sync", "qS", dbg["dbg_bcb"].ap(), BCb[:, :])
        DMA("sync", "qS", dbg["dbg_bbc_f"].ap(), Bbc[:, :])
        DMA("sync", "qS", dbg["dbg_cbc_f"].ap(), Cbc[:, :])
    sb_free("BCb")
    W("scalar", "sT", semc["sT"])
    W("gpsimd", "sT", semc["sT"])
    W("sync", "sC", sv_bc)

    dd_s = [sb(f"dd_s{j}", [128, 128], BF16) for j in range(2)]
    W("gpsimd", "sT", semc["sT"])

    dab = [sb(f"dab{j}", [128, T], F32) for j in range(2)]
    initrow = sb("initrow", [1, 2048], BF16)
    agi_sb = sb("agi_sb", [8, 2048], BF16)
    dt_s = [sb(f"dt_s{j}", [128, T], BF16) for j in range(2)]
    xt_s = [sb(f"xt_s{j}", [128, T], BF16) for j in range(2)]
    xs_s = [sb("xs_s0", [128, T], BF16)]
    sz_s = [sb("sz_s0", [128, T], BF16)]
    dbb = [sb("dbb0", [128, T], BF16)]
    hb = [sb(f"hb{j}", [128, T], BF16) for j in range(2)]
    tmpb = [sb("tmpb0", [128, T], BF16)]
    gtd = [sb("gtd0", [128, T], BF16)]
    init_sb = [sb(f"init{j}", [128, S], F32) for j in range(2)]
    hend = [sb(f"hend{j}", [128, S], BF16) for j in range(2)]
    pY = ps("pY", [128, 2048])
    psel = ps("psel", [1, 512])

    em("gpsimd", lambda e: e.memset(agi_sb[:, :], 0.0))
    sg_z = INC("gpsimd", "sG")

    stage_end_sV = {}
    stage_end_sC = {}
    stage_start_sC = {}
    stage_start_sV = {}
    stage_end_sT = {}
    stage_gtd_dma = {}
    stage_he_dma = {}
    stage_sel = {}
    stage_d_done = {}
    stage_gate_sv = {}
    sv_gtd_prev = 0
    for r in range(NST):
        lg = r % G
        rp = r % 2
        if r >= 2:
            W("sync", "sV", stage_end_sV[r - 2])
            W("sync", "sC", stage_end_sC[r - 2])
            W("sync", "sT", stage_end_sT[r - 2])
        if r >= 1:
            W("sync", "sC", stage_start_sC[r - 1])
            W("sync", "sV", stage_start_sV[r - 1])
        q_dt = DMA("sync", "qD", dt_s[rp][:, :], sc_dt.ap()[lg, :, :])
        q_xt = DMA("sync", "qT2", xt_s[rp][:, :], sc_xt.ap()[lg, :, :])
        if r >= 1:
            W("sync", "sT", stage_d_done[r - 1])
            W("sync", "sV", stage_gate_sv[r - 1])
        q_xs = DMA("sync", "qX2", xs_s[0][:, :], sc_xs.ap()[lg, :, :])
        q_sz = DMA("sync", "qZ2", sz_s[0][:, :], sc_sz.ap()[lg, :, :])

        if r >= 2:
            W("gpsimd", "sT", stage_d_done[r - 1])
        q_dd = DMA("gpsimd", "qG", dd_s[rp][:, :],
                   ddiag.ap()[128 * lg:128 * (lg + 1), :])

        if r > 0:
            W("gpsimd", "sCC", r)
            if r >= 2:
                W("gpsimd", "sT", stage_sel[r - 1])
            q_agi = DMA("gpsimd", "qG2", agi_sb[:, :], ag_o[r - 1].ap())
            W("tensor", "qG2", q_agi)
        else:
            W("tensor", "sG", sg_z)
            W("tensor", "qS", qs_p0)
        if r >= 2:
            W("tensor", "sV", sv_gtd_prev)
        sel_ev = []
        for j4 in range(4):
            if j4 >= 1:
                W("tensor", "sC", sel_ev[-1])
            elif r >= 1:
                W("tensor", "qI", q_init_prev)
            em("tensor", lambda e, j4=j4: e.matmul(
                psel[:, :], onehot[:, :], agi_sb[:, 512 * j4:512 * (j4 + 1)],
                start=True, stop=True))
            stq = INC("tensor", "sT")
            W("scalar", "sT", stq)
            em("scalar", lambda e, j4=j4: e.copy(
                initrow[:, 512 * j4:512 * (j4 + 1)], psel[:, :]))
            sel_ev.append(INC("scalar", "sC"))
        stage_sel[r] = semc["sT"]
        W("gpsimd", "sC", sel_ev[-1])
        if r >= 2:
            W("gpsimd", "sV", stage_end_sV[r - 2])
        q_init = DMA("gpsimd", "qI", init_sb[rp][:, :], initrow[0:1, :])
        q_init_prev = q_init

        if r == 0:
            W("vector", "sC", sv_bc)
        W("vector", "qT2", q_xt)
        W("scalar", "qD", q_dt)
        if r == 0:
            W("scalar", "sV", sv_negA)
        sv_base = semc["sV"]
        sc_base = semc["sC"]
        st_base = semc["sT"]
        hend_sc = {}
        stage_start_sC[r] = sc_base + 1
        stage_start_sV[r] = sv_base + 1
        for s_ in range(S):
            sp = s_ % 2
            if s_ >= 2:
                W("scalar", "sV", sv_base + s_ - 1)
            em("scalar", lambda e, lg=lg, s_=s_, sp=sp, rp=rp: e.activation(
                dab[sp][:, :], dt_s[rp][:, :], Act.Copy, bias=1.0,
                scale=negA[:, lg * S + s_:lg * S + s_ + 1]))
            sc_da = INC("scalar", "sC")
            em("vector", lambda e, s_=s_, rp=rp: e.tensor_tensor(
                dbb[0][:, :], xt_s[rp][:, :], Bbc[:, s_ * T:(s_ + 1) * T],
                Alu.mult))
            W("vector", "sC", sc_da)
            if s_ == 0:
                W("vector", "qI", q_init)
            if s_ >= 2:
                W("vector", "sC", hend_sc[s_ - 2])
            em("vector", lambda e, s_=s_, sp=sp, rp=rp: e.tensor_tensor_scan(
                hb[sp][:, :], dab[sp][:, :], dbb[0][:, :],
                init_sb[rp][:, s_:s_ + 1], Alu.mult, Alu.add))
            if s_ >= 1:
                W("vector", "sT", st_base + s_)
            em("vector", lambda e, s_=s_, sp=sp: e.tensor_tensor(
                tmpb[0][:, :], hb[sp][:, :], Cbc[:, s_ * T:(s_ + 1) * T],
                Alu.mult))
            sv_tmp = INC("vector", "sV")
            if debug and debug != 2 and r == 0 and s_ == 0:
                W("sync", "sV", sv_tmp)
                DMA("sync", "qS", dbg["dbg_da0"].ap(), dab[sp][:, :])
                DMA("sync", "qS", dbg["dbg_db0"].ap(), dbb[0][:, :])
                DMA("sync", "qS", dbg["dbg_h0"].ap(), hb[sp][:, :])
                DMA("sync", "qS", dbg["dbg_tmp0"].ap(), tmpb[0][:, :])
                DMA("sync", "qS", dbg["dbg_init0"].ap(), init_sb[rp][:, :])
                DMA("sync", "qS", dbg["dbg_bb0"].ap(), Bbc[:, 0:T])
                DMA("sync", "qS", dbg["dbg_cb0"].ap(), Cbc[:, 0:T])
                DMA("sync", "qS", dbg["dbg_na"].ap(), negA[:, :])
            W("scalar", "sV", sv_tmp)
            if s_ == 0 and r >= 2:
                W("scalar", "qG2", stage_he_dma[r - 2])
            em("scalar", lambda e, s_=s_, sp=sp, rp=rp: e.copy(
                hend[rp][:, s_:s_ + 1], hb[sp][:, T - 1:T]))
            hend_sc[s_] = INC("scalar", "sC")
            if s_ == 0:
                W("tensor", "qX2", q_xs)
                W("tensor", "qG", q_dd)
                for j4 in range(4):
                    em("tensor", lambda e, j4=j4, rp=rp: e.matmul(
                        pY[:, 512 * j4:512 * (j4 + 1)], dd_s[rp][:, :],
                        xs_s[0][:, 512 * j4:512 * (j4 + 1)],
                        start=True, stop=False))
            W("tensor", "sV", sv_tmp)
            for j4 in range(4):
                em("tensor", lambda e, s_=s_, j4=j4: e.matmul(
                    pY[:, 512 * j4:512 * (j4 + 1)], ident[:, :],
                    tmpb[0][:, 512 * j4:512 * (j4 + 1)],
                    start=False, stop=(s_ == S - 1)))
            INC("tensor", "sT")
        stage_d_done[r] = st_base + 1
        st_y = semc["sT"]
        W("vector", "sT", st_y)
        W("vector", "qZ2", q_sz)
        if r >= 1:
            W("vector", "qGtW", stage_gtd_dma[r - 1])
        for j4 in range(4):
            em("vector", lambda e, j4=j4: e.tensor_tensor(
                gtd[0][:, 512 * j4:512 * (j4 + 1)],
                pY[:, 512 * j4:512 * (j4 + 1)],
                sz_s[0][:, 512 * j4:512 * (j4 + 1)], Alu.mult))
        sv_gtd = INC("vector", "sV")
        sv_gtd_prev = sv_gtd
        stage_gate_sv[r] = sv_gtd
        W("sync", "sV", sv_gtd)
        stage_gtd_dma[r] = DMA("sync", "qGtW", sc_gated.ap()[lg, :, :],
                               gtd[0][:, :])
        W("gpsimd", "sC", hend_sc[S - 1])
        stage_he_dma[r] = DMA("gpsimd", "qG2", ag_i[r].ap(), hend[rp][:, :])
        W("gpsimd", "qG2", stage_he_dma[r])
        CC(ag_i[r], ag_o[r])
        stage_end_sV[r] = semc["sV"]
        stage_end_sC[r] = semc["sC"]
        stage_end_sT[r] = semc["sT"]

    if debug and debug != 2:
        for g in range(G):
            for nm, scd in [("dbg_xs", sc_xs), ("dbg_dt", sc_dt),
                            ("dbg_xt", sc_xt), ("dbg_gtd", sc_gated),
                            ("dbg_sz", sc_sz)]:
                DMA("sync", "qS", dbg[nm].ap()[g, :, :], scd.ap()[g, :, :])

    sb_free("Bbc", "Cbc", "dt_s0", "dt_s1", "xt_s0", "xt_s1", "xs_s0",
            "sz_s0", "dab0", "dab1", "dbb0",
            "hb0", "hb1", "tmpb0", "gtd0", "agi_sb",
            "initrow", "init0", "init1", "hend0", "hend1", "pY", "psel",
            "pbc2", "dd_s0", "dd_s1")

    x2 = sb("x2", [128, 16 * DM], F32)
    xr_b = [sb(f"xr{j}", [128, DM], F32) for j in range(2)]
    gin_b = [sb(f"gin{j}", [128, G * 128], BF16) for j in range(2)]
    wo_sb = sb("wo_sb", [128, G * DM], BF16)
    pO = [ps(f"pO{j}", [128, 512]) for j in range(2)]
    for e_ in ("gpsimd", "sync"):
        W(e_, "sT", semc["sT"])
        W(e_, "sV", semc["sV"])
        W(e_, "sC", semc["sC"])
        W(e_, "sG", semc["sG"])
    W("sync", "qGtW", semc["qGtW"])
    W("sync", "qG2", semc["qG2"])
    for k in range(G):
        DMA("gpsimd", "qG", wo_sb[:, k * DM:(k + 1) * DM],
            w_out.ap()[128 * k:128 * (k + 1), :])
    qg_wo = semc["qG"]
    x2_ev = []
    st_o = []
    for i in range(16):
        bj = i % 2
        if i >= 1:
            W("sync", "sT", st_o[-1])
        if i >= 2:
            W("sync", "sV", x2_ev[-1])
        for k in range(G):
            DMA("sync", "qP7", gin_b[bj][:, k * 128:(k + 1) * 128],
                sc_gated.ap()[k, :, 128 * i:128 * (i + 1)])
        q_x = DMA("sync", "qP7", xr_b[bj][:, :],
                  x_in.ap()[3 + 128 * i:3 + 128 * (i + 1), :])
        W("tensor", "qP7", q_x)
        if i == 0:
            W("tensor", "qG", qg_wo)
        for hf in range(2):
            if i >= 1:
                W("tensor", "sV", x2_ev[-2] if len(x2_ev) >= 2 else 0)
            for k in range(G):
                em("tensor", lambda e, k=k, hf=hf, bj=bj: e.matmul(
                    pO[hf][:, :], gin_b[bj][:, k * 128:(k + 1) * 128],
                    wo_sb[:, k * DM + 512 * hf:k * DM + 512 * (hf + 1)],
                    start=(k == 0), stop=(k == G - 1)))
            st_o.append(INC("tensor", "sT"))
            W("vector", "sT", st_o[-1])
            W("vector", "qP7", q_x)
            em("vector", lambda e, i=i, hf=hf, bj=bj: e.tensor_tensor(
                x2[:, i * DM + 512 * hf:i * DM + 512 * (hf + 1)],
                xr_b[bj][:, 512 * hf:512 * (hf + 1)], pO[hf][:, :], Alu.add))
            x2_ev.append(INC("vector", "sV"))
    sv_x2 = x2_ev[-1]
    sb_free("gin0", "gin1", "xr0", "xr1", "pO0", "pO1", "wo_sb", "negA")

    W("sync", "sV", sv_x2)
    for i in range(16):
        DMA("sync", "qOut", sc_x2.ap()[i, :, :], x2[:, i * DM:(i + 1) * DM])
    q_x2w = semc["qOut"]
    if debug:
        for i in range(16):
            DMA("sync", "qS", dbg["dbg_x2"].ap()[128 * i:128 * (i + 1), :],
                x2[:, i * DM:(i + 1) * DM])

    xln2 = sb("xln2", [128, 16 * DM], F32)
    st2 = sb("st2", [128, 16 * 6], F32, high=True)
    scr2 = sb("scr2", [128, DM], F32, high=True)
    mx = sb("mx", [128, 4], F32, high=True)
    psc1 = ps("psc1", [128, 512])
    mrow = sb("mrow", [1, 128], F32, high=True)
    sc1 = sb("sc1", [128, 8], F32, high=True)
    sc1c = sb("sc1c", [1, 24], BF16, high=True)
    rsc = sb("rsc", [1, 8], F32, high=True)
    su2, sq2, rc2, rs2, mn2, b22 = (st2[:, k * 16:(k + 1) * 16]
                                    for k in range(6))
    for i in range(16):
        W("vector", "sV", sv_x2)
        W("scalar", "sV", sv_x2)
        em("vector", lambda e, i=i: e.tensor_reduce(
            su2[:, i:i + 1], x2[:, i * DM:(i + 1) * DM], AxX, Alu.add))
        em("scalar", lambda e, i=i: e.activation(
            scr2[:, :], x2[:, i * DM:(i + 1) * DM], Act.Square,
            accum_out=sq2[:, i:i + 1]))
    scq = INC("scalar", "sC")
    vpad()
    vpad()
    em("vector", lambda e: e.tensor_scalar(mn2[:, :], su2[:, :], 1.0 / DM,
                                           None, Alu.mult))
    W("vector", "sC", scq)
    em("vector", lambda e: e.tensor_scalar(rc2[:, :], sq2[:, :], 1.0 / DM, EPS,
                                           Alu.mult, Alu.add))
    em("vector", lambda e: e.scalar_tensor_tensor(rs2[:, :], mn2[:, :], 1.0,
                                                  mn2[:, :], Alu.mult,
                                                  Alu.mult))
    vpad()
    vpad()
    em("vector", lambda e: e.tensor_tensor(rc2[:, :], rc2[:, :], rs2[:, :],
                                           Alu.subtract))
    vpad()
    vpad()
    em("vector", lambda e: e.reciprocal(rc2[:, :], rc2[:, :]))
    svq = INC("vector", "sV")
    W("scalar", "sV", svq)
    em("scalar", lambda e: e.activation(rs2[:, :], rc2[:, :], Act.Sqrt))
    scq = INC("scalar", "sC")
    W("vector", "sC", scq)
    em("vector", lambda e: e.scalar_tensor_tensor(b22[:, :], mn2[:, :], -1.0,
                                                  rs2[:, :], Alu.mult,
                                                  Alu.mult))
    svq = INC("vector", "sV")
    W("scalar", "sV", svq)
    for i in range(16):
        em("scalar", lambda e, i=i: e.activation(
            xln2[:, i * DM:(i + 1) * DM], x2[:, i * DM:(i + 1) * DM],
            Act.Identity, bias=b22[:, i:i + 1], scale=rs2[:, i:i + 1]))
    sc_ln2 = INC("scalar", "sC")

    psc1_ev = [0]

    def gmax_to_scale(nsrc, src_fn, agk, col, wsem, wval, pair=False):
        nmax = 2 if pair else 1
        if nsrc:
            W("vector", wsem, wval)
            for i in range(nsrc):
                em("vector", lambda e, i=i: e.tensor_reduce(
                    mx[:, 2:3] if i else mx[:, 0:1], src_fn(i), AxX, Alu.max,
                    apply_absolute_value=True))
                if i:
                    vpad()
                    em("vector", lambda e: e.tensor_tensor(
                        mx[:, 0:1], mx[:, 0:1], mx[:, 2:3], Alu.max))
            sgq = INC("vector", "sV")
        else:
            assert wsem == "sV"
            sgq = wval
        W("sync", "sV", sgq)
        qv = DMA("sync", "qS", mrow[0:1, :], mx[:, 0:1])
        W("vector", "qS", qv)
        em("vector", lambda e: e.tensor_reduce(mx[0:1, 0:1], mrow[0:1, :],
                                               AxX, Alu.max))
        em("vector", lambda e: e.tensor_reduce(mx[0:1, 2:3], mrow[0:1, :],
                                               AxX, Alu.max))
        if pair:
            mr1 = INC("vector", "sV")
            W("sync", "sV", mr1)
            qv = DMA("sync", "qS", mrow[0:1, :], mx[:, 1:2])
            W("vector", "qS", qv)
            em("vector", lambda e: e.tensor_reduce(mx[0:1, 1:2], mrow[0:1, :],
                                                   AxX, Alu.max))
            em("vector", lambda e: e.tensor_reduce(mx[0:1, 3:4], mrow[0:1, :],
                                                   AxX, Alu.max))
        else:
            em("vector", lambda e: e.tensor_reduce(mx[0:1, 1:2], mrow[0:1, :],
                                                   AxX, Alu.max))
            em("vector", lambda e: e.tensor_reduce(mx[0:1, 3:4], mrow[0:1, :],
                                                   AxX, Alu.max))
        svq2 = INC("vector", "sV")
        W("sync", "sV", svq2)
        qv = DMA("sync", "qS", agm_i[agk].ap(), mx[0:1, 0:4])
        W("gpsimd", "qS", qv)
        ccv = CC(agm_i[agk], agm_o[agk])
        W("gpsimd", "sCC", ccv)
        qv = DMA("gpsimd", "qG2", mrow[0:1, 0:32], agm_o[agk].ap())
        W("vector", "qG2", qv)
        if pair:
            em("vector", lambda e: e.tensor_reduce(
                mx[0:1, 0:1],
                mrow[0:1, 0:32].rearrange("p (a b) -> p b a", b=2)[:, 0:1, :],
                AxX, Alu.max))
            em("vector", lambda e: e.tensor_reduce(
                mx[0:1, 1:2],
                mrow[0:1, 0:32].rearrange("p (a b) -> p b a", b=2)[:, 1:2, :],
                AxX, Alu.max))
        else:
            em("vector", lambda e: e.tensor_reduce(
                mx[0:1, 0:1], mrow[0:1, 0:32], AxX, Alu.max))
            vpad()
        vpad()
        for j in range(nmax):
            em("vector", lambda e, j=j, col=col: e.tensor_scalar(
                sc1[0:1, col + 2 * j:col + 2 * j + 1], mx[0:1, j:j + 1],
                1.0 / QMAX, 1e-12, Alu.mult, Alu.max))
        vpad()
        vpad()
        for j in range(nmax):
            em("vector", lambda e, j=j, col=col: e.reciprocal(
                sc1[0:1, col + 2 * j + 1:col + 2 * j + 2],
                sc1[0:1, col + 2 * j:col + 2 * j + 1]))
        wid = 2 * nmax
        vpad()
        vpad()
        em("vector", lambda e, col=col, wid=wid: e.tensor_copy(
            sc1c[0:1, col:col + wid], sc1[0:1, col:col + wid]))
        vpad()
        vpad()
        em("vector", lambda e, col=col, wid=wid: e.tensor_tensor(
            rsc[0:1, 0:wid], sc1[0:1, col:col + wid],
            sc1c[0:1, col:col + wid], Alu.subtract))
        vpad()
        vpad()
        em("vector", lambda e, col=col, wid=wid: e.tensor_copy(
            sc1c[0:1, 8 + col:8 + col + wid], rsc[0:1, 0:wid]))
        vpad()
        vpad()
        em("vector", lambda e, col=col, wid=wid: e.tensor_tensor(
            rsc[0:1, 0:wid], rsc[0:1, 0:wid],
            sc1c[0:1, 8 + col:8 + col + wid], Alu.subtract))
        vpad()
        vpad()
        em("vector", lambda e, col=col, wid=wid: e.tensor_copy(
            sc1c[0:1, 16 + col:16 + col + wid], rsc[0:1, 0:wid]))
        svq3 = INC("vector", "sV")
        if debug and agk == 0:
            W("sync", "sV", svq3)
            DMA("sync", "qS", dbg["dbg_mx"].ap(), mx[0:1, 0:4])
            DMA("sync", "qS", dbg["dbg_mrow"].ap(), mrow[0:1, :])
            DMA("sync", "qS", dbg["dbg_sc1a"].ap()[0:1, 0:2], sc1[0:1, 0:2])
            for c_ in range(3):
                DMA("sync", "qS", dbg["dbg_sc1c"].ap()[0:1, 8 * c_:8 * c_ + 2],
                    sc1c[0:1, 8 * c_:8 * c_ + 2])
        W("tensor", "sV", svq3)
        W("tensor", "sC", psc1_ev[0])
        for c in range(3):
            em("tensor", lambda e, col=col, wid=wid, c=c: e.matmul(
                psc1[:, 0:wid], ones1F[0:1, :],
                sc1c[0:1, 8 * c + col:8 * c + col + wid],
                start=(c == 0), stop=(c == 2)))
        stq = INC("tensor", "sT")
        W("scalar", "sT", stq)
        em("scalar", lambda e, col=col, wid=wid: e.copy(
            sc1[:, col:col + wid], psc1[:, 0:wid]))
        psc1_ev[0] = INC("scalar", "sC")
        return psc1_ev[0]

    qv_s1 = gmax_to_scale(16, lambda i: xln2[:, i * DM:(i + 1) * DM], 0, 0,
                          "sC", sc_ln2)

    x_q = sb("x_q", [128, 16 * DM], BF16)
    for i in range(16):
        em("scalar", lambda e, i=i: e.activation(
            scr2[:, :], xln2[:, i * DM:(i + 1) * DM], Act.Identity,
            bias=magics[:, 0:1], scale=sc1[:, 1:2]))
        em("scalar", lambda e: e.activation(
            scr2[:, :], scr2[:, :], Act.Identity, bias=magics[:, 1:2], scale=1.0))
        em("scalar", lambda e, i=i: e.activation(
            x_q[:, i * DM:(i + 1) * DM], scr2[:, :], Act.Identity, bias=0.0,
            scale=sc1[:, 0:1]))
    sv_xq = INC("scalar", "sC")
    sb_free("xln2", "st2")

    xqT = sb("xqT", [128, 8 * T], BF16)
    ptp2 = [ps(f"ptp2_{j}", [128, 128], BF16) for j in range(2)]
    W("tensor", "sC", sv_xq)
    ntp = 0
    xq_ev = []
    for i in range(16):
        for k in range(8):
            j = ntp % 2
            if ntp >= 2:
                W("tensor", "sC", xq_ev[ntp - 2])
            em("tensor", lambda e, i=i, k=k, j=j: e.transpose(
                ptp2[j][:, :], x_q[:, i * DM + 128 * k:i * DM + 128 * (k + 1)],
                ident[:, :]))
            stq = INC("tensor", "sT")
            W("scalar", "sT", stq)
            em("scalar", lambda e, i=i, k=k, j=j: e.copy(
                xqT[:, k * T + 128 * i:k * T + 128 * (i + 1)], ptp2[j][:, :]))
            xq_ev.append(INC("scalar", "sC"))
            ntp += 1
    sc_xqT = xq_ev[-1]
    if debug:
        W("sync", "sC", sc_xqT)
        DMA("sync", "qS", dbg["dbg_xqT"].ap(), xqT[:, :])
    sb_free("x_q", "ptp2_0", "ptp2_1")

    wq_t = [sb(f"wq_t{j}", [128, 8 * 128], BF16, high=True) for j in range(2)]
    pF = [ps(f"pF{j}", [128, 512]) for j in range(2)]
    fevac = [sb(f"fev{j}", [128, T], F32, high=True) for j in range(2)]

    def ffn_mm(w_scr, dst_scr, mcol, first, prev_sg, prev_dma):
        f_ev = []
        st_f = []
        red_done = []
        f_dma = []
        W("gpsimd", "sT", semc["sT"])
        if first:
            W("gpsimd", "qTP", qS2_dq)
        for o in range(OC):
            bj = o % 2
            if o >= 1:
                W("gpsimd", "sT", st_f[-1])
            wd_ = [DMA("gpsimd", "qG", wq_t[bj][:, :],
                       w_scr.ap()[o, :, :])]
            W("tensor", "qG", wd_[-1])
            if o == 0 and first:
                W("tensor", "sC", sc_xqT)
                W("tensor", "qTP", qS2_dq)
            for si in range(4):
                gs = o * 4 + si
                pj = gs % 2
                if gs >= 2:
                    W("tensor", "sC", f_ev[gs - 2])
                for k in range(8):
                    em("tensor", lambda e, k=k, si=si, pj=pj, bj=bj: e.matmul(
                        pF[pj][:, :], wq_t[bj][:, k * 128:(k + 1) * 128],
                        xqT[:, k * T + 512 * si:k * T + 512 * (si + 1)],
                        start=(k == 0), stop=(k == 7)))
                st_f.append(INC("tensor", "sT"))
                W("scalar", "sT", st_f[-1])
                if si == 0:
                    if o >= 2:
                        W("scalar", "sV", red_done[o - 2])
                    elif not first:
                        W("scalar", "sV", prev_sg - (1 - o))
                    if o >= 1:
                        W("scalar", "qFW", f_dma[o - 1])
                    elif not first:
                        W("scalar", "qFW", prev_dma[-1])
                em("scalar", lambda e, si=si, pj=pj, bj=bj: e.copy(
                    fevac[bj][:, 512 * si:512 * (si + 1)], pF[pj][:, :]))
                f_ev.append(INC("scalar", "sC"))
            W("sync", "sC", f_ev[-1])
            f_dma.append(DMA("sync", "qFW", dst_scr.ap()[o, :, :],
                             fevac[bj][:, :]))
            W("vector", "sC", f_ev[-1])
            em("vector", lambda e, bj=bj: e.tensor_reduce(
                mx[:, 2:3], fevac[bj][:, :], AxX, Alu.max,
                apply_absolute_value=True))
            vpad()
            if o == 0:
                em("vector", lambda e, mcol=mcol: e.tensor_copy(
                    mx[:, mcol:mcol + 1], mx[:, 2:3]))
            else:
                em("vector", lambda e, mcol=mcol: e.tensor_tensor(
                    mx[:, mcol:mcol + 1], mx[:, mcol:mcol + 1], mx[:, 2:3],
                    Alu.max))
            red_done.append(INC("vector", "sV"))
        return red_done[-1], f_dma

    sg_up, up_dma = ffn_mm(sc_wuT, sc_up, 0, True, 0, None)
    sg_gate, _gd = ffn_mm(sc_wgT, sc_gate, 1, False, sg_up, up_dma)
    sb_free("xqT", "wq_t0", "wq_t1", "pF0", "pF1", "fev0", "fev1")

    qv_s23 = gmax_to_scale(0, None, 1, 2, "sV", sg_gate, pair=True)

    f_b = [sb(f"f_b{j}", [128, T], F32, high=True) for j in range(2)]
    f_a = [sb(f"f_a{j}", [128, T], F32, high=True) for j in range(4)]
    f_b += [sb(f"f_b{j}", [128, T], F32, high=True) for j in range(2, 4)]
    f_c = [sb(f"f_c{j}", [128, T], F32, high=True) for j in range(2)]
    QN4 = ("qFA", "qFB", "qFC", "qFD")

    def qdq_ops(buf_ap, scol, icol):
        em("vector", lambda e: e.tensor_scalar(
            buf_ap, buf_ap, sc1[:, icol:icol + 1], MAGIC, Alu.mult, Alu.add))
        em("vector", lambda e: e.tensor_scalar(
            buf_ap, buf_ap, MAGIC, None, Alu.subtract))
        em("vector", lambda e: e.tensor_scalar(
            buf_ap, buf_ap, sc1[:, scol:scol + 1], None, Alu.mult))

    g2_dma = []
    g2_red = []
    sig_sc = []
    sv_qdq1 = []
    W("vector", "sC", qv_s23)
    W("sync", "qFW", semc["qFW"])
    if debug:
        DMA("sync", "qS", dbg["dbg_up"].ap(), sc_up.ap()[0, :, :])
        DMA("sync", "qS", dbg["dbg_gate"].ap(), sc_gate.ap()[0, :, :])
        DMA("sync", "qS", dbg["dbg_wuT"].ap(), sc_wuT.ap()[0, :, :])
        DMA("sync", "qS", dbg["dbg_wdT"].ap(), sc_wdT.ap()[0, :, :])
    for o in range(OC):
        bj = o % 4
        qn = QN4[bj]
        if o >= 4:
            W("sync", "sV", sv_qdq1[o - 4])
        qv = DMA("sync", qn, f_a[bj][:, :], sc_gate.ap()[o, :, :])
        W("scalar", qn, qv)
        if o >= 4:
            W("scalar", "sV", sv_qdq1[o - 4])
        em("scalar", lambda e, bj=bj: e.activation(
            f_a[bj][:, :], f_a[bj][:, :], Act.Identity, bias=magics[:, 0:1],
            scale=sc1[:, 5:6]))
        em("scalar", lambda e, bj=bj: e.activation(
            f_a[bj][:, :], f_a[bj][:, :], Act.Identity, bias=magics[:, 1:2],
            scale=1.0))
        em("scalar", lambda e, bj=bj: e.activation(
            f_a[bj][:, :], f_a[bj][:, :], Act.Identity, bias=0.0,
            scale=sc1[:, 4:5]))
        em("scalar", lambda e, bj=bj: e.activation(f_b[bj][:, :],
                                                   f_a[bj][:, :], Act.Sigmoid))
        sig_sc.append(INC("scalar", "sC"))
        W("vector", "sC", sig_sc[-1])
        if o >= 2:
            W("vector", "qFW" if o % 2 == 0 else "qW2", g2_dma[o - 2])
        em("vector", lambda e, bj=bj: e.tensor_scalar(
            f_b[bj][:, :], f_b[bj][:, :], 65536.0, MAGIC, Alu.mult, Alu.add))
        em("vector", lambda e, bj=bj: e.tensor_scalar(
            f_b[bj][:, :], f_b[bj][:, :], MAGIC, 65535.0, Alu.subtract,
            Alu.min))
        em("vector", lambda e, bj=bj: e.tensor_scalar(
            f_b[bj][:, :], f_b[bj][:, :], 1.0 / 65536.0, None, Alu.mult))
        em("vector", lambda e, bj=bj, c2=o % 2: e.tensor_tensor(
            f_c[c2][:, :], f_a[bj][:, :], f_b[bj][:, :], Alu.mult))
        sv2 = INC("vector", "sV")
        sv_qdq1.append(sv2)
        W("sync", "sV", sv2)
        g2_dma.append(DMA("sync", "qFW" if o % 2 == 0 else "qW2",
                          sc_g2.ap()[o, :, :], f_c[o % 2][:, :]))
        em("vector", lambda e, c2=o % 2: e.tensor_reduce(
            mx[:, 2:3], f_c[c2][:, :], AxX, Alu.max,
            apply_absolute_value=True))
        vpad()
        if o == 0:
            em("vector", lambda e: e.tensor_copy(mx[:, 0:1], mx[:, 2:3]))
        else:
            em("vector", lambda e: e.tensor_tensor(mx[:, 0:1], mx[:, 0:1],
                                                   mx[:, 2:3], Alu.max))
        g2_red.append(INC("vector", "sV"))

    qv_s4 = gmax_to_scale(0, None, 2, 6, "sV", g2_red[-1])

    o_dma = []
    o_red = []
    sv_qdq2 = []
    W("vector", "sC", qv_s4)
    W("sync", "qFW", semc["qFW"])
    if debug:
        DMA("sync", "qS", dbg["dbg_g2"].ap(), sc_g2.ap()[0, :, :])
    for o in range(OC):
        bj = o % 4
        qn = QN4[bj]
        if o >= 4:
            W("sync", "sV", sv_qdq2[o - 4])
        qv1 = DMA("sync", qn, f_a[bj][:, :], sc_g2.ap()[o, :, :])
        qv2 = DMA("sync", qn, f_b[bj][:, :], sc_up.ap()[o, :, :])
        W("scalar", qn, qv2)
        if o >= 4:
            W("scalar", "sV", sv_qdq2[o - 4])
        em("scalar", lambda e, bj=bj: e.activation(
            f_a[bj][:, :], f_a[bj][:, :], Act.Identity, bias=magics[:, 0:1],
            scale=sc1[:, 7:8]))
        em("scalar", lambda e, bj=bj: e.activation(
            f_a[bj][:, :], f_a[bj][:, :], Act.Identity, bias=magics[:, 1:2],
            scale=1.0))
        em("scalar", lambda e, bj=bj: e.activation(
            f_a[bj][:, :], f_a[bj][:, :], Act.Identity, bias=0.0,
            scale=sc1[:, 6:7]))
        sq_sc = INC("scalar", "sC")
        W("vector", qn, qv2)
        if o >= 2:
            W("vector", "qFW" if o % 2 == 0 else "qW2", o_dma[o - 2])
        qdq_ops(f_b[bj][:, :], 2, 3)
        W("vector", "sC", sq_sc)
        em("vector", lambda e, bj=bj, c2=o % 2: e.tensor_tensor(
            f_c[c2][:, :], f_a[bj][:, :], f_b[bj][:, :], Alu.mult))
        sv2 = INC("vector", "sV")
        sv_qdq2.append(sv2)
        W("sync", "sV", sv2)
        o_dma.append(DMA("sync", "qFW" if o % 2 == 0 else "qW2",
                         sc_o.ap()[o, :, :], f_c[o % 2][:, :]))
        em("vector", lambda e, c2=o % 2: e.tensor_reduce(
            mx[:, 2:3], f_c[c2][:, :], AxX, Alu.max,
            apply_absolute_value=True))
        vpad()
        if o == 0:
            em("vector", lambda e: e.tensor_copy(mx[:, 0:1], mx[:, 2:3]))
        else:
            em("vector", lambda e: e.tensor_tensor(mx[:, 0:1], mx[:, 0:1],
                                                   mx[:, 2:3], Alu.max))
        o_red.append(INC("vector", "sV"))

    qv_s5 = gmax_to_scale(0, None, 3, 2, "sV", o_red[-1])

    sb_free("x2", "f_c0", "f_c1")
    oq = sb("oq", [128, OC * T], BF16)
    oq_dma_in = []
    oq_sv = []
    W("sync", "sV", sv_qdq2[-1])
    W("sync", "qFW", semc["qFW"])
    W("sync", "qOut", q_x2w)
    if debug:
        DMA("sync", "qS", dbg["dbg_oo"].ap(), sc_o.ap()[0, :, :])
        W("sync", "sC", qv_s5)
        DMA("sync", "qS", dbg["dbg_sc1"].ap(), sc1[:, :])
    W("vector", "sC", qv_s5)
    for o in range(OC):
        bj = o % 4
        qn = QN4[bj]
        if o >= 4:
            W("sync", oq_sv[o - 4][0], oq_sv[o - 4][1])
        qv = DMA("sync", qn, f_a[bj][:, :], sc_o.ap()[o, :, :])
        oq_dma_in.append(qv)
        if o % 2 == 0:
            W("scalar", qn, qv)
            em("scalar", lambda e, bj=bj: e.activation(
                f_a[bj][:, :], f_a[bj][:, :], Act.Identity,
                bias=magics[:, 0:1], scale=sc1[:, 3:4]))
            em("scalar", lambda e, bj=bj: e.activation(
                f_a[bj][:, :], f_a[bj][:, :], Act.Identity,
                bias=magics[:, 1:2], scale=1.0))
            em("scalar", lambda e, o=o, bj=bj: e.activation(
                oq[:, o * T:(o + 1) * T], f_a[bj][:, :], Act.Identity,
                bias=0.0, scale=sc1[:, 2:3]))
            oq_sv.append(("sC", INC("scalar", "sC")))
        else:
            W("vector", qn, qv)
            em("vector", lambda e, bj=bj: e.tensor_scalar(
                f_a[bj][:, :], f_a[bj][:, :], sc1[:, 3:4], MAGIC,
                Alu.mult, Alu.add))
            em("vector", lambda e, bj=bj: e.tensor_scalar(
                f_a[bj][:, :], f_a[bj][:, :], MAGIC, None, Alu.subtract))
            em("vector", lambda e, o=o, bj=bj: e.tensor_scalar(
                oq[:, o * T:(o + 1) * T], f_a[bj][:, :], sc1[:, 2:3], None,
                Alu.mult))
            oq_sv.append(("sV", INC("vector", "sV")))
    oq_last_sc = max(v for s, v in oq_sv if s == "sC")
    oq_last_sv = max(v for s, v in oq_sv if s == "sV")

    sb_free("f_a0", "f_a1", "f_a2", "f_a3", "f_b2", "f_b3")
    wd_sb = sb("wd_sb", [128, OC * DM], BF16)
    W("gpsimd", "sT", semc["sT"])
    W("gpsimd", "sV", semc["sV"])
    W("gpsimd", "sC", semc["sC"])
    W("gpsimd", "qTP", qS2_dq)
    for j2 in range(DM // 128):
        DMA("gpsimd", "qG",
            wd_sb[:, :].rearrange("p (k c) -> p k c", c=DM)[
                :, :, 128 * j2:128 * (j2 + 1)],
            sc_wdT.ap()[j2, :, :].rearrange("p (k c) -> p k c", c=128))
    qg_wd = semc["qG"]
    pD = [ps(f"pD{j}", [128, 512]) for j in range(2)]
    W("tensor", "sC", oq_last_sc)
    W("tensor", "sV", oq_last_sv)
    W("tensor", "qG", qg_wd)
    res_ev = []
    out_dmas = []
    x2_rd = []
    W("sync", "qOut", q_x2w)
    W("sync", "sC", oq_last_sc)
    W("sync", "sV", oq_last_sv)
    for i in range(16):
        bj2 = i % 2
        if i >= 1:
            W("sync", "sV", res_ev[-1])
        if i >= 2:
            W("sync", "sV", res_ev[-3])
        x2_rd.append(DMA("sync", "qFA", f_b[bj2][:, 0:DM], sc_x2.ap()[i, :, :]))
        for hf in range(2):
            if len(res_ev) >= 2:
                W("tensor", "sV", res_ev[-2])
            for k in range(OC):
                em("tensor", lambda e, k=k, i=i, hf=hf: e.matmul(
                    pD[hf][:, :], oq[:, k * T + 128 * i:k * T + 128 * (i + 1)],
                    wd_sb[:, k * DM + 512 * hf:k * DM + 512 * (hf + 1)],
                    start=(k == 0), stop=(k == OC - 1)))
            stq = INC("tensor", "sT")
            W("vector", "sT", stq)
            if i >= 1 and hf == 0:
                W("vector", "qOut", out_dmas[-1])
            if hf == 0:
                W("vector", "qFA", x2_rd[i])
            em("vector", lambda e, i=i, hf=hf, bj=bj2: e.tensor_tensor(
                scr2[:, 512 * hf:512 * (hf + 1)],
                f_b[bj][:, 512 * hf:512 * (hf + 1)],
                pD[hf][:, :], Alu.add))
            res_ev.append(INC("vector", "sV"))
        W("sync", "sV", res_ev[-1])
        out_dmas.append(DMA("sync", "qOut",
                            out_d.ap()[128 * i:128 * (i + 1), :],
                            scr2[:, 0:DM]))

    W("gpsimd", "qOut", semc["qOut"])
    W("gpsimd", "qS", semc["qS"])
    W("gpsimd", "qS2", semc["qS2"])

    bad, stuck = audit_races()
    if stuck:
        raise RuntimeError(f"audit deadlock: {stuck}")
    if bad and debug:
        print(f"audit: {len(bad)} races waived in debug build")
        bad = []
    if bad:
        summ = {}
        for (e, s, v, wv, we) in bad:
            summ.setdefault((e, s), []).append((v, wv, we))
        for k2, vs in summ.items():
            print("RACE", k2, vs[:4], f"(+{len(vs)-4} more)" if len(vs) > 4
                  else "")
        raise RuntimeError(f"{len(bad)} potential semaphore races")

    with nc.Block() as block:
        @block.sync
        def _(eng):
            for f in EMIT["sync"]:
                f(eng)

        @block.vector
        def _(eng):
            for f in EMIT["vector"]:
                f(eng)

        @block.scalar
        def _(eng):
            for f in EMIT["scalar"]:
                f(eng)

        @block.gpsimd
        def _(eng):
            for f in EMIT["gpsimd"]:
                f(eng)

        @block.tensor
        def _(eng):
            for f in EMIT["tensor"]:
                f(eng)

    for name, cnt in semc.items():
        assert cnt < 30000, (name, cnt)
    return nc


def _sel32():
    s = np.zeros((32, 32, 128), BF)
    for r in range(32):
        s[r, r, :] = BF(1.0)
    return s.reshape(32, 32 * 128)


def _perm(p):
    idx = []
    for j in range(G):
        gblk = (j - p) % G
        idx.extend(range(128 * gblk, 128 * (gblk + 1)))
    return np.asarray(idx)


def kernel(L0_attn_x, L0_attn_W_in, L0_attn_conv_w, L0_attn_conv_b,
           L0_attn_W_xproj, L0_attn_W_dt, L0_attn_b_dt, L0_attn_A_log,
           L0_attn_D_param, L0_attn_W_out, L0_ffn_w_gate, L0_ffn_w_up,
           L0_ffn_w_down, _debug=False, _nc=None):
    x = np.ascontiguousarray(np.asarray(L0_attn_x, np.float32))
    W_in = np.asarray(L0_attn_W_in, np.float32)
    conv_w = np.asarray(L0_attn_conv_w, np.float32)[:, 0, :]
    conv_b = np.asarray(L0_attn_conv_b, np.float32).reshape(DI, 1)
    W_xp = np.asarray(L0_attn_W_xproj, np.float32)
    W_dtm = np.asarray(L0_attn_W_dt, np.float32)
    b_dt = np.asarray(L0_attn_b_dt, np.float32).reshape(DI, 1)
    A_log = np.asarray(L0_attn_A_log, np.float32)
    D_par = np.asarray(L0_attn_D_param, np.float32)
    W_outm = np.asarray(L0_attn_W_out, np.float32)
    wg = np.ascontiguousarray(np.asarray(L0_ffn_w_gate, np.float32))
    wu = np.ascontiguousarray(np.asarray(L0_ffn_w_up, np.float32))
    wd = np.ascontiguousarray(np.asarray(L0_ffn_w_down, np.float32))

    base = {
        "wu_nat": wu, "wg_nat": wg, "wd_nat": wd,
        "ident": np.eye(128, dtype=BF),
        "ones1F": np.ones((1, 128), BF),
        "magics": np.repeat(np.array([[MAGIC, -MAGIC]], np.float32), 128,
                            axis=0),
        "sel32": _sel32(),
    }
    perm_sets = []
    for p in range(4):
        pe = _perm(p)
        dd = np.zeros((DI, 128), np.float32)
        for j in range(G):
            np.fill_diagonal(dd[128 * j:128 * (j + 1), :],
                             D_par[pe[128 * j:128 * (j + 1)]])
        perm_sets.append({
            "w_in": np.ascontiguousarray(
                np.concatenate([W_in[:, :DI][:, pe], W_in[:, DI:][:, pe]],
                               axis=1)),
            "w_dt": np.ascontiguousarray(W_dtm[pe][:, pe]),
            "w_xp": np.ascontiguousarray(W_xp[pe]),
            "w_out": np.ascontiguousarray(W_outm[pe]),
            "cw": np.ascontiguousarray(conv_w[pe]),
            "cb": np.ascontiguousarray(conv_b[pe]),
            "bdt": np.ascontiguousarray(b_dt[pe]),
            "alog": np.ascontiguousarray(A_log[pe]),
            "ddiag": dd,
        })

    in_maps = []
    for c in range(NCORE):
        bb, q = c // 4, c % 4
        t0 = q * T
        xin = np.zeros((TH, DM), np.float32)
        if q > 0:
            xin[:3] = x[bb, t0 - 3:t0]
        xin[3:] = x[bb, t0:t0 + T]
        onehot = np.zeros((8, 1), BF)
        if q > 0:
            onehot[c - 1, 0] = BF(1.0)
        im = dict(base)
        im.update(perm_sets[q])
        im["x_in"] = xin
        im["onehot"] = onehot
        in_maps.append(im)

    nc = _nc if _nc is not None else build_nc(debug=_debug)
    res = run_bass_kernel_spmd(nc, in_maps, core_ids=list(range(NCORE)))
    out = np.empty((B, L, DM), np.float32)
    for c in range(NCORE):
        bb, q = c // 4, c % 4
        out[bb, q * T:(q + 1) * T] = np.asarray(res.results[c]["out"])
    if _debug:
        return out, res
    return out


# version 09 (final)
# speedup vs baseline: 1.0769x; 1.0690x over previous
import contextlib
import numpy as np
import ml_dtypes
import concourse.bass as bass
import concourse.mybir as mybir
from concourse.bass_utils import run_bass_kernel_spmd

F32 = mybir.dt.float32
BF16 = mybir.dt.bfloat16
Alu = mybir.AluOpType
Act = mybir.ActivationFunctionType
AxX = mybir.AxisListType.X
BF = ml_dtypes.bfloat16

NCORE = 8
B, L, DM = 2, 8192, 1024
DI, S, G = 2048, 16, 16
T = 2048
TH = T + 3
NST = G + 3
OC = 24
FI = 3072
MAGIC = 12582912.0
QMAX = 32767.0
EPS = 1e-5
ENGS = ("sync", "vector", "scalar", "gpsimd", "tensor")
RG = [list(range(NCORE))]


def build_nc(debug=False):
    nc = bass.Bass("TRN2", target_bir_lowering=False,
                   detect_race_conditions=False)

    def din(name, shape, dt=F32):
        return nc.dram_tensor(name, shape, dt, kind="ExternalInput")

    x_in = din("x_in", [TH, DM])
    w_in = din("w_in", [DM, 2 * DI])
    w_dt = din("w_dt", [DI, DI])
    w_xp = din("w_xp", [DI, 2 * S])
    w_out = din("w_out", [DI, DM])
    cw = din("cw", [DI, 4])
    cb = din("cb", [DI, 1])
    bdt = din("bdt", [DI, 1])
    alog = din("alog", [DI, S])
    ddiag = din("ddiag", [DI, 128])
    wu_nat = din("wu_nat", [FI, DM])
    wg_nat = din("wg_nat", [FI, DM])
    wd_nat = din("wd_nat", [DM, FI])
    ident_in = din("ident", [128, 128], BF16)
    onesF_in = din("ones1F", [1, 128], BF16)
    sel32_in = din("sel32", [32, 32 * 128], BF16)
    onehot_in = din("onehot", [8, 1], BF16)
    magic_in = din("magics", [128, 2], F32)

    out_d = nc.dram_tensor("out", [T, DM], F32, kind="ExternalOutput")
    dbg = {}
    if debug:
        for nm in ("dbg_xs", "dbg_dt", "dbg_xt", "dbg_gtd", "dbg_sz"):
            dbg[nm] = nc.dram_tensor(nm, [G, 128, T], BF16, kind="ExternalOutput")
        dbg["dbg_x2"] = nc.dram_tensor("dbg_x2", [T, DM], F32, kind="ExternalOutput")
        dbg["dbg_da0"] = nc.dram_tensor("dbg_da0", [128, T], F32, kind="ExternalOutput")
        dbg["dbg_db0"] = nc.dram_tensor("dbg_db0", [128, T], BF16, kind="ExternalOutput")
        dbg["dbg_h0"] = nc.dram_tensor("dbg_h0", [128, T], BF16, kind="ExternalOutput")
        dbg["dbg_tmp0"] = nc.dram_tensor("dbg_tmp0", [128, T], BF16, kind="ExternalOutput")
        dbg["dbg_init0"] = nc.dram_tensor("dbg_init0", [128, S], F32, kind="ExternalOutput")
        dbg["dbg_bb0"] = nc.dram_tensor("dbg_bb0", [128, T], BF16, kind="ExternalOutput")
        dbg["dbg_cb0"] = nc.dram_tensor("dbg_cb0", [128, T], BF16, kind="ExternalOutput")
        dbg["dbg_na"] = nc.dram_tensor("dbg_na", [128, G * S], F32, kind="ExternalOutput")
        dbg["dbg_y0"] = nc.dram_tensor("dbg_y0", [128, T], F32, kind="ExternalOutput")
        dbg["dbg_bcb"] = nc.dram_tensor("dbg_bcb", [32, T], BF16, kind="ExternalOutput")
        dbg["dbg_bbc_f"] = nc.dram_tensor("dbg_bbc_f", [128, S * T], BF16, kind="ExternalOutput")
        dbg["dbg_cbc_f"] = nc.dram_tensor("dbg_cbc_f", [128, S * T], BF16, kind="ExternalOutput")
        dbg["dbg_wuT"] = nc.dram_tensor("dbg_wuT", [128, DM], BF16, kind="ExternalOutput")
        dbg["dbg_wdT"] = nc.dram_tensor("dbg_wdT", [128, FI], BF16, kind="ExternalOutput")
        dbg["dbg_xqT"] = nc.dram_tensor("dbg_xqT", [128, 8 * T], BF16, kind="ExternalOutput")
        dbg["dbg_up"] = nc.dram_tensor("dbg_up", [128, T], F32, kind="ExternalOutput")
        dbg["dbg_gate"] = nc.dram_tensor("dbg_gate", [128, T], F32, kind="ExternalOutput")
        dbg["dbg_g2"] = nc.dram_tensor("dbg_g2", [128, T], F32, kind="ExternalOutput")
        dbg["dbg_oo"] = nc.dram_tensor("dbg_oo", [128, T], F32, kind="ExternalOutput")
        dbg["dbg_sc1"] = nc.dram_tensor("dbg_sc1", [128, 8], F32, kind="ExternalOutput")
        dbg["dbg_mx"] = nc.dram_tensor("dbg_mx", [1, 4], F32, kind="ExternalOutput")
        dbg["dbg_mrow"] = nc.dram_tensor("dbg_mrow", [1, 128], F32, kind="ExternalOutput")
        dbg["dbg_sc1a"] = nc.dram_tensor("dbg_sc1a", [1, 8], F32, kind="ExternalOutput")
        dbg["dbg_sc1c"] = nc.dram_tensor("dbg_sc1c", [1, 24], BF16, kind="ExternalOutput")

    sc_dt = nc.dram_tensor("sc_dt", [G, 128, T], BF16)
    sc_xt = nc.dram_tensor("sc_xt", [G, 128, T], BF16)
    sc_xs = nc.dram_tensor("sc_xs", [G, 128, T], BF16)
    sc_sz = nc.dram_tensor("sc_sz", [G, 128, T], BF16)
    sc_gated = nc.dram_tensor("sc_gated", [G, 128, T], BF16)
    sc_wuT = nc.dram_tensor("sc_wuT", [FI // 128, 128, DM], BF16)
    sc_wgT = nc.dram_tensor("sc_wgT", [FI // 128, 128, DM], BF16)
    sc_wdT = nc.dram_tensor("sc_wdT", [DM // 128, 128, FI], BF16)
    sc_up = nc.dram_tensor("sc_up", [OC, 128, T], F32)
    sc_gate = nc.dram_tensor("sc_gate", [OC, 128, T], F32)
    sc_g2 = nc.dram_tensor("sc_g2", [OC, 128, T], F32)
    sc_o = nc.dram_tensor("sc_o", [OC, 128, T], F32)
    sc_x2 = nc.dram_tensor("sc_x2", [16, 128, DM], F32)
    ag_i = [nc.dram_tensor(f"ag_i{r}", [1, 2048], BF16) for r in range(NST)]
    ag_o = [nc.dram_tensor(f"ag_o{r}", [8, 2048], BF16) for r in range(NST)]
    agm_i = [nc.dram_tensor(f"agm_i{k}", [1, 4], F32) for k in range(4)]
    agm_o = [nc.dram_tensor(f"agm_o{k}", [8, 4], F32) for k in range(4)]

    ARENA_N = 52992
    arena_cm = nc.sbuf_tensor("arena", [128, ARENA_N], F32)
    arena = arena_cm.__enter__()
    parena_cm = nc.psum_tensor("parena", [128, 4096], F32)
    parena = parena_cm.__enter__()
    segs = {"s": [[0, ARENA_N, None]], "p": [[0, 4096, None]]}
    seg_names = {}

    def _alloc(pool, name, n_f32, align, high=False):
        n_f32 = -(-n_f32 // align) * align
        it = reversed(segs[pool]) if high else segs[pool]
        for sg in list(it):
            if sg[2] is None and sg[1] >= n_f32:
                if high:
                    off = sg[0] + sg[1] - n_f32
                    if sg[1] > n_f32:
                        segs[pool].insert(segs[pool].index(sg) + 1,
                                          [off, n_f32, name])
                        sg[1] -= n_f32
                    else:
                        sg[2] = name
                    seg_names[name] = (pool, off, n_f32)
                    return off
                off = sg[0]
                if sg[1] > n_f32:
                    segs[pool].insert(segs[pool].index(sg) + 1,
                                      [off + n_f32, sg[1] - n_f32, None])
                sg[1] = n_f32
                sg[2] = name
                seg_names[name] = (pool, off, n_f32)
                return off
        raise MemoryError(f"{pool} arena full for {name} ({n_f32} f32/part); "
                          f"segs={[(s[0], s[1], s[2]) for s in segs[pool]]}")

    def _view(base, off, shape, dt):
        p = shape[0]
        n = int(np.prod(shape[1:]))
        if dt == F32:
            ap = base[0:p, off:off + n]
        elif dt in (BF16,):
            ap = base[0:p, off:off + (n + 1) // 2].bitcast(BF16)[0:p, 0:n]
        else:
            raise ValueError(dt)
        if len(shape) > 2:
            pat = "q (" + " ".join(f"d{i}" for i in range(1, len(shape))) + ") -> q " +                   " ".join(f"d{i}" for i in range(1, len(shape)))
            ap = ap.rearrange(pat, **{f"d{i}": shape[i] for i in range(1, len(shape))})
        return ap

    def sb(name, shape, dt, high=False):
        n = int(np.prod(shape[1:]))
        nf = n if dt == F32 else (n + 1) // 2
        off = _alloc("s", name, nf, 16, high)
        return _view(arena, off, shape, dt)

    def ps(name, shape, dt=F32):
        n = int(np.prod(shape[1:]))
        nf = n if dt == F32 else (n + 1) // 2
        off = _alloc("p", name, nf, 512)
        return _view(parena, off, shape, dt)

    def sb_free(*names):
        for nm in names:
            pool, off, nf = seg_names.pop(nm)
            for sg in segs[pool]:
                if sg[0] == off:
                    sg[2] = None
                    break
            merged = []
            for sg in segs[pool]:
                if merged and merged[-1][2] is None and sg[2] is None:
                    merged[-1][1] += sg[1]
                else:
                    merged.append(sg)
            segs[pool] = merged

    sems, semc = {}, {}

    _semcms = []

    def mksem(name):
        cm = nc.semaphore(name)
        h = cm.__enter__()
        _semcms.append(cm)
        sems[name] = h
        semc[name] = 0

    for s_ in ("qS", "qS2", "qTP", "qFB", "qFC", "qFD", "qW2", "qG", "qG2",
               "sV", "sC", "sT", "sG", "sCC",
               "qX", "qD", "qT2", "qX2", "qZ2", "qI", "qDtW", "qXtW",
               "qSzW", "qGtW", "qP7", "qFW", "qFA", "qOut"):
        mksem(s_)

    EMIT = {e: [] for e in ENGS}
    _iswait = {e: [] for e in ENGS}
    EV = {e: [] for e in ENGS}

    def em(eng, fn, wait=False):
        EMIT[eng].append(fn)
        _iswait[eng].append(wait)

    def W(eng, semname, val):
        if val and val > 0:
            em(eng, lambda e, s=sems[semname], v=val: e.wait_ge(s, v),
               wait=True)
            EV[eng].append(("wait", semname, val))

    def INC(eng, semname, n=1):
        assert EMIT[eng] and not _iswait[eng][-1], (eng, semname)
        semc[semname] += n
        prev = EMIT[eng][-1]
        EMIT[eng][-1] = (lambda e, f=prev, s=sems[semname], k=n:
                         f(e).then_inc(s, k))
        EV[eng].append(("inc", semname, semc[semname]))
        return semc[semname]

    def DMA(eng, q, out_ap, in_ap):
        semc[q] += 16
        em(eng, lambda e, o=out_ap, i=in_ap, s=sems[q]:
           e.dma_start(out=o, in_=i).then_inc(s, 16))
        EV[eng].append(("dma", q, semc[q]))
        return semc[q]

    def CCEV(val):
        EV["gpsimd"].append(("inc", "sCC", val))

    def audit_races():
        waited = {}
        for e in ENGS:
            for (k, s, v) in EV[e]:
                if k == "wait":
                    waited.setdefault(s, set()).add((e, v))
        pos = {e: 0 for e in ENGS}
        clock = {e: {} for e in ENGS}
        snap = {}
        semval = {s: 0 for s in semc}
        bad = []
        progress = True
        while progress:
            progress = False
            for e in ENGS:
                while pos[e] < len(EV[e]):
                    k, s, v = EV[e][pos[e]]
                    if k == "wait":
                        if semval.get(s, 0) < v:
                            break
                        c = clock[e]
                        for ss, vv in snap.get((s, v), {}).items():
                            if c.get(ss, 0) < vv:
                                c[ss] = vv
                        if c.get(s, 0) < v:
                            c[s] = v
                    else:
                        if k == "dma":
                            kn = clock[e].get(s, 0)
                            for (we, wv) in waited.get(s, ()):
                                if we != e and kn < wv < v:
                                    bad.append((e, s, v, wv, we))
                        semval[s] = max(semval.get(s, 0), v)
                        sn = dict(clock[e])
                        sn[s] = v
                        prev = snap.get((s, v))
                        if prev is None:
                            snap[(s, v)] = sn
                        else:
                            for ss, vv in sn.items():
                                if prev.get(ss, 0) > vv:
                                    sn[ss] = prev[ss]
                            snap[(s, v)] = sn
                    pos[e] += 1
                    progress = True
        stuck = {e: (pos[e], EV[e][pos[e]] if pos[e] < len(EV[e]) else None)
                 for e in ENGS if pos[e] < len(EV[e])}
        return bad, stuck

    def CC(k_in, k_out):
        semc["sCC"] += 1
        em("gpsimd", lambda e, a=k_in, b=k_out: e.collective_compute(
            "AllGather", Alu.bypass, replica_groups=RG,
            ins=[a.ap().opt()], outs=[b.ap().opt()]).then_inc(sems["sCC"], 1))
        EV["gpsimd"].append(("inc", "sCC", semc["sCC"]))
        return semc["sCC"]

    def dbl_rows(q, dst2d, width_slice, upto=128):
        k, qv = 1, semc[q]
        while k < upto:
            kk = min(k, upto - k)
            qv = DMA("sync", q, dst2d[k:k + kk, width_slice],
                     dst2d[0:kk, width_slice])
            k += kk
        return qv

    ident = sb("identb_s", [128, 128], BF16)
    vdead = sb("vdead", [1, 8], F32, high=True)

    def vpad():
        em("vector", lambda e: e.tensor_copy(vdead[0:1, 0:4],
                                             ident[0:1, 0:4]))
    ones1F = sb("ones1F_s", [1, 128], BF16)
    sel32 = sb("sel32_s", [32, 32 * 128], BF16)
    onehot = sb("onehot_s", [8, 1], BF16)
    magics = sb("magics_s", [128, 2], F32)
    negA = sb("negA", [128, G * S], F32)
    cw_sb = sb("cw_sb", [128, G * 4], F32)
    cb_sb = sb("cb_sb", [128, G], F32)
    bdt_sb = sb("bdt_sb", [128, G], F32)
    DMA("sync", "qS", ident[:, :], ident_in.ap())
    DMA("sync", "qS", ones1F[:, :], onesF_in.ap())
    DMA("sync", "qS", sel32[:, :], sel32_in.ap())
    DMA("sync", "qS", onehot[:, :], onehot_in.ap())
    DMA("sync", "qS", magics[:, :], magic_in.ap())
    for g in range(G):
        DMA("sync", "qS", negA[:, g * S:(g + 1) * S],
            alog.ap()[128 * g:128 * (g + 1), :])
        DMA("sync", "qS", cw_sb[:, g * 4:(g + 1) * 4],
            cw.ap()[128 * g:128 * (g + 1), :])
        DMA("sync", "qS", cb_sb[:, g:g + 1], cb.ap()[128 * g:128 * (g + 1), :])
        DMA("sync", "qS", bdt_sb[:, g:g + 1], bdt.ap()[128 * g:128 * (g + 1), :])
    qs_p0 = semc["qS"]

    x_sb = sb("x_sb", [128, 17 * DM], F32)
    xrows = [128] * 16 + [3]
    em("gpsimd", lambda e: e.memset(x_sb[0:128, 16 * DM:17 * DM], 0.0))
    sg_xz = INC("gpsimd", "sG")
    qs_x = [DMA("sync", "qX", x_sb[0:xrows[i], i * DM:(i + 1) * DM],
                x_in.ap()[128 * i:128 * i + xrows[i], :]) for i in range(16)]
    W("sync", "sG", sg_xz)
    qs_x.append(DMA("sync", "qX", x_sb[0:3, 16 * DM:17 * DM],
                    x_in.ap()[2048:2051, :]))

    W("scalar", "qS", qs_p0)
    em("scalar", lambda e: e.activation(negA[:, :], negA[:, :], Act.Exp))
    sc_nA = INC("scalar", "sC")
    W("vector", "sC", sc_nA)
    em("vector", lambda e: e.tensor_scalar(negA[:, :], negA[:, :], -1.0, None,
                                           Alu.mult))
    sv_negA = INC("vector", "sV")

    xln = sb("xln", [128, 17 * DM], BF16)
    st = sb("st", [128, 17 * 6], F32)
    scr_sq = sb("scr_sq", [128, DM], F32)
    NT1 = 17
    su, sq_, rc, rs, mn, b2 = (st[:, k * NT1:(k + 1) * NT1] for k in range(6))
    W("vector", "qX", qs_x[-1])
    W("scalar", "qX", qs_x[-1])
    W("vector", "sG", sg_xz)
    W("scalar", "sG", sg_xz)
    for i in range(17):
        em("vector", lambda e, i=i: e.tensor_reduce(
            su[:, i:i + 1], x_sb[:, i * DM:(i + 1) * DM], AxX, Alu.add))
        em("scalar", lambda e, i=i: e.activation(
            scr_sq[:, :], x_sb[:, i * DM:(i + 1) * DM], Act.Square,
            accum_out=sq_[:, i:i + 1]))
    sc_sq = INC("scalar", "sC")
    vpad()
    vpad()
    em("vector", lambda e: e.tensor_scalar(mn[:, :], su[:, :], 1.0 / DM, None,
                                           Alu.mult))
    W("vector", "sC", sc_sq)
    em("vector", lambda e: e.tensor_scalar(rc[:, :], sq_[:, :], 1.0 / DM, EPS,
                                           Alu.mult, Alu.add))
    em("vector", lambda e: e.scalar_tensor_tensor(rs[:, :], mn[:, :], 1.0,
                                                  mn[:, :], Alu.mult, Alu.mult))
    vpad()
    vpad()
    em("vector", lambda e: e.tensor_tensor(rc[:, :], rc[:, :], rs[:, :],
                                           Alu.subtract))
    vpad()
    vpad()
    em("vector", lambda e: e.reciprocal(rc[:, :], rc[:, :]))
    sv1 = INC("vector", "sV")
    W("scalar", "sV", sv1)
    em("scalar", lambda e: e.activation(rs[:, :], rc[:, :], Act.Sqrt))
    sc_rs = INC("scalar", "sC")
    W("vector", "sC", sc_rs)
    em("vector", lambda e: e.scalar_tensor_tensor(b2[:, :], mn[:, :], -1.0,
                                                  rs[:, :], Alu.mult, Alu.mult))
    sv_b2 = INC("vector", "sV")
    W("scalar", "sV", sv_b2)
    for i in range(17):
        em("scalar", lambda e, i=i: e.activation(
            xln[0:xrows[i], i * DM:(i + 1) * DM],
            x_sb[0:xrows[i], i * DM:(i + 1) * DM], Act.Identity,
            bias=b2[0:xrows[i], i:i + 1], scale=rs[0:xrows[i], i:i + 1]))
    sc_ln = INC("scalar", "sC")

    xlnT = sb("xlnT", [128, 8 * TH], BF16)
    ptp = [ps(f"ptp{j}", [128, 128], BF16) for j in range(2)]
    W("tensor", "qS", qs_p0)
    W("tensor", "sC", sc_ln)
    ntp = 0
    for i in range(17):
        for k in range(8):
            j = ntp % 2
            if ntp >= 2:
                W("tensor", "sC", sc_ln + ntp - 1)
            em("tensor", lambda e, i=i, k=k, j=j: e.transpose(
                ptp[j][0:128, 0:xrows[i]],
                xln[0:xrows[i], i * DM + 128 * k:i * DM + 128 * (k + 1)],
                ident[0:xrows[i], 0:xrows[i]]))
            stq = INC("tensor", "sT")
            W("scalar", "sT", stq)
            em("scalar", lambda e, i=i, k=k, j=j: e.copy(
                xlnT[:, k * TH + 128 * i:k * TH + 128 * i + xrows[i]],
                ptp[j][0:128, 0:xrows[i]]))
            INC("scalar", "sC")
            ntp += 1
    sc_xlnT = semc["sC"]
    sb_free("x_sb", "xln", "st", "ptp0", "ptp1", "scr_sq")

    xsT = sb("xsT", [128, G * T], BF16)
    buf_w = [sb(f"buf_w{j}", [128, FI], F32) for j in range(2)]
    wqb = [sb(f"wqb{j}", [128, FI], BF16) for j in range(2)]
    tpb = [sb(f"tpb{j}", [128, FI], BF16) for j in range(2)]
    sq_s = sb("sq_s", [128, 2 * 96], F32)
    sq_i = sb("sq_i", [128, 2 * 96], F32)
    pQs = ps("pQs", [128, 128], F32)

    def emit_dequant():
        vdone = [0, 0]
        tdone = [0, 0]
        tpdma = [0, 0]
        evQ = []
        for (w_n, w_q, O, I) in [
                (wu_nat, sc_wuT, FI, DM),
                (wg_nat, sc_wgT, FI, DM),
                (wd_nat, sc_wdT, DM, FI)]:
            ot, kt, NB = O // 128, I // 128, I // 32
            for j in range(ot):
                bj = j % 2
                W("sync", "sV", vdone[1 - bj])
                qv = DMA("sync", "qS2", buf_w[bj][:, 0:I],
                         w_n.ap()[128 * j:128 * (j + 1), :])
                W("vector", "qS2", qv)
                W("vector", "sT", tdone[bj])
                em("vector", lambda e, bj=bj, I=I, NB=NB: e.tensor_reduce(
                    sq_s[:, bj * 96:bj * 96 + NB].rearrange(
                        "p (b o) -> p b o", o=1),
                    buf_w[bj][:, 0:I].rearrange("p (b q) -> p b q", q=32),
                    AxX, Alu.max, apply_absolute_value=True))
                vpad()
                vpad()
                em("vector", lambda e, bj=bj, NB=NB: e.tensor_scalar(
                    sq_s[:, bj * 96:bj * 96 + NB],
                    sq_s[:, bj * 96:bj * 96 + NB], 1.0 / 7.0, 1e-12,
                    Alu.mult, Alu.max))
                vpad()
                vpad()
                em("vector", lambda e, bj=bj, NB=NB: e.reciprocal(
                    sq_i[:, bj * 96:bj * 96 + NB],
                    sq_s[:, bj * 96:bj * 96 + NB]))
                vpad()
                vpad()
                em("vector", lambda e, bj=bj, I=I, NB=NB: e.tensor_tensor(
                    buf_w[bj][:, 0:I].rearrange("p (b q) -> p b q", q=32),
                    buf_w[bj][:, 0:I].rearrange("p (b q) -> p b q", q=32),
                    sq_i[:, bj * 96:bj * 96 + NB, None].broadcast_to(
                        [128, NB, 32]), Alu.mult))
                em("vector", lambda e, bj=bj, I=I: e.tensor_scalar(
                    buf_w[bj][:, 0:I], buf_w[bj][:, 0:I], MAGIC, MAGIC,
                    Alu.add, Alu.subtract))
                em("vector", lambda e, bj=bj, I=I, NB=NB: e.tensor_tensor(
                    wqb[bj][:, 0:I].rearrange("p (b q) -> p b q", q=32),
                    buf_w[bj][:, 0:I].rearrange("p (b q) -> p b q", q=32),
                    sq_s[:, bj * 96:bj * 96 + NB, None].broadcast_to(
                        [128, NB, 32]), Alu.mult))
                vdone[bj] = INC("vector", "sV")
                W("tensor", "sV", vdone[bj])
                for k in range(kt):
                    if evQ:
                        W("tensor", "sC", evQ[-1])
                    em("tensor", lambda e, bj=bj, k=k: e.matmul(
                        pQs[:, :], wqb[bj][:, 128 * k:128 * (k + 1)],
                        ident[:, :], start=True, stop=True))
                    stq = INC("tensor", "sT")
                    W("scalar", "sT", stq)
                    if k == 0:
                        W("scalar", "qTP", tpdma[1 - bj])
                    em("scalar", lambda e, bj=bj, k=k: e.copy(
                        tpb[bj][:, 128 * k:128 * (k + 1)], pQs[:, :]))
                    evQ.append(INC("scalar", "sC"))
                tdone[bj] = semc["sT"]
                W("sync", "sC", evQ[-1])
                tpdma[bj] = DMA("sync", "qTP", w_q.ap()[j, :, :],
                                tpb[bj][:, 0:I])

    W("sync", "sC", semc["sC"])
    W("sync", "sT", semc["sT"])
    W("sync", "sV", semc["sV"])
    emit_dequant()
    qS2_dq = semc["qTP"]

    win_b = [sb(f"win{j}", [128, 8 * 128], BF16) for j in range(2)]
    xc_b = [sb(f"xc{j}", [128, TH + 1], BF16) for j in range(2)]
    cv = [sb(f"cv{j}", [128, T], BF16) for j in range(4)]
    pxc = [ps(f"pxc{j}", [128, 512]) for j in range(2)]
    slabs = [(0, 3), (3, 512), (515, 512), (1027, 512), (1539, 512)]
    xs_dma = []
    xc_ev = []
    cv_done = []
    st_xc = []

    for g in range(G):
        bj = g % 2
        if g >= 1:
            W("gpsimd", "sT", st_xc[-1])
        w_dmas = [DMA("gpsimd", "qG", win_b[bj][:, k * 128:(k + 1) * 128],
                      w_in.ap()[128 * k:128 * (k + 1), 128 * g:128 * (g + 1)])
                  for k in range(8)]
        W("tensor", "qG", w_dmas[-1])
        if g == 0:
            W("tensor", "sC", sc_xlnT)
        if g >= 2:
            W("tensor", "sV", cv_done[g - 2])
        for si, (t0, ln_) in enumerate(slabs):
            gs = g * 5 + si
            pj = gs % 2
            if gs >= 2:
                W("tensor", "sC", xc_ev[gs - 2])
            for k in range(8):
                em("tensor", lambda e, k=k, t0=t0, ln_=ln_, pj=pj, bj=bj:
                   e.matmul(pxc[pj][:, 0:ln_],
                            win_b[bj][:, k * 128:(k + 1) * 128],
                            xlnT[:, k * TH + t0:k * TH + t0 + ln_],
                            start=(k == 0), stop=(k == 7)))
            st_xc.append(INC("tensor", "sT"))
            W("scalar", "sT", st_xc[-1])
            if g >= 2 and si == 0:
                W("scalar", "sV", cv_done[g - 2])
            em("scalar", lambda e, t0=t0, ln_=ln_, pj=pj, bj=bj: e.copy(
                xc_b[bj][:, t0:t0 + ln_], pxc[pj][:, 0:ln_]))
            xc_ev.append(INC("scalar", "sC"))
        W("vector", "sC", xc_ev[-1])
        for kk in range(4):
            em("vector", lambda e, g=g, bj=bj, kk=kk: e.tensor_scalar(
                cv[kk][:, :], xc_b[bj][:, kk:kk + T],
                cw_sb[:, 4 * g + kk:4 * g + kk + 1], None, Alu.mult))
        em("vector", lambda e: e.tensor_tensor(cv[0][:, :], cv[0][:, :],
                                               cv[1][:, :], Alu.add))
        em("vector", lambda e: e.tensor_tensor(cv[2][:, :], cv[2][:, :],
                                               cv[3][:, :], Alu.add))
        em("vector", lambda e: e.tensor_tensor(cv[0][:, :], cv[0][:, :],
                                               cv[2][:, :], Alu.add))
        cv_done.append(INC("vector", "sV"))
        W("scalar", "sV", cv_done[-1])
        em("scalar", lambda e, g=g: e.activation(
            xsT[:, g * T:(g + 1) * T], cv[0][:, :], Act.Silu,
            bias=cb_sb[:, g:g + 1]))
        sc_sil = INC("scalar", "sC")
        W("sync", "sC", sc_sil)
        xs_dma.append(DMA("sync", "qS", sc_xs.ap()[g, :, :],
                          xsT[:, g * T:(g + 1) * T]))
        W("vector", "sC", sc_sil)
    sc_xsT_all = semc["sC"]
    sb_free("win0", "win1", "xc0", "xc1", "cv0", "cv1", "cv2", "cv3",
            "pxc0", "pxc1")

    wdt_b = [sb(f"wdt{j}", [128, 16 * 128], BF16) for j in range(2)]
    dtb = sb("dtb", [128, T], BF16)
    xtb = sb("xtb", [128, T], BF16)
    pdt = [ps(f"pdt{j}", [128, 512]) for j in range(2)]
    dt_scr = sb("dt_scr", [128, 512], F32)
    BCb = sb("BCb", [32, T], BF16)
    pbc = ps("pbc", [32, 512])
    dt_ev = []
    st_dt = []
    W("gpsimd", "sT", semc["sT"])
    for g in range(G):
        bj = g % 2
        if g >= 1:
            W("gpsimd", "sT", st_dt[-1])
        w_dmas = [DMA("gpsimd", "qG", wdt_b[bj][:, k * 128:(k + 1) * 128],
                      w_dt.ap()[128 * k:128 * (k + 1), 128 * g:128 * (g + 1)])
                  for k in range(G)]
        W("tensor", "qG", w_dmas[-1])
        if g == 0:
            W("tensor", "sC", sc_xsT_all)
        for si in range(4):
            gs = g * 4 + si
            pj = gs % 2
            if gs >= 2:
                W("tensor", "sC", dt_ev[gs - 2])
            for k in range(G):
                em("tensor", lambda e, k=k, si=si, pj=pj, bj=bj: e.matmul(
                    pdt[pj][:, :], wdt_b[bj][:, k * 128:(k + 1) * 128],
                    xsT[:, k * T + 512 * si:k * T + 512 * (si + 1)],
                    start=(k == 0), stop=(k == G - 1)))
            st_dt.append(INC("tensor", "sT"))
            W("scalar", "sT", st_dt[-1])
            em("scalar", lambda e, g=g, pj=pj: e.activation(
                dt_scr[:, :], pdt[pj][:, :], Act.Exp, bias=bdt_sb[:, g:g + 1]))
            em("scalar", lambda e, si=si: e.activation(
                dtb[:, 512 * si:512 * (si + 1)], dt_scr[:, :], Act.Ln,
                bias=1.0))
            dt_ev.append(INC("scalar", "sC"))
        W("vector", "sC", dt_ev[-1])
        em("vector", lambda e, g=g: e.tensor_tensor(
            xtb[:, :], xsT[:, g * T:(g + 1) * T], dtb[:, :], Alu.mult))
        sv_xt = INC("vector", "sV")
        W("sync", "sC", dt_ev[-1])
        q_dt = DMA("sync", "qDtW", sc_dt.ap()[g, :, :], dtb[:, :])
        W("sync", "sV", sv_xt)
        q_xt = DMA("sync", "qXtW", sc_xt.ap()[g, :, :], xtb[:, :])
        W("scalar", "qDtW", q_dt)
        W("scalar", "sV", sv_xt)
        W("vector", "qXtW", q_xt)

    wxp_sb = sb("wxp_sb", [128, G * 32], BF16)
    W("gpsimd", "sT", semc["sT"])
    for k in range(G):
        DMA("gpsimd", "qG", wxp_sb[:, k * 32:(k + 1) * 32],
            w_xp.ap()[128 * k:128 * (k + 1), :])
    qg_xp = semc["qG"]
    W("tensor", "qG", qg_xp)
    bc_ev = []
    for si in range(4):
        if si >= 1:
            W("tensor", "sC", bc_ev[-1])
        for k in range(G):
            em("tensor", lambda e, k=k, si=si: e.matmul(
                pbc[:, :], wxp_sb[:, k * 32:(k + 1) * 32],
                xsT[:, k * T + 512 * si:k * T + 512 * (si + 1)],
                start=(k == 0), stop=(k == G - 1)))
        stq = INC("tensor", "sT")
        W("scalar", "sT", stq)
        em("scalar", lambda e, si=si: e.copy(BCb[:, 512 * si:512 * (si + 1)],
                                             pbc[:, :]))
        bc_ev.append(INC("scalar", "sC"))
    sc_bc = bc_ev[-1]

    winz_b = [sb(f"winz{j}", [128, 8 * 128], BF16) for j in range(2)]
    szb = sb("szb", [128, T], BF16)
    psz = [ps(f"psz{j}", [128, 512]) for j in range(2)]
    sz_ev = []
    st_z = []
    W("gpsimd", "sT", semc["sT"])
    for g in range(G):
        bj = g % 2
        if g >= 1:
            W("gpsimd", "sT", st_z[-1])
        w_dmas = [DMA("gpsimd", "qG", winz_b[bj][:, k * 128:(k + 1) * 128],
                      w_in.ap()[128 * k:128 * (k + 1),
                                DI + 128 * g:DI + 128 * (g + 1)])
                  for k in range(8)]
        W("tensor", "qG", w_dmas[-1])
        for si in range(4):
            gs = g * 4 + si
            pj = gs % 2
            if gs >= 2:
                W("tensor", "sC", sz_ev[gs - 2])
            for k in range(8):
                em("tensor", lambda e, k=k, si=si, pj=pj, bj=bj: e.matmul(
                    psz[pj][:, :], winz_b[bj][:, k * 128:(k + 1) * 128],
                    xlnT[:, k * TH + 3 + 512 * si:k * TH + 3 + 512 * (si + 1)],
                    start=(k == 0), stop=(k == 7)))
            st_z.append(INC("tensor", "sT"))
            W("scalar", "sT", st_z[-1])
            em("scalar", lambda e, si=si, pj=pj: e.activation(
                szb[:, 512 * si:512 * (si + 1)], psz[pj][:, :], Act.Silu))
            sz_ev.append(INC("scalar", "sC"))
        W("sync", "sC", sz_ev[-1])
        q_sz = DMA("sync", "qSzW", sc_sz.ap()[g, :, :], szb[:, :])
        W("scalar", "qSzW", q_sz)

    sb_free("xlnT", "winz0", "winz1", "psz0", "psz1", "szb", "wdt0", "wdt1",
            "dtb", "xtb", "pdt0", "pdt1", "dt_scr", "wxp_sb", "pbc",
            "buf_w0", "buf_w1", "wqb0", "wqb1", "tpb0", "tpb1", "sq_s",
            "sq_i", "pQs", "xsT", "cw_sb", "cb_sb", "bdt_sb")

    Bbc = sb("Bbc", [128, S * T], BF16)
    Cbc = sb("Cbc", [128, S * T], BF16)
    pbc2 = ps("pbc2", [128, 512])
    W("sync", "sC", semc["sC"])
    W("sync", "sT", semc["sT"])
    W("sync", "sV", semc["sV"])
    W("sync", "sG", semc["sG"])
    W("sync", "qS", xs_dma[-1])
    W("sync", "qS2", semc["qS2"])
    W("sync", "qDtW", semc["qDtW"])
    W("sync", "qXtW", semc["qXtW"])
    W("sync", "qSzW", semc["qSzW"])
    W("tensor", "sC", sc_bc)
    W("tensor", "qS", qs_p0)
    bc_ev2 = []
    for idx in range(2 * S):
        s_ = idx % S
        dst = Bbc if idx < S else Cbc
        srow = s_ if idx < S else S + s_
        for sl in range(4):
            if bc_ev2:
                if len(bc_ev2) >= 1:
                    W("tensor", "sC", bc_ev2[-1])
            em("tensor", lambda e, srow=srow, sl=sl: e.matmul(
                pbc2[:, :], sel32[:, 128 * srow:128 * (srow + 1)]
                if srow < 32 else None,
                BCb[0:32, 512 * sl:512 * (sl + 1)],
                start=True, stop=True))
            stq = INC("tensor", "sT")
            W("scalar", "sT", stq)
            em("scalar", lambda e, dst=dst, s_=s_, sl=sl: e.copy(
                dst[:, s_ * T + 512 * sl:s_ * T + 512 * (sl + 1)],
                pbc2[:, :]))
            bc_ev2.append(INC("scalar", "sC"))
    qs_bc = 0
    sv_bc = bc_ev2[-1]
    if debug and debug != 2:
        DMA("sync", "qS", dbg["dbg_bcb"].ap(), BCb[:, :])
        DMA("sync", "qS", dbg["dbg_bbc_f"].ap(), Bbc[:, :])
        DMA("sync", "qS", dbg["dbg_cbc_f"].ap(), Cbc[:, :])
    sb_free("BCb")
    W("scalar", "sT", semc["sT"])
    W("gpsimd", "sT", semc["sT"])
    W("sync", "sC", sv_bc)

    dd_s = [sb(f"dd_s{j}", [128, 128], BF16) for j in range(2)]
    W("gpsimd", "sT", semc["sT"])

    dab = [sb(f"dab{j}", [128, T], F32) for j in range(2)]
    initrow = sb("initrow", [1, 2048], BF16)
    agi_sb = sb("agi_sb", [8, 2048], BF16)
    dt_s = [sb(f"dt_s{j}", [128, T], BF16) for j in range(2)]
    xt_s = [sb(f"xt_s{j}", [128, T], BF16) for j in range(2)]
    xs_s = [sb("xs_s0", [128, T], BF16)]
    sz_s = [sb("sz_s0", [128, T], BF16)]
    dbb = [sb("dbb0", [128, T], BF16)]
    hb = [sb(f"hb{j}", [128, T], BF16) for j in range(2)]
    tmpb = [sb("tmpb0", [128, T], BF16)]
    gtd = [sb("gtd0", [128, T], BF16)]
    init_sb = [sb(f"init{j}", [128, S], F32) for j in range(2)]
    hend = [sb(f"hend{j}", [128, S], BF16) for j in range(2)]
    pY = ps("pY", [128, 2048])
    psel = ps("psel", [1, 512])

    em("gpsimd", lambda e: e.memset(agi_sb[:, :], 0.0))
    sg_z = INC("gpsimd", "sG")

    stage_end_sV = {}
    stage_end_sC = {}
    stage_start_sC = {}
    stage_start_sV = {}
    stage_end_sT = {}
    stage_gtd_dma = {}
    stage_he_dma = {}
    stage_sel = {}
    stage_d_done = {}
    stage_gate_sv = {}
    sv_gtd_prev = 0
    for r in range(NST):
        lg = r % G
        rp = r % 2
        if r >= 2:
            W("sync", "sV", stage_end_sV[r - 2])
            W("sync", "sC", stage_end_sC[r - 2])
            W("sync", "sT", stage_end_sT[r - 2])
        if r >= 1:
            W("sync", "sC", stage_start_sC[r - 1])
            W("sync", "sV", stage_start_sV[r - 1])
        q_dt = DMA("sync", "qD", dt_s[rp][:, :], sc_dt.ap()[lg, :, :])
        q_xt = DMA("sync", "qT2", xt_s[rp][:, :], sc_xt.ap()[lg, :, :])
        if r >= 1:
            W("sync", "sT", stage_d_done[r - 1])
            W("sync", "sV", stage_gate_sv[r - 1])
        q_xs = DMA("sync", "qX2", xs_s[0][:, :], sc_xs.ap()[lg, :, :])
        q_sz = DMA("sync", "qZ2", sz_s[0][:, :], sc_sz.ap()[lg, :, :])

        if r >= 2:
            W("gpsimd", "sT", stage_d_done[r - 1])
        q_dd = DMA("gpsimd", "qG", dd_s[rp][:, :],
                   ddiag.ap()[128 * lg:128 * (lg + 1), :])

        if r > 0:
            W("gpsimd", "sCC", r)
            if r >= 2:
                W("gpsimd", "sT", stage_sel[r - 1])
            q_agi = DMA("gpsimd", "qG2", agi_sb[:, :], ag_o[r - 1].ap())
            W("tensor", "qG2", q_agi)
        else:
            W("tensor", "sG", sg_z)
            W("tensor", "qS", qs_p0)
        if r >= 2:
            W("tensor", "sV", sv_gtd_prev)
        sel_ev = []
        for j4 in range(4):
            if j4 >= 1:
                W("tensor", "sC", sel_ev[-1])
            elif r >= 1:
                W("tensor", "qI", q_init_prev)
            em("tensor", lambda e, j4=j4: e.matmul(
                psel[:, :], onehot[:, :], agi_sb[:, 512 * j4:512 * (j4 + 1)],
                start=True, stop=True))
            stq = INC("tensor", "sT")
            W("scalar", "sT", stq)
            em("scalar", lambda e, j4=j4: e.copy(
                initrow[:, 512 * j4:512 * (j4 + 1)], psel[:, :]))
            sel_ev.append(INC("scalar", "sC"))
        stage_sel[r] = semc["sT"]
        W("gpsimd", "sC", sel_ev[-1])
        if r >= 2:
            W("gpsimd", "sV", stage_end_sV[r - 2])
        q_init = DMA("gpsimd", "qI", init_sb[rp][:, :], initrow[0:1, :])
        q_init_prev = q_init

        if r == 0:
            W("vector", "sC", sv_bc)
        W("vector", "qT2", q_xt)
        W("scalar", "qD", q_dt)
        if r == 0:
            W("scalar", "sV", sv_negA)
        sv_base = semc["sV"]
        sc_base = semc["sC"]
        st_base = semc["sT"]
        hend_sc = {}
        stage_start_sC[r] = sc_base + 1
        stage_start_sV[r] = sv_base + 1
        for s_ in range(S):
            sp = s_ % 2
            if s_ >= 2:
                W("scalar", "sV", sv_base + s_ - 1)
            em("scalar", lambda e, lg=lg, s_=s_, sp=sp, rp=rp: e.activation(
                dab[sp][:, :], dt_s[rp][:, :], Act.Copy, bias=1.0,
                scale=negA[:, lg * S + s_:lg * S + s_ + 1]))
            sc_da = INC("scalar", "sC")
            em("vector", lambda e, s_=s_, rp=rp: e.tensor_tensor(
                dbb[0][:, :], xt_s[rp][:, :], Bbc[:, s_ * T:(s_ + 1) * T],
                Alu.mult))
            W("vector", "sC", sc_da)
            if s_ == 0:
                W("vector", "qI", q_init)
            if s_ >= 2:
                W("vector", "sC", hend_sc[s_ - 2])
            em("vector", lambda e, s_=s_, sp=sp, rp=rp: e.tensor_tensor_scan(
                hb[sp][:, :], dab[sp][:, :], dbb[0][:, :],
                init_sb[rp][:, s_:s_ + 1], Alu.mult, Alu.add))
            if s_ >= 1:
                W("vector", "sT", st_base + s_)
            em("vector", lambda e, s_=s_, sp=sp: e.tensor_tensor(
                tmpb[0][:, :], hb[sp][:, :], Cbc[:, s_ * T:(s_ + 1) * T],
                Alu.mult))
            sv_tmp = INC("vector", "sV")
            if debug and debug != 2 and r == 0 and s_ == 0:
                W("sync", "sV", sv_tmp)
                DMA("sync", "qS", dbg["dbg_da0"].ap(), dab[sp][:, :])
                DMA("sync", "qS", dbg["dbg_db0"].ap(), dbb[0][:, :])
                DMA("sync", "qS", dbg["dbg_h0"].ap(), hb[sp][:, :])
                DMA("sync", "qS", dbg["dbg_tmp0"].ap(), tmpb[0][:, :])
                DMA("sync", "qS", dbg["dbg_init0"].ap(), init_sb[rp][:, :])
                DMA("sync", "qS", dbg["dbg_bb0"].ap(), Bbc[:, 0:T])
                DMA("sync", "qS", dbg["dbg_cb0"].ap(), Cbc[:, 0:T])
                DMA("sync", "qS", dbg["dbg_na"].ap(), negA[:, :])
            W("scalar", "sV", sv_tmp)
            if s_ == 0 and r >= 2:
                W("scalar", "qG2", stage_he_dma[r - 2])
            em("scalar", lambda e, s_=s_, sp=sp, rp=rp: e.copy(
                hend[rp][:, s_:s_ + 1], hb[sp][:, T - 1:T]))
            hend_sc[s_] = INC("scalar", "sC")
            if s_ == 0:
                W("tensor", "qX2", q_xs)
                W("tensor", "qG", q_dd)
                for j4 in range(4):
                    em("tensor", lambda e, j4=j4, rp=rp: e.matmul(
                        pY[:, 512 * j4:512 * (j4 + 1)], dd_s[rp][:, :],
                        xs_s[0][:, 512 * j4:512 * (j4 + 1)],
                        start=True, stop=False))
            W("tensor", "sV", sv_tmp)
            for j4 in range(4):
                em("tensor", lambda e, s_=s_, j4=j4: e.matmul(
                    pY[:, 512 * j4:512 * (j4 + 1)], ident[:, :],
                    tmpb[0][:, 512 * j4:512 * (j4 + 1)],
                    start=False, stop=(s_ == S - 1)))
            INC("tensor", "sT")
        stage_d_done[r] = st_base + 1
        st_y = semc["sT"]
        W("vector", "sT", st_y)
        W("vector", "qZ2", q_sz)
        if r >= 1:
            W("vector", "qGtW", stage_gtd_dma[r - 1])
        for j4 in range(4):
            em("vector", lambda e, j4=j4: e.tensor_tensor(
                gtd[0][:, 512 * j4:512 * (j4 + 1)],
                pY[:, 512 * j4:512 * (j4 + 1)],
                sz_s[0][:, 512 * j4:512 * (j4 + 1)], Alu.mult))
        sv_gtd = INC("vector", "sV")
        sv_gtd_prev = sv_gtd
        stage_gate_sv[r] = sv_gtd
        W("sync", "sV", sv_gtd)
        stage_gtd_dma[r] = DMA("sync", "qGtW", sc_gated.ap()[lg, :, :],
                               gtd[0][:, :])
        W("gpsimd", "sC", hend_sc[S - 1])
        stage_he_dma[r] = DMA("gpsimd", "qG2", ag_i[r].ap(), hend[rp][:, :])
        W("gpsimd", "qG2", stage_he_dma[r])
        CC(ag_i[r], ag_o[r])
        stage_end_sV[r] = semc["sV"]
        stage_end_sC[r] = semc["sC"]
        stage_end_sT[r] = semc["sT"]

    if debug and debug != 2:
        for g in range(G):
            for nm, scd in [("dbg_xs", sc_xs), ("dbg_dt", sc_dt),
                            ("dbg_xt", sc_xt), ("dbg_gtd", sc_gated),
                            ("dbg_sz", sc_sz)]:
                DMA("sync", "qS", dbg[nm].ap()[g, :, :], scd.ap()[g, :, :])

    sb_free("Bbc", "Cbc", "dt_s0", "dt_s1", "xt_s0", "xt_s1", "xs_s0",
            "sz_s0", "dab0", "dab1", "dbb0",
            "hb0", "hb1", "tmpb0", "gtd0", "agi_sb",
            "initrow", "init0", "init1", "hend0", "hend1", "pY", "psel",
            "pbc2", "dd_s0", "dd_s1")

    x2 = sb("x2", [128, 16 * DM], F32)
    xr_b = [sb(f"xr{j}", [128, DM], F32) for j in range(2)]
    gin_b = [sb(f"gin{j}", [128, G * 128], BF16) for j in range(2)]
    wo_sb = sb("wo_sb", [128, G * DM], BF16)
    pO = [ps(f"pO{j}", [128, 512]) for j in range(2)]
    for e_ in ("gpsimd", "sync"):
        W(e_, "sT", semc["sT"])
        W(e_, "sV", semc["sV"])
        W(e_, "sC", semc["sC"])
        W(e_, "sG", semc["sG"])
    W("sync", "qGtW", semc["qGtW"])
    W("sync", "qG2", semc["qG2"])
    for k in range(G):
        DMA("gpsimd", "qG", wo_sb[:, k * DM:(k + 1) * DM],
            w_out.ap()[128 * k:128 * (k + 1), :])
    qg_wo = semc["qG"]
    x2_ev = []
    st_o = []
    for i in range(16):
        bj = i % 2
        if i >= 1:
            W("sync", "sT", st_o[-1])
        if i >= 2:
            W("sync", "sV", x2_ev[-1])
        for k in range(G):
            DMA("sync", "qP7", gin_b[bj][:, k * 128:(k + 1) * 128],
                sc_gated.ap()[k, :, 128 * i:128 * (i + 1)])
        q_x = DMA("sync", "qP7", xr_b[bj][:, :],
                  x_in.ap()[3 + 128 * i:3 + 128 * (i + 1), :])
        W("tensor", "qP7", q_x)
        if i == 0:
            W("tensor", "qG", qg_wo)
        for hf in range(2):
            if i >= 1:
                W("tensor", "sV", x2_ev[-2] if len(x2_ev) >= 2 else 0)
            for k in range(G):
                em("tensor", lambda e, k=k, hf=hf, bj=bj: e.matmul(
                    pO[hf][:, :], gin_b[bj][:, k * 128:(k + 1) * 128],
                    wo_sb[:, k * DM + 512 * hf:k * DM + 512 * (hf + 1)],
                    start=(k == 0), stop=(k == G - 1)))
            st_o.append(INC("tensor", "sT"))
            W("vector", "sT", st_o[-1])
            W("vector", "qP7", q_x)
            em("vector", lambda e, i=i, hf=hf, bj=bj: e.tensor_tensor(
                x2[:, i * DM + 512 * hf:i * DM + 512 * (hf + 1)],
                xr_b[bj][:, 512 * hf:512 * (hf + 1)], pO[hf][:, :], Alu.add))
            x2_ev.append(INC("vector", "sV"))
    sv_x2 = x2_ev[-1]
    sb_free("gin0", "gin1", "xr0", "xr1", "pO0", "pO1", "wo_sb", "negA")

    W("sync", "sV", sv_x2)
    for i in range(16):
        DMA("sync", "qOut", sc_x2.ap()[i, :, :], x2[:, i * DM:(i + 1) * DM])
    q_x2w = semc["qOut"]
    if debug:
        for i in range(16):
            DMA("sync", "qS", dbg["dbg_x2"].ap()[128 * i:128 * (i + 1), :],
                x2[:, i * DM:(i + 1) * DM])

    xln2 = sb("xln2", [128, 16 * DM], F32)
    st2 = sb("st2", [128, 16 * 6], F32, high=True)
    scr2 = sb("scr2", [128, DM], F32, high=True)
    mx = sb("mx", [128, 4], F32, high=True)
    psc1 = ps("psc1", [128, 512])
    mrow = sb("mrow", [1, 128], F32, high=True)
    sc1 = sb("sc1", [128, 8], F32, high=True)
    sc1c = sb("sc1c", [1, 24], BF16, high=True)
    rsc = sb("rsc", [1, 8], F32, high=True)
    su2, sq2, rc2, rs2, mn2, b22 = (st2[:, k * 16:(k + 1) * 16]
                                    for k in range(6))
    for i in range(16):
        W("vector", "sV", sv_x2)
        W("scalar", "sV", sv_x2)
        em("vector", lambda e, i=i: e.tensor_reduce(
            su2[:, i:i + 1], x2[:, i * DM:(i + 1) * DM], AxX, Alu.add))
        em("scalar", lambda e, i=i: e.activation(
            scr2[:, :], x2[:, i * DM:(i + 1) * DM], Act.Square,
            accum_out=sq2[:, i:i + 1]))
    scq = INC("scalar", "sC")
    vpad()
    vpad()
    em("vector", lambda e: e.tensor_scalar(mn2[:, :], su2[:, :], 1.0 / DM,
                                           None, Alu.mult))
    W("vector", "sC", scq)
    em("vector", lambda e: e.tensor_scalar(rc2[:, :], sq2[:, :], 1.0 / DM, EPS,
                                           Alu.mult, Alu.add))
    em("vector", lambda e: e.scalar_tensor_tensor(rs2[:, :], mn2[:, :], 1.0,
                                                  mn2[:, :], Alu.mult,
                                                  Alu.mult))
    vpad()
    vpad()
    em("vector", lambda e: e.tensor_tensor(rc2[:, :], rc2[:, :], rs2[:, :],
                                           Alu.subtract))
    vpad()
    vpad()
    em("vector", lambda e: e.reciprocal(rc2[:, :], rc2[:, :]))
    svq = INC("vector", "sV")
    W("scalar", "sV", svq)
    em("scalar", lambda e: e.activation(rs2[:, :], rc2[:, :], Act.Sqrt))
    scq = INC("scalar", "sC")
    W("vector", "sC", scq)
    em("vector", lambda e: e.scalar_tensor_tensor(b22[:, :], mn2[:, :], -1.0,
                                                  rs2[:, :], Alu.mult,
                                                  Alu.mult))
    svq = INC("vector", "sV")
    W("scalar", "sV", svq)
    for i in range(16):
        em("scalar", lambda e, i=i: e.activation(
            xln2[:, i * DM:(i + 1) * DM], x2[:, i * DM:(i + 1) * DM],
            Act.Identity, bias=b22[:, i:i + 1], scale=rs2[:, i:i + 1]))
    sc_ln2 = INC("scalar", "sC")

    psc1_ev = [0]

    def gmax_to_scale(nsrc, src_fn, agk, col, wsem, wval, pair=False):
        nmax = 2 if pair else 1
        if nsrc:
            W("vector", wsem, wval)
            for i in range(nsrc):
                em("vector", lambda e, i=i: e.tensor_reduce(
                    mx[:, 2:3] if i else mx[:, 0:1], src_fn(i), AxX, Alu.max,
                    apply_absolute_value=True))
                if i:
                    vpad()
                    em("vector", lambda e: e.tensor_tensor(
                        mx[:, 0:1], mx[:, 0:1], mx[:, 2:3], Alu.max))
            sgq = INC("vector", "sV")
        else:
            assert wsem == "sV"
            sgq = wval
        W("sync", "sV", sgq)
        qv = DMA("sync", "qS", mrow[0:1, :], mx[:, 0:1])
        W("vector", "qS", qv)
        em("vector", lambda e: e.tensor_reduce(mx[0:1, 0:1], mrow[0:1, :],
                                               AxX, Alu.max))
        em("vector", lambda e: e.tensor_reduce(mx[0:1, 2:3], mrow[0:1, :],
                                               AxX, Alu.max))
        if pair:
            mr1 = INC("vector", "sV")
            W("sync", "sV", mr1)
            qv = DMA("sync", "qS", mrow[0:1, :], mx[:, 1:2])
            W("vector", "qS", qv)
            em("vector", lambda e: e.tensor_reduce(mx[0:1, 1:2], mrow[0:1, :],
                                                   AxX, Alu.max))
            em("vector", lambda e: e.tensor_reduce(mx[0:1, 3:4], mrow[0:1, :],
                                                   AxX, Alu.max))
        else:
            em("vector", lambda e: e.tensor_reduce(mx[0:1, 1:2], mrow[0:1, :],
                                                   AxX, Alu.max))
            em("vector", lambda e: e.tensor_reduce(mx[0:1, 3:4], mrow[0:1, :],
                                                   AxX, Alu.max))
        svq2 = INC("vector", "sV")
        W("sync", "sV", svq2)
        qv = DMA("sync", "qS", agm_i[agk].ap(), mx[0:1, 0:4])
        W("gpsimd", "qS", qv)
        ccv = CC(agm_i[agk], agm_o[agk])
        W("gpsimd", "sCC", ccv)
        qv = DMA("gpsimd", "qG2", mrow[0:1, 0:32], agm_o[agk].ap())
        W("vector", "qG2", qv)
        if pair:
            em("vector", lambda e: e.tensor_reduce(
                mx[0:1, 0:1],
                mrow[0:1, 0:32].rearrange("p (a b) -> p b a", b=2)[:, 0:1, :],
                AxX, Alu.max))
            em("vector", lambda e: e.tensor_reduce(
                mx[0:1, 1:2],
                mrow[0:1, 0:32].rearrange("p (a b) -> p b a", b=2)[:, 1:2, :],
                AxX, Alu.max))
        else:
            em("vector", lambda e: e.tensor_reduce(
                mx[0:1, 0:1], mrow[0:1, 0:32], AxX, Alu.max))
            vpad()
        vpad()
        for j in range(nmax):
            em("vector", lambda e, j=j, col=col: e.tensor_scalar(
                sc1[0:1, col + 2 * j:col + 2 * j + 1], mx[0:1, j:j + 1],
                1.0 / QMAX, 1e-12, Alu.mult, Alu.max))
        vpad()
        vpad()
        for j in range(nmax):
            em("vector", lambda e, j=j, col=col: e.reciprocal(
                sc1[0:1, col + 2 * j + 1:col + 2 * j + 2],
                sc1[0:1, col + 2 * j:col + 2 * j + 1]))
        wid = 2 * nmax
        vpad()
        vpad()
        em("vector", lambda e, col=col, wid=wid: e.tensor_copy(
            sc1c[0:1, col:col + wid], sc1[0:1, col:col + wid]))
        vpad()
        vpad()
        em("vector", lambda e, col=col, wid=wid: e.tensor_tensor(
            rsc[0:1, 0:wid], sc1[0:1, col:col + wid],
            sc1c[0:1, col:col + wid], Alu.subtract))
        vpad()
        vpad()
        em("vector", lambda e, col=col, wid=wid: e.tensor_copy(
            sc1c[0:1, 8 + col:8 + col + wid], rsc[0:1, 0:wid]))
        vpad()
        vpad()
        em("vector", lambda e, col=col, wid=wid: e.tensor_tensor(
            rsc[0:1, 0:wid], rsc[0:1, 0:wid],
            sc1c[0:1, 8 + col:8 + col + wid], Alu.subtract))
        vpad()
        vpad()
        em("vector", lambda e, col=col, wid=wid: e.tensor_copy(
            sc1c[0:1, 16 + col:16 + col + wid], rsc[0:1, 0:wid]))
        svq3 = INC("vector", "sV")
        if debug and agk == 0:
            W("sync", "sV", svq3)
            DMA("sync", "qS", dbg["dbg_mx"].ap(), mx[0:1, 0:4])
            DMA("sync", "qS", dbg["dbg_mrow"].ap(), mrow[0:1, :])
            DMA("sync", "qS", dbg["dbg_sc1a"].ap()[0:1, 0:2], sc1[0:1, 0:2])
            for c_ in range(3):
                DMA("sync", "qS", dbg["dbg_sc1c"].ap()[0:1, 8 * c_:8 * c_ + 2],
                    sc1c[0:1, 8 * c_:8 * c_ + 2])
        W("tensor", "sV", svq3)
        W("tensor", "sC", psc1_ev[0])
        for c in range(3):
            em("tensor", lambda e, col=col, wid=wid, c=c: e.matmul(
                psc1[:, 0:wid], ones1F[0:1, :],
                sc1c[0:1, 8 * c + col:8 * c + col + wid],
                start=(c == 0), stop=(c == 2)))
        stq = INC("tensor", "sT")
        W("scalar", "sT", stq)
        em("scalar", lambda e, col=col, wid=wid: e.copy(
            sc1[:, col:col + wid], psc1[:, 0:wid]))
        psc1_ev[0] = INC("scalar", "sC")
        return psc1_ev[0]

    qv_s1 = gmax_to_scale(16, lambda i: xln2[:, i * DM:(i + 1) * DM], 0, 0,
                          "sC", sc_ln2)

    x_q = sb("x_q", [128, 16 * DM], BF16)
    for i in range(16):
        em("scalar", lambda e, i=i: e.activation(
            scr2[:, :], xln2[:, i * DM:(i + 1) * DM], Act.Identity,
            bias=magics[:, 0:1], scale=sc1[:, 1:2]))
        em("scalar", lambda e: e.activation(
            scr2[:, :], scr2[:, :], Act.Identity, bias=magics[:, 1:2], scale=1.0))
        em("scalar", lambda e, i=i: e.activation(
            x_q[:, i * DM:(i + 1) * DM], scr2[:, :], Act.Identity, bias=0.0,
            scale=sc1[:, 0:1]))
    sv_xq = INC("scalar", "sC")
    sb_free("xln2", "st2")

    xqT = sb("xqT", [128, 8 * T], BF16)
    ptp2 = [ps(f"ptp2_{j}", [128, 128], BF16) for j in range(2)]
    W("tensor", "sC", sv_xq)
    ntp = 0
    xq_ev = []
    for i in range(16):
        for k in range(8):
            j = ntp % 2
            if ntp >= 2:
                W("tensor", "sC", xq_ev[ntp - 2])
            em("tensor", lambda e, i=i, k=k, j=j: e.transpose(
                ptp2[j][:, :], x_q[:, i * DM + 128 * k:i * DM + 128 * (k + 1)],
                ident[:, :]))
            stq = INC("tensor", "sT")
            W("scalar", "sT", stq)
            em("scalar", lambda e, i=i, k=k, j=j: e.copy(
                xqT[:, k * T + 128 * i:k * T + 128 * (i + 1)], ptp2[j][:, :]))
            xq_ev.append(INC("scalar", "sC"))
            ntp += 1
    sc_xqT = xq_ev[-1]
    if debug:
        W("sync", "sC", sc_xqT)
        DMA("sync", "qS", dbg["dbg_xqT"].ap(), xqT[:, :])
    sb_free("x_q", "ptp2_0", "ptp2_1")

    wq_t = [sb(f"wq_t{j}", [128, 8 * 128], BF16, high=True) for j in range(2)]
    pF = [ps(f"pF{j}", [128, 512]) for j in range(2)]
    fevac = [sb(f"fev{j}", [128, T], F32, high=True) for j in range(2)]

    def ffn_mm(w_scr, dst_scr, mcol, first, prev_sg, prev_dma):
        f_ev = []
        st_f = []
        red_done = []
        f_dma = []
        W("gpsimd", "sT", semc["sT"])
        if first:
            W("gpsimd", "qTP", qS2_dq)
        for o in range(OC):
            bj = o % 2
            if o >= 1:
                W("gpsimd", "sT", st_f[-1])
            wd_ = [DMA("gpsimd", "qG", wq_t[bj][:, :],
                       w_scr.ap()[o, :, :])]
            W("tensor", "qG", wd_[-1])
            if o == 0 and first:
                W("tensor", "sC", sc_xqT)
                W("tensor", "qTP", qS2_dq)
            for si in range(4):
                gs = o * 4 + si
                pj = gs % 2
                if gs >= 2:
                    W("tensor", "sC", f_ev[gs - 2])
                for k in range(8):
                    em("tensor", lambda e, k=k, si=si, pj=pj, bj=bj: e.matmul(
                        pF[pj][:, :], wq_t[bj][:, k * 128:(k + 1) * 128],
                        xqT[:, k * T + 512 * si:k * T + 512 * (si + 1)],
                        start=(k == 0), stop=(k == 7)))
                st_f.append(INC("tensor", "sT"))
                W("scalar", "sT", st_f[-1])
                if si == 0:
                    if o >= 2:
                        W("scalar", "sV", red_done[o - 2])
                    elif not first:
                        W("scalar", "sV", prev_sg - (1 - o))
                    if o >= 1:
                        W("scalar", "qFW", f_dma[o - 1])
                    elif not first:
                        W("scalar", "qFW", prev_dma[-1])
                em("scalar", lambda e, si=si, pj=pj, bj=bj: e.copy(
                    fevac[bj][:, 512 * si:512 * (si + 1)], pF[pj][:, :]))
                f_ev.append(INC("scalar", "sC"))
            W("sync", "sC", f_ev[-1])
            f_dma.append(DMA("sync", "qFW", dst_scr.ap()[o, :, :],
                             fevac[bj][:, :]))
            W("vector", "sC", f_ev[-1])
            em("vector", lambda e, bj=bj: e.tensor_reduce(
                mx[:, 2:3], fevac[bj][:, :], AxX, Alu.max,
                apply_absolute_value=True))
            vpad()
            if o == 0:
                em("vector", lambda e, mcol=mcol: e.tensor_copy(
                    mx[:, mcol:mcol + 1], mx[:, 2:3]))
            else:
                em("vector", lambda e, mcol=mcol: e.tensor_tensor(
                    mx[:, mcol:mcol + 1], mx[:, mcol:mcol + 1], mx[:, 2:3],
                    Alu.max))
            red_done.append(INC("vector", "sV"))
        return red_done[-1], f_dma

    sg_up, up_dma = ffn_mm(sc_wuT, sc_up, 0, True, 0, None)
    sg_gate, _gd = ffn_mm(sc_wgT, sc_gate, 1, False, sg_up, up_dma)
    sb_free("xqT", "wq_t0", "wq_t1", "pF0", "pF1", "fev0", "fev1")

    qv_s23 = gmax_to_scale(0, None, 1, 2, "sV", sg_gate, pair=True)

    f_b = [sb(f"f_b{j}", [128, T], F32, high=True) for j in range(2)]
    f_a = [sb(f"f_a{j}", [128, T], F32, high=True) for j in range(4)]
    f_b += [sb(f"f_b{j}", [128, T], F32, high=True) for j in range(2, 4)]
    f_c = [sb(f"f_c{j}", [128, T], F32, high=True) for j in range(2)]
    QN4 = ("qFA", "qFB", "qFC", "qFD")

    def qdq_ops(buf_ap, scol, icol):
        em("vector", lambda e: e.tensor_scalar(
            buf_ap, buf_ap, sc1[:, icol:icol + 1], MAGIC, Alu.mult, Alu.add))
        em("vector", lambda e: e.tensor_scalar(
            buf_ap, buf_ap, MAGIC, None, Alu.subtract))
        em("vector", lambda e: e.tensor_scalar(
            buf_ap, buf_ap, sc1[:, scol:scol + 1], None, Alu.mult))

    g2_dma = []
    g2_red = []
    sig_sc = []
    sv_qdq1 = []
    W("vector", "sC", qv_s23)
    W("sync", "qFW", semc["qFW"])
    W("gpsimd", "qFW", semc["qFW"])
    if debug:
        DMA("sync", "qS", dbg["dbg_up"].ap(), sc_up.ap()[0, :, :])
        DMA("sync", "qS", dbg["dbg_gate"].ap(), sc_gate.ap()[0, :, :])
        DMA("sync", "qS", dbg["dbg_wuT"].ap(), sc_wuT.ap()[0, :, :])
        DMA("sync", "qS", dbg["dbg_wdT"].ap(), sc_wdT.ap()[0, :, :])
    for o in range(OC):
        bj = o % 4
        qn = QN4[bj]
        if o >= 4:
            W("sync", "sV", sv_qdq1[o - 4])
        qv = DMA("sync", qn, f_a[bj][:, :], sc_gate.ap()[o, :, :])
        W("scalar", qn, qv)
        if o >= 4:
            W("scalar", "sV", sv_qdq1[o - 4])
        em("scalar", lambda e, bj=bj: e.activation(
            f_a[bj][:, :], f_a[bj][:, :], Act.Identity, bias=magics[:, 0:1],
            scale=sc1[:, 5:6]))
        em("scalar", lambda e, bj=bj: e.activation(
            f_a[bj][:, :], f_a[bj][:, :], Act.Identity, bias=magics[:, 1:2],
            scale=1.0))
        em("scalar", lambda e, bj=bj: e.activation(
            f_a[bj][:, :], f_a[bj][:, :], Act.Identity, bias=0.0,
            scale=sc1[:, 4:5]))
        em("scalar", lambda e, bj=bj: e.activation(f_b[bj][:, :],
                                                   f_a[bj][:, :], Act.Sigmoid))
        sig_sc.append(INC("scalar", "sC"))
        W("vector", "sC", sig_sc[-1])
        if o >= 2:
            W("vector", "qFW" if o % 2 == 0 else "qW2", g2_dma[o - 2])
        em("vector", lambda e, bj=bj: e.tensor_scalar(
            f_b[bj][:, :], f_b[bj][:, :], 65536.0, MAGIC, Alu.mult, Alu.add))
        em("vector", lambda e, bj=bj: e.tensor_scalar(
            f_b[bj][:, :], f_b[bj][:, :], MAGIC, 65535.0, Alu.subtract,
            Alu.min))
        em("vector", lambda e, bj=bj: e.tensor_scalar(
            f_b[bj][:, :], f_b[bj][:, :], 1.0 / 65536.0, None, Alu.mult))
        em("vector", lambda e, bj=bj, c2=o % 2: e.tensor_tensor(
            f_c[c2][:, :], f_a[bj][:, :], f_b[bj][:, :], Alu.mult))
        sv2 = INC("vector", "sV")
        sv_qdq1.append(sv2)
        W("gpsimd", "sV", sv2)
        g2_dma.append(DMA("gpsimd", "qFW" if o % 2 == 0 else "qW2",
                          sc_g2.ap()[o, :, :], f_c[o % 2][:, :]))
        em("vector", lambda e, c2=o % 2: e.tensor_reduce(
            mx[:, 2:3], f_c[c2][:, :], AxX, Alu.max,
            apply_absolute_value=True))
        vpad()
        if o == 0:
            em("vector", lambda e: e.tensor_copy(mx[:, 0:1], mx[:, 2:3]))
        else:
            em("vector", lambda e: e.tensor_tensor(mx[:, 0:1], mx[:, 0:1],
                                                   mx[:, 2:3], Alu.max))
        g2_red.append(INC("vector", "sV"))

    qv_s4 = gmax_to_scale(0, None, 2, 6, "sV", g2_red[-1])

    o_dma = []
    o_red = []
    sv_qdq2 = []
    W("vector", "sC", qv_s4)
    W("sync", "qFW", semc["qFW"])
    W("gpsimd", "qFW", semc["qFW"])
    W("gpsimd", "qW2", semc["qW2"])
    if debug:
        DMA("sync", "qS", dbg["dbg_g2"].ap(), sc_g2.ap()[0, :, :])
    for o in range(OC):
        bj = o % 4
        qn = QN4[bj]
        if o >= 4:
            W("sync", "sV", sv_qdq2[o - 4])
        qv1 = DMA("sync", qn, f_a[bj][:, :], sc_g2.ap()[o, :, :])
        qv2 = DMA("sync", qn, f_b[bj][:, :], sc_up.ap()[o, :, :])
        W("scalar", qn, qv2)
        if o >= 4:
            W("scalar", "sV", sv_qdq2[o - 4])
        em("scalar", lambda e, bj=bj: e.activation(
            f_a[bj][:, :], f_a[bj][:, :], Act.Identity, bias=magics[:, 0:1],
            scale=sc1[:, 7:8]))
        em("scalar", lambda e, bj=bj: e.activation(
            f_a[bj][:, :], f_a[bj][:, :], Act.Identity, bias=magics[:, 1:2],
            scale=1.0))
        em("scalar", lambda e, bj=bj: e.activation(
            f_a[bj][:, :], f_a[bj][:, :], Act.Identity, bias=0.0,
            scale=sc1[:, 6:7]))
        sq_sc = INC("scalar", "sC")
        W("vector", qn, qv2)
        if o >= 2:
            W("vector", "qFW" if o % 2 == 0 else "qW2", o_dma[o - 2])
        qdq_ops(f_b[bj][:, :], 2, 3)
        W("vector", "sC", sq_sc)
        em("vector", lambda e, bj=bj, c2=o % 2: e.tensor_tensor(
            f_c[c2][:, :], f_a[bj][:, :], f_b[bj][:, :], Alu.mult))
        sv2 = INC("vector", "sV")
        sv_qdq2.append(sv2)
        W("gpsimd", "sV", sv2)
        o_dma.append(DMA("gpsimd", "qFW" if o % 2 == 0 else "qW2",
                         sc_o.ap()[o, :, :], f_c[o % 2][:, :]))
        em("vector", lambda e, c2=o % 2: e.tensor_reduce(
            mx[:, 2:3], f_c[c2][:, :], AxX, Alu.max,
            apply_absolute_value=True))
        vpad()
        if o == 0:
            em("vector", lambda e: e.tensor_copy(mx[:, 0:1], mx[:, 2:3]))
        else:
            em("vector", lambda e: e.tensor_tensor(mx[:, 0:1], mx[:, 0:1],
                                                   mx[:, 2:3], Alu.max))
        o_red.append(INC("vector", "sV"))

    qv_s5 = gmax_to_scale(0, None, 3, 2, "sV", o_red[-1])

    sb_free("x2", "f_c0", "f_c1")
    oq = sb("oq", [128, OC * T], BF16)
    oq_dma_in = []
    oq_sv = []
    W("sync", "sV", sv_qdq2[-1])
    W("sync", "qFW", semc["qFW"])
    W("gpsimd", "sV", sv_qdq2[-1])
    W("gpsimd", "qFW", semc["qFW"])
    W("gpsimd", "qW2", semc["qW2"])
    W("sync", "qOut", q_x2w)
    if debug:
        DMA("sync", "qS", dbg["dbg_oo"].ap(), sc_o.ap()[0, :, :])
        W("sync", "sC", qv_s5)
        DMA("sync", "qS", dbg["dbg_sc1"].ap(), sc1[:, :])
    W("vector", "sC", qv_s5)
    for o in range(OC):
        bj = o % 4
        qn = QN4[bj]
        if o >= 4:
            W("sync", oq_sv[o - 4][0], oq_sv[o - 4][1])
        qv = DMA("sync", qn, f_a[bj][:, :], sc_o.ap()[o, :, :])
        oq_dma_in.append(qv)
        if o % 2 == 0:
            W("scalar", qn, qv)
            em("scalar", lambda e, bj=bj: e.activation(
                f_a[bj][:, :], f_a[bj][:, :], Act.Identity,
                bias=magics[:, 0:1], scale=sc1[:, 3:4]))
            em("scalar", lambda e, bj=bj: e.activation(
                f_a[bj][:, :], f_a[bj][:, :], Act.Identity,
                bias=magics[:, 1:2], scale=1.0))
            em("scalar", lambda e, o=o, bj=bj: e.activation(
                oq[:, o * T:(o + 1) * T], f_a[bj][:, :], Act.Identity,
                bias=0.0, scale=sc1[:, 2:3]))
            oq_sv.append(("sC", INC("scalar", "sC")))
        else:
            W("vector", qn, qv)
            em("vector", lambda e, bj=bj: e.tensor_scalar(
                f_a[bj][:, :], f_a[bj][:, :], sc1[:, 3:4], MAGIC,
                Alu.mult, Alu.add))
            em("vector", lambda e, bj=bj: e.tensor_scalar(
                f_a[bj][:, :], f_a[bj][:, :], MAGIC, None, Alu.subtract))
            em("vector", lambda e, o=o, bj=bj: e.tensor_scalar(
                oq[:, o * T:(o + 1) * T], f_a[bj][:, :], sc1[:, 2:3], None,
                Alu.mult))
            oq_sv.append(("sV", INC("vector", "sV")))
    oq_last_sc = max(v for s, v in oq_sv if s == "sC")
    oq_last_sv = max(v for s, v in oq_sv if s == "sV")

    sb_free("f_a0", "f_a1", "f_a2", "f_a3", "f_b2", "f_b3")
    wd_sb = sb("wd_sb", [128, OC * DM], BF16)
    W("gpsimd", "sT", semc["sT"])
    W("gpsimd", "sV", semc["sV"])
    W("gpsimd", "sC", semc["sC"])
    W("gpsimd", "qTP", qS2_dq)
    for j2 in range(DM // 128):
        DMA("gpsimd", "qG",
            wd_sb[:, :].rearrange("p (k c) -> p k c", c=DM)[
                :, :, 128 * j2:128 * (j2 + 1)],
            sc_wdT.ap()[j2, :, :].rearrange("p (k c) -> p k c", c=128))
    qg_wd = semc["qG"]
    pD = [ps(f"pD{j}", [128, 512]) for j in range(2)]
    W("tensor", "sC", oq_last_sc)
    W("tensor", "sV", oq_last_sv)
    W("tensor", "qG", qg_wd)
    res_ev = []
    out_dmas = []
    x2_rd = []
    W("sync", "qOut", q_x2w)
    W("gpsimd", "qOut", q_x2w)
    W("sync", "sC", oq_last_sc)
    W("sync", "sV", oq_last_sv)
    for i in range(16):
        bj2 = i % 2
        qn2 = "qFA" if bj2 == 0 else "qFB"
        if i >= 1:
            W("sync", "sV", res_ev[-1])
        if i >= 2:
            W("sync", "sV", res_ev[-3])
        x2_rd.append(DMA("sync", qn2, f_b[bj2][:, 0:DM],
                         sc_x2.ap()[i, :, :]))
        for hf in range(2):
            if len(res_ev) >= 2:
                W("tensor", "sV", res_ev[-2])
            for k in range(OC):
                em("tensor", lambda e, k=k, i=i, hf=hf: e.matmul(
                    pD[hf][:, :], oq[:, k * T + 128 * i:k * T + 128 * (i + 1)],
                    wd_sb[:, k * DM + 512 * hf:k * DM + 512 * (hf + 1)],
                    start=(k == 0), stop=(k == OC - 1)))
            stq = INC("tensor", "sT")
            W("vector", "sT", stq)
            if i >= 1 and hf == 0:
                W("vector", "qOut", out_dmas[-1])
            if hf == 0:
                W("vector", qn2, x2_rd[i])
            em("vector", lambda e, i=i, hf=hf, bj=bj2: e.tensor_tensor(
                scr2[:, 512 * hf:512 * (hf + 1)],
                f_b[bj][:, 512 * hf:512 * (hf + 1)],
                pD[hf][:, :], Alu.add))
            res_ev.append(INC("vector", "sV"))
        W("sync", "sV", res_ev[-1])
        out_dmas.append(DMA("sync", "qOut",
                            out_d.ap()[128 * i:128 * (i + 1), :],
                            scr2[:, 0:DM]))

    W("gpsimd", "qOut", semc["qOut"])
    W("gpsimd", "qS", semc["qS"])
    W("gpsimd", "qS2", semc["qS2"])

    bad, stuck = audit_races()
    if stuck:
        raise RuntimeError(f"audit deadlock: {stuck}")
    if bad and debug:
        print(f"audit: {len(bad)} races waived in debug build")
        bad = []
    if bad:
        summ = {}
        for (e, s, v, wv, we) in bad:
            summ.setdefault((e, s), []).append((v, wv, we))
        for k2, vs in summ.items():
            print("RACE", k2, vs[:4], f"(+{len(vs)-4} more)" if len(vs) > 4
                  else "")
        raise RuntimeError(f"{len(bad)} potential semaphore races")

    with nc.Block() as block:
        @block.sync
        def _(eng):
            for f in EMIT["sync"]:
                f(eng)

        @block.vector
        def _(eng):
            for f in EMIT["vector"]:
                f(eng)

        @block.scalar
        def _(eng):
            for f in EMIT["scalar"]:
                f(eng)

        @block.gpsimd
        def _(eng):
            for f in EMIT["gpsimd"]:
                f(eng)

        @block.tensor
        def _(eng):
            for f in EMIT["tensor"]:
                f(eng)

    for name, cnt in semc.items():
        assert cnt < 30000, (name, cnt)
    return nc


def _sel32():
    s = np.zeros((32, 32, 128), BF)
    for r in range(32):
        s[r, r, :] = BF(1.0)
    return s.reshape(32, 32 * 128)


def _perm(p):
    idx = []
    for j in range(G):
        gblk = (j - p) % G
        idx.extend(range(128 * gblk, 128 * (gblk + 1)))
    return np.asarray(idx)


def kernel(L0_attn_x, L0_attn_W_in, L0_attn_conv_w, L0_attn_conv_b,
           L0_attn_W_xproj, L0_attn_W_dt, L0_attn_b_dt, L0_attn_A_log,
           L0_attn_D_param, L0_attn_W_out, L0_ffn_w_gate, L0_ffn_w_up,
           L0_ffn_w_down, _debug=False, _nc=None):
    x = np.ascontiguousarray(np.asarray(L0_attn_x, np.float32))
    W_in = np.asarray(L0_attn_W_in, np.float32)
    conv_w = np.asarray(L0_attn_conv_w, np.float32)[:, 0, :]
    conv_b = np.asarray(L0_attn_conv_b, np.float32).reshape(DI, 1)
    W_xp = np.asarray(L0_attn_W_xproj, np.float32)
    W_dtm = np.asarray(L0_attn_W_dt, np.float32)
    b_dt = np.asarray(L0_attn_b_dt, np.float32).reshape(DI, 1)
    A_log = np.asarray(L0_attn_A_log, np.float32)
    D_par = np.asarray(L0_attn_D_param, np.float32)
    W_outm = np.asarray(L0_attn_W_out, np.float32)
    wg = np.ascontiguousarray(np.asarray(L0_ffn_w_gate, np.float32))
    wu = np.ascontiguousarray(np.asarray(L0_ffn_w_up, np.float32))
    wd = np.ascontiguousarray(np.asarray(L0_ffn_w_down, np.float32))

    base = {
        "wu_nat": wu, "wg_nat": wg, "wd_nat": wd,
        "ident": np.eye(128, dtype=BF),
        "ones1F": np.ones((1, 128), BF),
        "magics": np.repeat(np.array([[MAGIC, -MAGIC]], np.float32), 128,
                            axis=0),
        "sel32": _sel32(),
    }
    perm_sets = []
    for p in range(4):
        pe = _perm(p)
        dd = np.zeros((DI, 128), np.float32)
        for j in range(G):
            np.fill_diagonal(dd[128 * j:128 * (j + 1), :],
                             D_par[pe[128 * j:128 * (j + 1)]])
        perm_sets.append({
            "w_in": np.ascontiguousarray(
                np.concatenate([W_in[:, :DI][:, pe], W_in[:, DI:][:, pe]],
                               axis=1)),
            "w_dt": np.ascontiguousarray(W_dtm[pe][:, pe]),
            "w_xp": np.ascontiguousarray(W_xp[pe]),
            "w_out": np.ascontiguousarray(W_outm[pe]),
            "cw": np.ascontiguousarray(conv_w[pe]),
            "cb": np.ascontiguousarray(conv_b[pe]),
            "bdt": np.ascontiguousarray(b_dt[pe]),
            "alog": np.ascontiguousarray(A_log[pe]),
            "ddiag": dd,
        })

    in_maps = []
    for c in range(NCORE):
        bb, q = c // 4, c % 4
        t0 = q * T
        xin = np.zeros((TH, DM), np.float32)
        if q > 0:
            xin[:3] = x[bb, t0 - 3:t0]
        xin[3:] = x[bb, t0:t0 + T]
        onehot = np.zeros((8, 1), BF)
        if q > 0:
            onehot[c - 1, 0] = BF(1.0)
        im = dict(base)
        im.update(perm_sets[q])
        im["x_in"] = xin
        im["onehot"] = onehot
        in_maps.append(im)

    nc = _nc if _nc is not None else build_nc(debug=_debug)
    res = run_bass_kernel_spmd(nc, in_maps, core_ids=list(range(NCORE)))
    out = np.empty((B, L, DM), np.float32)
    for c in range(NCORE):
        bb, q = c // 4, c % 4
        out[bb, q * T:(q + 1) * T] = np.asarray(res.results[c]["out"])
    if _debug:
        return out, res
    return out
